# version 02
# baseline (speedup 1.0000x reference)
import os
os.environ.setdefault("TRN_ALLOW_ICI_ALL_TO_ALL", "1")
import numpy as np
import ml_dtypes
import contextlib
import concourse.bass as bass
import concourse.mybir as mybir
from concourse.bass_utils import run_bass_kernel_spmd

F32 = mybir.dt.float32
BF16 = mybir.dt.bfloat16
AF = mybir.ActivationFunctionType
ALU = mybir.AluOpType
AX = mybir.AxisListType
BFD = np.dtype(ml_dtypes.bfloat16)

NCORES = 8
B, T, D = 2, 8192, 1024
TC = B * T // NCORES
FC = D // 128
NT = 512
NPANEL = 8


class Sem:
    def __init__(self, h):
        self.h = h
        self.n = 0

    def inc(self, instr, k=1):
        instr.then_inc(self.h, k)
        self.n += k
        return self.n


MODULE_FINAL_OUT = "out"

class Ctx:
    def __init__(self, name, fused=False):
        self.nc = bass.Bass("TRN2", target_bir_lowering=False, debug=False)
        self.es = contextlib.ExitStack()
        self.block = None
        self.fused = fused
        self.tag = ""
        self.dram = {}
        self.sems = {}
        self.sbuf_base0 = self.nc.sbuf_base
        self.psum_base0 = self.nc.psum_base

    def phase(self, tag):
        self.tag = tag + "_"
        self.sems = {}
        self.nc.sbuf_base = self.sbuf_base0
        self.nc.psum_base = self.psum_base0

    def sem(self, name):
        if name not in self.sems:
            self.sems[name] = Sem(self.nc.alloc_semaphore(name=self.tag + name))
        return self.sems[name]

    def sb(self, name, shape, dtype):
        return self.nc.alloc_sbuf_tensor(self.tag + name, shape, dtype)

    def sba(self, name, shape, dtype, offset):
        return self.nc.alloc_sbuf_tensor_at(self.tag + name, shape, dtype,
                                            offset=offset)

    def ps(self, name, shape, dtype):
        return self.nc.alloc_psum_tensor(self.tag + name, shape, dtype)

    def dt(self, name, shape, dtype, io, shared=False):
        if name in self.dram:
            return self.dram[name]
        kind = {"in": "ExternalInput", "int": "Internal",
                "out": "ExternalOutput"}[io]
        kw = {"addr_space": "Shared"} if shared else {}
        ap = self.nc.dram_tensor(name, shape, dtype, kind=kind, **kw).ap()
        self.dram[name] = ap
        return ap



def emit_norm_fm(ctx, xs, yout, sq, rows, ones1, onesc, psA, psB,
                 mmd, vec, act, ntok, kind="ln", eps_ap=None, gcol=None,
                 in_stride=None, in_off=0):
    nc, block = ctx.nc, ctx.block
    ntt = ntok // NT
    ist = in_stride if in_stride is not None else ntok
    def xv_of(fcb):
        return xs[:, fcb * ist + in_off:fcb * ist + in_off + ntok]
    mean = rows[0:1, 0:ntok]
    work = rows[0:1, ntok:2 * ntok]
    base_act = act.n
    base_mmd = mmd.n

    @block.scalar
    def _(scalar):
        for fcb in range(FC):
            if fcb >= 1:
                scalar.wait_ge(mmd.h, base_mmd + fcb)
            act.inc(scalar.activation(sq[:, 0:ntok], xv_of(fcb), AF.Square), 1)

    @block.tensor
    def _(tensor):
        for fcb in range(FC):
            tensor.wait_ge(act.h, base_act + fcb + 1)
            for tt in range(ntt):
                sl = slice(tt * NT, (tt + 1) * NT)
                if kind == "ln":
                    tensor.matmul(psA[0:1, sl], onesc[:, :],
                                  xv_of(fcb)[:, tt * NT:(tt + 1) * NT],
                                  start=(fcb == 0), stop=(fcb == FC - 1))
                mm2 = tensor.matmul(psB[0:1, sl], onesc[:, :], sq[:, sl],
                                    start=(fcb == 0), stop=(fcb == FC - 1))
            mmd.inc(mm2, 1)

    @block.vector
    def _(vector):
        vector.wait_ge(mmd.h, mmd.n)
        if kind == "ln":
            vector.tensor_scalar_mul(work, psB[0:1, 0:ntok], 1.0 / D)
            vec.inc(vector.tensor_scalar_mul(mean, psA[0:1, 0:ntok], 1.0 / D), 1)
            vector.wait_ge(vec.h, vec.n)
        else:
            vec.inc(vector.tensor_scalar_mul(work, psB[0:1, 0:ntok], 1.0 / D), 1)

    if kind == "ln":
        @block.scalar
        def _(scalar):
            scalar.wait_ge(vec.h, vec.n)
            act.inc(scalar.activation(psA[0:1, 0:ntok], mean, AF.Square), 1)

        @block.vector
        def _(vector):
            vector.wait_ge(act.h, act.n)
            vec.inc(vector.tensor_sub(work, work, psA[0:1, 0:ntok]), 1)

    @block.scalar
    def _(scalar):
        scalar.wait_ge(vec.h, vec.n)
        act.inc(scalar.activation(work, work, AF.Ln, bias=eps_ap), 1)
        scalar.wait_ge(act.h, act.n)
        act.inc(scalar.activation(work, work, AF.Exp, scale=-0.5), 1)

    if kind == "ln":
        @block.vector
        def _(vector):
            vector.wait_ge(act.h, act.n)
            vec.inc(vector.scalar_tensor_tensor(mean, mean, -1.0, work,
                                                ALU.mult, ALU.mult), 1)

    @block.tensor
    def _(tensor):
        if kind == "ln":
            tensor.wait_ge(vec.h, vec.n)
        else:
            tensor.wait_ge(act.h, act.n)
        for tt in range(ntt):
            sl = slice(tt * NT, (tt + 1) * NT)
            mm = tensor.matmul(psA[:, sl], ones1[0:1, :], work[0:1, sl],
                               start=True, stop=True)
            if kind == "ln":
                mm = tensor.matmul(psB[:, sl], ones1[0:1, :], mean[0:1, sl],
                                   start=True, stop=True)
        mmd.inc(mm, 1)

    @block.vector
    def _(vector):
        vector.wait_ge(mmd.h, mmd.n)
        for fcb in range(FC):
            xv = xv_of(fcb)
            yv = yout[:, fcb * ntok:(fcb + 1) * ntok]
            buf = sq[:, 0:ntok]
            if fcb > 0:
                vector.wait_ge(vec.h, vec.n)
            if kind == "ln":
                vec.inc(vector.tensor_mul(buf, xv, psA[:, 0:ntok]), 1)
                vector.wait_ge(vec.h, vec.n)
                if gcol is None:
                    vec.inc(vector.tensor_add(yv, buf, psB[:, 0:ntok]), 1)
                else:
                    vec.inc(vector.tensor_add(buf, buf, psB[:, 0:ntok]), 1)
                    vector.wait_ge(vec.h, vec.n)
                    vec.inc(vector.tensor_scalar_mul(yv, buf, gcol[:, fcb:fcb + 1]), 1)
            else:
                if gcol is None:
                    vec.inc(vector.tensor_mul(yv, xv, psA[:, 0:ntok]), 1)
                else:
                    vec.inc(vector.tensor_mul(buf, xv, psA[:, 0:ntok]), 1)
                    vector.wait_ge(vec.h, vec.n)
                    vec.inc(vector.tensor_scalar_mul(yv, buf, gcol[:, fcb:fcb + 1]), 1)
        vector.wait_ge(vec.h, vec.n)
        vec.inc(vector.tensor_copy(rows[0:1, 0:1], onesc[0:1, 0:1]), 1)



def build_pre0(ctx):
    ctx.phase("p0")
    nc = ctx.nc
    x = ctx.dt("x", [D, TC], F32, "in")
    y0o = ctx.dt("y0", [D, TC], BF16, "int")

    xs = ctx.sb("xs", [128, FC * TC], F32)
    y0 = ctx.sb("y0s", [128, FC * TC], BF16)
    sq2 = ctx.sb("sq2", [128, TC], F32)
    rows = ctx.sb("rows", [1, 2 * TC], F32)
    ones1 = ctx.sb("ones1", [1, 128], F32)
    onesc = ctx.sb("onesc", [128, 1], F32)
    consts = ctx.sb("consts", [128, 4], F32)
    psA = ctx.ps("psA", [128, 2048], F32)
    psB = ctx.ps("psB", [128, 2048], F32)

    ld = ctx.sem("ld"); mmd = ctx.sem("mmd"); vec = ctx.sem("vec")
    act = ctx.sem("act"); st = ctx.sem("st"); ig = ctx.sem("ig")

    with nc.Block() as block:
        ctx.block = block

        @block.sync
        def _(sync):
            for fcb in range(FC):
                ld.inc(sync.dma_start(out=xs[:, fcb * TC:(fcb + 1) * TC],
                                      in_=x[fcb * 128:(fcb + 1) * 128, :]), 16)

        @block.gpsimd
        def _(gpsimd):
            gpsimd.memset(ones1[:, :], 1.0)
            gpsimd.memset(consts[:, 0:1], 1e-5)
            ig.inc(gpsimd.memset(onesc[:, :], 1.0), 1)

        @block.scalar
        def _(scalar):
            scalar.wait_ge(ld.h, ld.n)
            scalar.wait_ge(ig.h, ig.n)

        @block.tensor
        def _(tensor):
            tensor.wait_ge(ld.h, ld.n)
            tensor.wait_ge(ig.h, ig.n)

        emit_norm_fm(ctx, xs, y0, sq2, rows, ones1, onesc, psA, psB,
                     mmd, vec, act, TC, kind="ln", eps_ap=consts[0:1, 0:1])

        @block.sync
        def _(sync):
            sync.wait_ge(vec.h, vec.n)
            for fcb in range(FC):
                st.inc(sync.dma_start(out=y0o[fcb * 128:(fcb + 1) * 128, :],
                                      in_=y0[:, fcb * TC:(fcb + 1) * TC]), 16)
            sync.wait_ge(st.h, st.n)
    return ctx



def build_mid0(ctx):
    ctx.phase("m0")
    nc = ctx.nc
    NP = NPANEL
    y0f = ctx.dt("y0f", [NP * D, TC], BF16, "int")
    wq = ctx.dt("wq", [D, 128], F32, "in")
    wk = ctx.dt("wk", [D, 128], F32, "in")
    wv = ctx.dt("wv", [D, 128], F32, "in")
    oo = ctx.dt("oh", [128, NP * TC], BF16, "int")

    y0p = ctx.sb("y0p", [128, 2 * FC * TC], BF16)
    wqt = ctx.sb("wqt", [128, FC * 128], BF16)
    wkt = ctx.sb("wkt", [128, FC * 128], BF16)
    wvt = ctx.sb("wvt", [128, FC * 128], BF16)
    qsil = ctx.sb("qsil", [128, TC], F32)
    Wfull = ctx.sb("Wfull", [128, TC], F32)
    cum = ctx.sb("cum", [128, TC], F32)
    scr = ctx.sb("scr", [128, TC], F32)
    qn = ctx.sb("qn", [128, TC], BF16)
    W_bf = ctx.sb("W_bf", [128, TC], BF16)
    v_bf = ctx.sb("v_bf", [128, TC], BF16)
    W_tok = ctx.sb("W_tok", [128, TC], BF16)
    v_tok = ctx.sb("v_tok", [128, TC], BF16)
    o_h = ctx.sb("o_h", [128, NP * TC], BF16)
    Am = ctx.sb("Am", [128, 128], BF16)
    tri = ctx.sb("tri", [128, 128], F32)
    onesbig = ctx.sb("onesbig", [128, TC], F32)
    Tst_f = ctx.sb("Tst_f", [128, 128], F32)
    Tst_bf = ctx.sb("Tst_bf", [128, 128], BF16)
    carry = ctx.sb("carry", [128, 1], F32)
    psA = ctx.ps("psA", [128, 2048], F32)
    psB = ctx.ps("psB", [128, 2048], F32)

    ld = [ctx.sem("ldA"), ctx.sem("ldB")]
    wld = ctx.sem("wld")
    ini = ctx.sem("ini")
    mmp = ctx.sem("mmp")
    evq = ctx.sem("evq")
    evv = ctx.sem("evv")
    rec = ctx.sem("rec")
    vsc = ctx.sem("vsc")
    dt = ctx.sem("dt")
    sA = ctx.sem("sA")
    sAm = ctx.sem("sAm")
    sOc = ctx.sem("sOc")
    sOe = ctx.sem("sOe")
    sTd = ctx.sem("sTd")
    sSt = ctx.sem("sSt")
    st = ctx.sem("st")

    proj_banks = [psA[:, 1024:1536], psA[:, 1536:2048],
                  psB[:, 1024:1536], psB[:, 1536:2048]]

    with nc.Block() as block:
        ctx.block = block

        @block.gpsimd
        def _(gpsimd):
            wld.inc(gpsimd.dma_start(out=wqt[:, :].rearrange("p (c i) -> p c i", c=FC),
                             in_=wq.rearrange("(c p) i -> p c i", p=128)), 16)
            wld.inc(gpsimd.dma_start(out=wkt[:, :].rearrange("p (c i) -> p c i", c=FC),
                             in_=wk.rearrange("(c p) i -> p c i", p=128)), 16)
            wld.inc(gpsimd.dma_start(out=wvt[:, :].rearrange("p (c i) -> p c i", c=FC),
                                     in_=wv.rearrange("(c p) i -> p c i", p=128)), 16)
            gpsimd.memset(onesbig[:, :], 1.0)
            ini.inc(gpsimd.iota(scr[:, 0:128], pattern=[[1, 128]], base=0,
                        channel_multiplier=-1, allow_small_or_imprecise_dtypes=True), 1)
            gpsimd.wait_ge(ini.h, ini.n)
            ini.inc(gpsimd.tensor_single_scalar(tri[:, :], scr[:, 0:128], 0.0, ALU.is_ge), 1)
            gpsimd.memset(Tst_f[:, :], 0.0)
            sSt.inc(gpsimd.memset(Tst_bf[:, :], 0.0), 1)
            ini.inc(gpsimd.memset(carry[:, :], 0.0), 1)

        state_ready = 1

        for p in range(NP):
            pb = (p % 2) * FC * TC

            @block.sync
            def _(sync, p=p, pb=pb):
                if p >= 2:
                    sync.wait_ge(mmp.h, 12 * (p - 1) + 1)
                for fcb in range(FC):
                    ld[p % 2].inc(sync.dma_start(
                        out=y0p[:, pb + fcb * TC:pb + (fcb + 1) * TC],
                        in_=y0f[p * D + fcb * 128:p * D + (fcb + 1) * 128, :]), 16)

            @block.tensor
            def _(tensor, p=p, pb=pb, state_ready=state_ready):
                tensor.wait_ge(ld[p % 2].h, 128 * (p // 2 + 1))
                if p == 0:
                    tensor.wait_ge(wld.h, 48)
                    tensor.wait_ge(ini.h, 1)
                if p >= 1:
                    tensor.wait_ge(sOe.h, 16 * p)
                    tensor.wait_ge(sSt.h, state_ready)
                for pi, wt in enumerate((wqt, wkt, wvt)):
                    for tt in range(4):
                        if pi == 1:
                            tensor.wait_ge(evq.h, 8 * p + tt + 1)
                        elif pi == 2:
                            tensor.wait_ge(evq.h, 8 * p + 4 + tt + 1)
                        bank = proj_banks[tt]
                        for fcb in range(FC):
                            mm = tensor.matmul(
                                bank[:, 0:NT],
                                wt[:, fcb * 128:(fcb + 1) * 128],
                                y0p[:, pb + fcb * TC + tt * NT:pb + fcb * TC + (tt + 1) * NT],
                                start=(fcb == 0), stop=(fcb == FC - 1))
                        mmp.inc(mm, 1)

            @block.scalar
            def _(scalar, p=p):
                for tt in range(4):
                    scalar.wait_ge(mmp.h, p * 12 + tt + 1)
                    evq.inc(scalar.activation(qsil[:, tt * NT:(tt + 1) * NT],
                                              proj_banks[tt][:, 0:NT], AF.Silu), 1)
                for tt in range(4):
                    scalar.wait_ge(mmp.h, p * 12 + 4 + tt + 1)
                    evq.inc(scalar.activation(Wfull[:, tt * NT:(tt + 1) * NT],
                                              proj_banks[tt][:, 0:NT], AF.Exp), 1)

            @block.vector
            def _(vector, p=p):
                vector.wait_ge(dt.h, p * 32 * 16)
                for tt in range(4):
                    vector.wait_ge(mmp.h, p * 12 + 8 + tt + 1)
                    evv.inc(vector.tensor_copy(v_bf[:, tt * NT:(tt + 1) * NT],
                                               proj_banks[tt][:, 0:NT]), 1)

            @block.gpsimd
            def _(gpsimd, p=p):
                if p == 4:
                    gpsimd.wait_ge(vsc.h, 4 * p)
                    ini.inc(gpsimd.memset(carry[:, :], 0.0), 1)

            @block.vector
            def _(vector, p=p):
                vector.wait_ge(evq.h, p * 8 + 8)
                if p == 0 or p == 4:
                    vector.wait_ge(ini.h, ini.n)
                if p >= 1:
                    vector.wait_ge(sTd.h, 16 * p)
                sc = vector.tensor_tensor_scan(cum[:, :], onesbig[:, :], Wfull[:, :],
                                               carry[:, 0:1], ALU.mult, ALU.add)
                vsc.inc(sc, 1)
                vector.wait_ge(vsc.h, vsc.n)
                vsc.inc(vector.tensor_copy(carry[:, 0:1], cum[:, TC - 1:TC]), 1)
                vsc.inc(vector.tensor_copy(W_bf[:, :], Wfull[:, :]), 1)

            @block.scalar
            def _(scalar, p=p):
                scalar.wait_ge(vsc.h, 4 * p + 1)
                rec.inc(scalar.activation(scr[:, :], cum[:, :], AF.Ln), 1)
                scalar.wait_ge(rec.h, rec.n)
                rec.inc(scalar.activation(scr[:, :], scr[:, :], AF.Exp, scale=-1.0), 1)

            @block.vector
            def _(vector, p=p):
                vector.wait_ge(rec.h, 2 * (p + 1))
                vsc.inc(vector.scalar_tensor_tensor(qn[:, :], qsil[:, :],
                                                    0.08838834764831845,
                                                    scr[:, :], ALU.mult, ALU.mult), 1)

            @block.scalar
            def _(scalar, p=p):
                scalar.wait_ge(vsc.h, 4 * p + 3)
                scalar.wait_ge(evv.h, 4 * p + 4)
                for m in range(16):
                    dt.inc(scalar.dma_start(out=W_tok[:, m * 128:(m + 1) * 128],
                                            in_=W_bf[:, m * 128:(m + 1) * 128],
                                            transpose=True), 16)
                    dt.inc(scalar.dma_start(out=v_tok[:, m * 128:(m + 1) * 128],
                                            in_=v_bf[:, m * 128:(m + 1) * 128],
                                            transpose=True), 16)

            for m in range(16):
                g = p * 16 + m
                msl = slice(m * 128, (m + 1) * 128)
                ocol = slice(g * 128, g * 128 + 128)
                ob = psB[:, 0:128] if g % 2 == 0 else psB[:, 512:640]

                @block.tensor
                def _(tensor, p=p, m=m, g=g, msl=msl, ob=ob, ocol=ocol,
                      state_ready=state_ready):
                    if m == 0:
                        tensor.wait_ge(vsc.h, 4 * p + 4)
                        tensor.wait_ge(dt.h, 512 * (p + 1))
                    if g >= 1:
                        tensor.wait_ge(sAm.h, g)
                    sA.inc(tensor.matmul(psA[:, 0:128], W_bf[:, msl], qn[:, msl],
                                         start=True, stop=True), 1)
                    tensor.wait_ge(sAm.h, g + 1)
                    tensor.wait_ge(sSt.h, state_ready)
                    if g >= 2:
                        tensor.wait_ge(sOe.h, g - 1)
                    tensor.matmul(ob, v_tok[:, msl], Am[:, :], start=True, stop=False)
                    sOc.inc(tensor.matmul(ob, Tst_bf[:, :], qn[:, msl],
                                          start=False, stop=True), 1)
                    sTd.inc(tensor.matmul(psA[:, 512:640], W_tok[:, msl], v_tok[:, msl],
                                          start=True, stop=True), 1)

                @block.vector
                def _(vector, g=g):
                    vector.wait_ge(sA.h, g + 1)
                    sAm.inc(vector.tensor_mul(Am[:, :], psA[:, 0:128], tri[:, :]), 1)

                @block.scalar
                def _(scalar, g=g, ocol=ocol, ob=ob):
                    scalar.wait_ge(sOc.h, g + 1)
                    sOe.inc(scalar.copy(o_h[:, ocol], ob), 1)

                @block.vector
                def _(vector, g=g):
                    vector.wait_ge(sTd.h, g + 1)
                    sSt.inc(vector.tensor_add(Tst_f[:, :], Tst_f[:, :], psA[:, 512:640]), 1)
                    vector.wait_ge(sSt.h, sSt.n)
                    sSt.inc(vector.tensor_copy(Tst_bf[:, :], Tst_f[:, :]), 1)
                state_ready += 2

            if p == 3:
                @block.gpsimd
                def _(gpsimd, state_ready=state_ready):
                    gpsimd.wait_ge(sSt.h, state_ready)
                    gpsimd.memset(Tst_f[:, :], 0.0)
                    sSt.inc(gpsimd.memset(Tst_bf[:, :], 0.0), 1)
                state_ready += 1

        @block.sync
        def _(sync):
            sync.wait_ge(sOe.h, 128)
            st.inc(sync.dma_start(out=oo, in_=o_h[:, :]), 16)
            sync.wait_ge(st.h, st.n)
    return ctx



def build_l0post(ctx, n_experts=32):
    ctx.phase("lp")
    nc = ctx.nc
    x_i = ctx.dt("x", [D, TC], F32, "in")
    o_i = ctx.dt("o", [D, TC], BF16, "int")
    y0_i = ctx.dt("y0", [D, TC], BF16, "int")
    wot_i = ctx.dt("wot", [FC * D, 128], F32, "in")
    wg1t_i = ctx.dt("wg1t", [D, 128], F32, "in")
    wg2t_i = ctx.dt("wg2t", [128, D], F32, "in")
    gnorm_i = ctx.dt("gnorm", [128, FC], F32, "in")
    wct_i = ctx.dt("wct", [D, 40], F32, "in")
    biasbc_i = ctx.dt("biasbc", [128, 40], F32, "in")
    sgain_i = ctx.dt("sgain", [128, FC], F32, "in")
    rgain_i = ctx.dt("rgain", [128, FC], F32, "in")
    l0wgu_i = ctx.dt("l0wgu", [32, D, D], F32, "in")
    l0wdn_i = ctx.dt("l0wdn", [32, 512, D], F32, "in")
    x2_o = ctx.dt("x2", [D, TC], F32, "int")
    ss_o = ctx.dt("ss", [D, TC], BF16, "int")
    sr_o = ctx.dt("sr", [D, TC], BF16, "int")
    wT_d = ctx.dt("wT_d", [128, TC], BF16, "int")
    w_db = ctx.dt("w_db", [TC, 128], BF16, "int")

    xs = ctx.sb("xs", [128, FC * TC], F32)
    bufB = ctx.sb("bufB", [128, FC * TC], BF16)
    wgu = ctx.sb("wgu", [128, 2 * 8 * 1024], BF16)
    wdn = ctx.sb("wdn", [128, 4 * 1024], BF16)
    sq2 = ctx.sb("sq2", [128, TC], F32)
    rows = ctx.sb("rows", [1, 2 * TC], F32)
    hI = ctx.sb("hI", [128, 4 * TC], BF16)
    bc_sb = ctx.sb("bc_sb", [128, TC], BF16)
    y0t = ctx.sb("y0t", [128, 2 * NT], BF16)
    g1 = ctx.sb("g1s", [128, TC], BF16)
    wot = ctx.sb("wots", [128, 2 * 1024], BF16)
    sgt = ctx.sb("sgt", [128, TC], BF16)
    wg1t = ctx.sb("wg1ts", [128, FC * 128], BF16)
    wg2t = ctx.sb("wg2ts", [128, 1024], BF16)
    wct = ctx.sb("wcts", [128, FC * 40], BF16)
    biasbc = ctx.sb("biasbcs", [128, 40], F32)
    gnorm = ctx.sb("gnorms", [128, FC], F32)
    sgain = ctx.sb("sgains", [128, FC], F32)
    rgain = ctx.sb("rgains", [128, FC], F32)
    w48b = ctx.sb("w48b", [128, 96], BF16)
    wTs = ctx.sb("wTs", [128, TC], BF16)
    rt_et = ctx.sb("rt_et", [128, 40], F32)
    rt_sc = ctx.sb("rt_sc", [128, 40], F32)
    rt_sb = ctx.sb("rt_sb", [128, 40], F32)
    rt_kp = ctx.sb("rt_kp", [128, 40], F32)
    rt_t8 = ctx.sb("rt_t8", [128, 8], F32)
    rt_cols = ctx.sb("rt_cols", [128, 2], F32)
    ones1 = ctx.sb("ones1", [1, 128], F32)
    onesc = ctx.sb("onesc", [128, 1], F32)
    consts = ctx.sb("consts", [128, 4], F32)
    psA = ctx.ps("psA", [128, 2048], F32)
    psB = ctx.ps("psB", [128, 2048], F32)
    stg = ctx.sba("stg", [128, 2 * TC], BF16, nc.lookup_mloc(y0t).addr)

    ld = ctx.sem("ld")
    ldg = ctx.sem("ldg")
    yld = ctx.sem("yld")
    wl = [ctx.sem("wlA"), ctx.sem("wlB")]
    wld2 = ctx.sem("wld2")
    bcl = ctx.sem("bcl")
    wol = [ctx.sem("wolA"), ctx.sem("wolB")]
    mmd = ctx.sem("mmd")
    vec = ctx.sem("vec")
    act = ctx.sem("act")
    gp = ctx.sem("gp")
    rtm = ctx.sem("rtm")
    rtr = ctx.sem("rtr")
    st = ctx.sem("st")
    stw = [ctx.sem("stwA"), ctx.sem("stwB")]

    with nc.Block() as block:
        ctx.block = block

        @block.sync
        def _(sync):
            for fcb in range(FC):
                ld.inc(sync.dma_start(out=xs[:, fcb * TC:(fcb + 1) * TC],
                                      in_=x_i[fcb * 128:(fcb + 1) * 128, :]), 16)
            for fcb in range(FC):
                ld.inc(sync.dma_start(out=bufB[:, fcb * TC:(fcb + 1) * TC],
                                      in_=o_i[fcb * 128:(fcb + 1) * 128, :]), 16)
            ld.inc(sync.dma_start(out=gnorm[:, :], in_=gnorm_i), 16)
            ld.inc(sync.dma_start(out=sgain[:, :], in_=sgain_i), 16)
            ld.inc(sync.dma_start(out=rgain[:, :], in_=rgain_i), 16)
            ld.inc(sync.dma_start(out=biasbc[:, :], in_=biasbc_i), 16)

        @block.gpsimd
        def _(gpsimd):
            ldg.inc(gpsimd.dma_start(out=wct[:, :].rearrange("p (c o) -> p c o", c=FC),
                                    in_=wct_i.rearrange("(c p) o -> p c o", p=128)), 16)
            ldg.inc(gpsimd.dma_start(out=wg1t[:, :].rearrange("p (c o) -> p c o", c=FC),
                                    in_=wg1t_i.rearrange("(c p) o -> p c o", p=128)), 16)
            ldg.inc(gpsimd.dma_start(out=wg2t[:, :], in_=wg2t_i), 16)
            gpsimd.memset(ones1[:, :], 1.0)
            gpsimd.memset(onesc[:, :], 1.0)
            gpsimd.memset(consts[:, 0:1], 1e-5)
            gpsimd.memset(consts[:, 1:2], 1e-6)
            gpsimd.memset(w48b[:, 41:48], 0.0)
            gpsimd.memset(w48b[:, 89:96], 0.0)
            gp.inc(gpsimd.memset(wTs[:, :], 0.0), 1)

        @block.scalar
        def _(scalar):
            scalar.wait_ge(ld.h, ld.n)
            scalar.wait_ge(ldg.h, ldg.n)
            scalar.wait_ge(gp.h, 1)

        @block.tensor
        def _(tensor):
            tensor.wait_ge(ld.h, ld.n)
            tensor.wait_ge(ldg.h, ldg.n)
            tensor.wait_ge(gp.h, 1)

        emit_norm_fm(ctx, bufB, bufB, sq2, rows, ones1, onesc, psA, psB,
                     mmd, vec, act, TC, kind="rms", eps_ap=consts[0:1, 0:1],
                     gcol=gnorm)

        b_mmd = mmd.n

        @block.sync
        def _(sync):
            for fcb in range(FC):
                for tt in range(4):
                    idx = fcb * 4 + tt
                    if idx >= 1:
                        sync.wait_ge(mmd.h, b_mmd + idx)
                    yld.inc(sync.dma_start(
                        out=y0t[:, (idx % 2) * NT:(idx % 2 + 1) * NT],
                        in_=y0_i[fcb * 128:(fcb + 1) * 128, tt * NT:(tt + 1) * NT]), 16)

        b_vec_g1 = vec.n

        @block.tensor
        def _(tensor):
            tensor.wait_ge(vec.h, b_vec_g1)
            for fcb in range(FC):
                for tt in range(4):
                    idx = fcb * 4 + tt
                    tensor.wait_ge(yld.h, 16 * (idx + 1))
                    mm = tensor.matmul(psA[:, tt * NT:(tt + 1) * NT],
                                       wg1t[:, fcb * 128:(fcb + 1) * 128],
                                       y0t[:, (idx % 2) * NT:(idx % 2 + 1) * NT],
                                       start=(fcb == 0), stop=(fcb == FC - 1))
                    mmd.inc(mm, 1)

        @block.vector
        def _(vector):
            vector.wait_ge(mmd.h, mmd.n)
            vec.inc(vector.tensor_copy(g1[:, :], psA[:, 0:TC]), 1)

        b2_mmd, b2_act, b2_vec = mmd.n, act.n, vec.n

        @block.tensor
        def _(tensor):
            tensor.wait_ge(vec.h, b2_vec)
            for oc in range(FC):
                if oc >= 2:
                    tensor.wait_ge(act.h, b2_act + oc - 1)
                bank = psB if oc % 2 == 0 else psA
                for tt in range(4):
                    mm = tensor.matmul(bank[:, tt * NT:(tt + 1) * NT],
                                       wg2t[:, oc * 128:(oc + 1) * 128],
                                       g1[:, tt * NT:(tt + 1) * NT],
                                       start=True, stop=True)
                mmd.inc(mm, 1)

        @block.scalar
        def _(scalar):
            for oc in range(FC):
                scalar.wait_ge(mmd.h, b2_mmd + oc + 1)
                if oc >= 1:
                    scalar.wait_ge(vec.h, b2_vec + oc)
                bank = psB if oc % 2 == 0 else psA
                act.inc(scalar.activation(sgt[:, :], bank[:, 0:TC], AF.Silu), 1)

        @block.vector
        def _(vector):
            for oc in range(FC):
                vector.wait_ge(act.h, b2_act + oc + 1)
                osl = slice(oc * TC, (oc + 1) * TC)
                vec.inc(vector.tensor_mul(bufB[:, osl], bufB[:, osl], sgt[:, :]), 1)
            vector.wait_ge(vec.h, vec.n)
            vec.inc(vector.tensor_copy(rows[0:1, 0:1], onesc[0:1, 0:1]), 1)

        b3_mmd, b3_vec = mmd.n, vec.n

        @block.gpsimd
        def _(gpsimd):
            for oc in range(FC):
                if oc >= 2:
                    gpsimd.wait_ge(mmd.h, b3_mmd + oc)
                wol[oc % 2].inc(gpsimd.dma_start(
                    out=wot[:, (oc % 2) * 1024:(oc % 2 + 1) * 1024]
                        .rearrange("p (c o) -> p c o", c=FC),
                    in_=wot_i[oc * D:(oc + 1) * D, :].rearrange("(c p) o -> p c o", p=128)), 16)

        @block.tensor
        def _(tensor):
            tensor.wait_ge(vec.h, b3_vec)
            for oc in range(FC):
                tensor.wait_ge(wol[oc % 2].h, 16 * (oc // 2 + 1))
                if oc >= 2:
                    tensor.wait_ge(vec.h, b3_vec + oc - 1)
                bank = psA if oc % 2 == 0 else psB
                for tt in range(4):
                    for hc in range(FC):
                        mm = tensor.matmul(
                            bank[:, tt * NT:(tt + 1) * NT],
                            wot[:, (oc % 2) * 1024 + hc * 128:(oc % 2) * 1024 + (hc + 1) * 128],
                            bufB[:, hc * TC + tt * NT:hc * TC + (tt + 1) * NT],
                            start=(hc == 0), stop=(hc == FC - 1))
                mmd.inc(mm, 1)

        @block.vector
        def _(vector):
            for oc in range(FC):
                vector.wait_ge(mmd.h, b3_mmd + oc + 1)
                bank = psA if oc % 2 == 0 else psB
                osl = slice(oc * TC, (oc + 1) * TC)
                vec.inc(vector.tensor_add(xs[:, osl], xs[:, osl], bank[:, 0:TC]), 1)

        @block.scalar
        def _(scalar):
            scalar.wait_ge(vec.h, vec.n)

        @block.tensor
        def _(tensor):
            tensor.wait_ge(vec.h, vec.n)

        emit_norm_fm(ctx, xs, bufB, sq2, rows, ones1, onesc, psA, psB,
                     mmd, vec, act, TC, kind="ln", eps_ap=consts[0:1, 0:1])
        y1 = bufB

        b4_mmd, b4_vec, b4_act, b4_rtm, b4_rtr = \
            mmd.n, vec.n, act.n, rtm.n, rtr.n

        for i in range(16):
            @block.tensor
            def _(tensor, i=i):
                if i == 0:
                    tensor.wait_ge(vec.h, b4_vec)
                else:
                    tensor.wait_ge(act.h, b4_act + i)
                for fcb in range(FC):
                    mm = tensor.matmul(psA[:, 0:40],
                                       y1[:, fcb * TC + i * 128:fcb * TC + (i + 1) * 128],
                                       wct[:, fcb * 40:(fcb + 1) * 40],
                                       start=(fcb == 0), stop=(fcb == FC - 1))
                rtm.inc(mm, 1)

            @block.vector
            def _(vector, i=i):
                vector.wait_ge(rtm.h, b4_rtm + i + 1)
                vec.inc(vector.reduce_max(rt_cols[:, 0:1], psA[:, 0:40], axis=AX.X,
                                          negate=True), 1)

            @block.scalar
            def _(scalar, i=i):
                scalar.wait_ge(vec.h, b4_vec + 9 * i + 1)
                act.inc(scalar.activation(rt_et[:, :], psA[:, 0:40], AF.Exp,
                                          bias=rt_cols[:, 0:1]), 1)

            @block.vector
            def _(vector, i=i):
                b = b4_vec + 9 * i
                vector.wait_ge(act.h, b4_act + i + 1)
                vec.inc(vector.reduce_sum(rt_cols[:, 1:2], rt_et[:, :], axis=AX.X), 1)
                vector.wait_ge(vec.h, b + 2)
                vec.inc(vector.reciprocal(rt_cols[:, 1:2], rt_cols[:, 1:2]), 1)
                vector.wait_ge(vec.h, b + 3)
                vec.inc(vector.tensor_scalar_mul(rt_sc[:, :], rt_et[:, :], rt_cols[:, 1:2]), 1)
                vector.wait_ge(vec.h, b + 4)
                vec.inc(vector.tensor_add(rt_sb[:, :], rt_sc[:, :], biasbc[:, :]), 1)
                vector.wait_ge(vec.h, b + 5)
                vec.inc(vector.max(out=rt_t8[:, :], in_=rt_sb[:, :]), 1)
                vector.wait_ge(vec.h, b + 6)
                if i >= 2:
                    vector.wait_ge(stw[i % 2].h, 16 * (i // 2))
                vec.inc(vector.tensor_scalar(rt_kp[:, :], rt_sb[:, :], rt_t8[:, 5:6],
                                             None, ALU.is_ge), 1)
                vector.wait_ge(vec.h, b + 7)
                wsl = w48b[:, (i % 2) * 48:(i % 2) * 48 + 48]
                vec.inc(vector.scalar_tensor_tensor(wsl[:, 0:40], rt_sc[:, :], 2.5,
                                                    rt_kp[:, :], ALU.mult, ALU.mult), 1)
                vector.wait_ge(vec.h, b + 8)
                with nc.allow_low_precision(reason="router zero_w sum output rounded to bf16 like all routing weights"):
                    vec.inc(vector.reduce_sum(wsl[:, 40:41], wsl[:, 32:40], axis=AX.X), 1)

            @block.sync
            def _(sync, i=i):
                if i == 0:
                    sync.wait_ge(gp.h, 1)
                    st.inc(sync.dma_start(
                        out=w_db[:, 48:128].rearrange("(i p) c -> p i c", p=128),
                        in_=wTs[:, 0:16 * 80].rearrange("p (i c) -> p i c", i=16)), 16)
                if i < 15:
                    sync.wait_ge(vec.h, b4_vec + 9 * (i + 1) + 1)
                else:
                    sync.wait_ge(vec.h, b4_vec + 9 * 16 + 1)
                stw[i % 2].inc(sync.dma_start(out=w_db[i * 128:(i + 1) * 128, 0:48],
                                       in_=w48b[:, (i % 2) * 48:(i % 2) * 48 + 48]), 16)

        @block.vector
        def _(vector):
            vector.wait_ge(vec.h, vec.n)
            vec.inc(vector.tensor_copy(rt_cols[:, 0:1], rt_cols[:, 0:1]), 1)

        @block.sync
        def _(sync):
            sync.wait_ge(stw[0].h, 16 * 8)
            sync.wait_ge(stw[1].h, 16 * 8)
            sync.wait_ge(st.h, st.n)
            stw[0].inc(sync.dma_start(out=wTs[:, :], in_=w_db, transpose=True), 16)
            sync.wait_ge(stw[0].h, 16 * 9)
            stw[0].inc(sync.dma_start(out=wT_d, in_=wTs[:, :]), 16)

        b5_vec, b5_bcl = vec.n, bcl.n

        @block.sync
        def _(sync):
            sync.wait_ge(stw[0].h, 16 * 10)
            bcl.inc(sync.dma_start(out=bc_sb[:, :],
                                   in_=wT_d[40:41, :].broadcast_to((128, TC))), 16)

        @block.vector
        def _(vector):
            vector.wait_ge(bcl.h, b5_bcl + 16)
            for fcb in range(FC):
                osl = slice(fcb * TC, (fcb + 1) * TC)
                if fcb >= 1:
                    vector.wait_ge(vec.h, vec.n)
                vec.inc(vector.tensor_mul(sq2[:, 0:TC], y1[:, osl], bc_sb[:, :]), 1)
                vector.wait_ge(vec.h, vec.n)
                vec.inc(vector.tensor_add(xs[:, osl], xs[:, osl], sq2[:, 0:TC]), 1)

        b6 = dict(mmd=mmd.n, vec=vec.n, act=act.n, bcl=bcl.n)

        @block.gpsimd
        def _(gpsimd):
            for e in range(n_experts):
                if e >= 2:
                    gpsimd.wait_ge(mmd.h, b6["mmd"] + 16 * (e - 2) + 9)
                wl[e % 2].inc(gpsimd.dma_start(
                    out=wgu[:, (e % 2) * 8 * 1024:((e % 2) + 1) * 8 * 1024]
                        .rearrange("p (c o) -> p c o", c=FC),
                    in_=l0wgu_i[e].rearrange("(c p) o -> p c o", p=128)), 16)
                if e >= 1:
                    gpsimd.wait_ge(mmd.h, b6["mmd"] + 16 * e + 1)
                    gpsimd.wait_ge(wld2.h, 16 * e)
                wld2.inc(gpsimd.dma_start(
                    out=wdn[:, :].rearrange("p (c o) -> p c o", c=4),
                    in_=l0wdn_i[e].rearrange("(c p) o -> p c o", p=128)), 16)

        @block.sync
        def _(sync):
            for e in range(n_experts):
                if e >= 1:
                    sync.wait_ge(vec.h, b6["vec"] + 18 * e - 9)
                else:
                    sync.wait_ge(vec.h, b6["vec"])
                bcl.inc(sync.dma_start(out=bc_sb[:, :],
                                       in_=wT_d[e:e + 1, :].broadcast_to((128, TC))), 16)

        for e in range(n_experts):
            wgub = wgu[:, (e % 2) * 8 * 1024:]

            @block.tensor
            def _(tensor, e=e, wgub=wgub):
                tensor.wait_ge(wl[e % 2].h, 16 * (e // 2 + 1))
                if e == 0:
                    tensor.wait_ge(vec.h, vec.n)
                for j in range(4):
                    if j >= 1:
                        tensor.wait_ge(act.h, b6["act"] + 4 * e + j)
                        tensor.wait_ge(vec.h, b6["vec"] + 18 * e + 2 * j)
                    elif e >= 1:
                        tensor.wait_ge(act.h, b6["act"] + 4 * e)
                        tensor.wait_ge(vec.h, b6["vec"] + 18 * e)
                    for fcb in range(FC):
                        for tt in range(4):
                            mm = tensor.matmul(
                                psA[:, tt * NT:(tt + 1) * NT],
                                wgub[:, fcb * 1024 + j * 128:fcb * 1024 + (j + 1) * 128],
                                y1[:, fcb * TC + tt * NT:fcb * TC + (tt + 1) * NT],
                                start=(fcb == 0), stop=(fcb == FC - 1))
                    mmd.inc(mm, 1)
                    for fcb in range(FC):
                        for tt in range(4):
                            mm = tensor.matmul(
                                psB[:, tt * NT:(tt + 1) * NT],
                                wgub[:, fcb * 1024 + (4 + j) * 128:fcb * 1024 + (5 + j) * 128],
                                y1[:, fcb * TC + tt * NT:fcb * TC + (tt + 1) * NT],
                                start=(fcb == 0), stop=(fcb == FC - 1))
                    mmd.inc(mm, 1)
                tensor.wait_ge(wld2.h, 16 * (e + 1))
                for hcb in range(FC):
                    if hcb == 0:
                        tensor.wait_ge(vec.h, b6["vec"] + 18 * e + 9)
                        tensor.wait_ge(act.h, b6["act"] + 4 * e + 4)
                    if hcb >= 2:
                        tensor.wait_ge(vec.h, b6["vec"] + 18 * e + 9 + hcb - 1)
                    bank = psA if hcb % 2 == 0 else psB
                    for ic in range(4):
                        for tt in range(4):
                            mm = tensor.matmul(
                                bank[:, tt * NT:(tt + 1) * NT],
                                wdn[:, ic * 1024 + hcb * 128:ic * 1024 + (hcb + 1) * 128],
                                hI[:, ic * TC + tt * NT:ic * TC + (tt + 1) * NT],
                                start=(ic == 0), stop=(ic == 3))
                    mmd.inc(mm, 1)

            @block.scalar
            def _(scalar, e=e):
                for j in range(4):
                    scalar.wait_ge(mmd.h, b6["mmd"] + e * 16 + 2 * j + 1)
                    if j >= 1:
                        scalar.wait_ge(vec.h, b6["vec"] + 18 * e + 2 * j)
                    elif e >= 1:
                        scalar.wait_ge(vec.h, b6["vec"] + 18 * e)
                    act.inc(scalar.activation(sgt[:, :], psA[:, 0:TC], AF.Silu), 1)

            @block.vector
            def _(vector, e=e):
                for j in range(4):
                    vector.wait_ge(act.h, b6["act"] + 4 * e + j + 1)
                    vector.wait_ge(mmd.h, b6["mmd"] + e * 16 + 2 * j + 2)
                    if j == 0:
                        vector.wait_ge(bcl.h, b6["bcl"] + 16 * (e + 1))
                    hsl = slice(j * TC, (j + 1) * TC)
                    vec.inc(vector.tensor_mul(hI[:, hsl], sgt[:, :], psB[:, 0:TC]), 1)
                    vector.wait_ge(vec.h, b6["vec"] + 18 * e + 2 * j + 1)
                    vec.inc(vector.tensor_mul(hI[:, hsl], hI[:, hsl], bc_sb[:, :]), 1)
                vector.wait_ge(vec.h, b6["vec"] + 18 * e + 8)
                vec.inc(vector.tensor_copy(rt_cols[:, 0:1], rt_cols[:, 0:1]), 1)
                for hcb in range(FC):
                    vector.wait_ge(mmd.h, b6["mmd"] + e * 16 + 8 + hcb + 1)
                    bank = psA if hcb % 2 == 0 else psB
                    osl = slice(hcb * TC, (hcb + 1) * TC)
                    vec.inc(vector.tensor_add(xs[:, osl], xs[:, osl], bank[:, 0:TC]), 1)
                vector.wait_ge(vec.h, b6["vec"] + 18 * e + 17)
                vec.inc(vector.tensor_copy(rt_cols[:, 0:1], rt_cols[:, 0:1]), 1)

        @block.sync
        def _(sync):
            sync.wait_ge(vec.h, vec.n)
            for fcb in range(FC):
                st.inc(sync.dma_start(out=x2_o[fcb * 128:(fcb + 1) * 128, :],
                                      in_=xs[:, fcb * TC:(fcb + 1) * TC]), 16)

        @block.scalar
        def _(scalar):
            scalar.wait_ge(vec.h, vec.n)

        @block.tensor
        def _(tensor):
            tensor.wait_ge(vec.h, vec.n)

        emit_norm_fm(ctx, xs, bufB, sq2, rows, ones1, onesc, psA, psB,
                     mmd, vec, act, TC, kind="ln", eps_ap=consts[0:1, 0:1])
        y2 = bufB

        @block.scalar
        def _(scalar):
            scalar.wait_ge(vec.h, vec.n)

        @block.tensor
        def _(tensor):
            tensor.wait_ge(vec.h, vec.n)

        b8_mmd, b8_act, b8_vec, b8_st = mmd.n, act.n, vec.n, st.n

        @block.scalar
        def _(scalar):
            for fcb in range(FC):
                if fcb >= 1:
                    scalar.wait_ge(mmd.h, b8_mmd + fcb)
                act.inc(scalar.activation(sq2[:, 0:TC], y2[:, fcb * TC:(fcb + 1) * TC],
                                          AF.Square), 1)

        @block.tensor
        def _(tensor):
            for fcb in range(FC):
                tensor.wait_ge(act.h, b8_act + fcb + 1)
                for tt in range(4):
                    sl = slice(tt * NT, (tt + 1) * NT)
                    mm2 = tensor.matmul(psB[0:1, sl], onesc[:, :], sq2[:, sl],
                                        start=(fcb == 0), stop=(fcb == FC - 1))
                mmd.inc(mm2, 1)

        work2 = rows[0:1, TC:2 * TC]

        @block.vector
        def _(vector):
            vector.wait_ge(mmd.h, mmd.n)
            vec.inc(vector.tensor_scalar_mul(work2, psB[0:1, 0:TC], 1.0 / D), 1)

        @block.scalar
        def _(scalar):
            scalar.wait_ge(vec.h, vec.n)
            act.inc(scalar.activation(work2, work2, AF.Ln, bias=consts[0:1, 1:2]), 1)
            scalar.wait_ge(act.h, act.n)
            act.inc(scalar.activation(work2, work2, AF.Exp, scale=-0.5), 1)

        @block.tensor
        def _(tensor):
            tensor.wait_ge(act.h, act.n)
            for tt in range(4):
                sl = slice(tt * NT, (tt + 1) * NT)
                mm = tensor.matmul(psA[:, sl], ones1[0:1, :], work2[0:1, sl],
                                   start=True, stop=True)
            mmd.inc(mm, 1)

        b9_vec, b9_st = vec.n, st.n

        @block.vector
        def _(vector):
            vector.wait_ge(mmd.h, mmd.n)
            for fcb in range(FC):
                osl = slice(fcb * TC, (fcb + 1) * TC)
                if fcb >= 1:
                    vector.wait_ge(st.h, b9_st + 32 * fcb)
                vec.inc(vector.tensor_mul(sq2[:, 0:TC], y2[:, osl], psA[:, 0:TC]), 1)
                vector.wait_ge(vec.h, vec.n)
                vec.inc(vector.tensor_scalar_mul(stg[:, 0:TC], sq2[:, 0:TC],
                                                 rgain[:, fcb:fcb + 1]), 1)
                vector.wait_ge(vec.h, vec.n)
                vec.inc(vector.tensor_scalar_mul(stg[:, TC:2 * TC], sq2[:, 0:TC],
                                                 sgain[:, fcb:fcb + 1]), 1)

        @block.sync
        def _(sync):
            for fcb in range(FC):
                sync.wait_ge(vec.h, b9_vec + 3 * (fcb + 1))
                st.inc(sync.dma_start(out=sr_o[fcb * 128:(fcb + 1) * 128, :],
                                      in_=stg[:, 0:TC]), 16)
                st.inc(sync.dma_start(out=ss_o[fcb * 128:(fcb + 1) * 128, :],
                                      in_=stg[:, TC:2 * TC]), 16)
            sync.wait_ge(st.h, st.n)
    return ctx



def build_mid1(ctx):
    ctx.phase("m1")
    nc = ctx.nc
    NP = NPANEL
    CH = 64
    NCH = TC // CH
    ssf = ctx.dt("ssf", [NP * D, TC], BF16, "int")
    srf = ctx.dt("srf", [NP * D, TC], BF16, "int")
    zpad = ctx.dt("zpad", [128, CH], BF16, "in")
    wk_i = ctx.dt("wk1", [D, 128], F32, "in")
    wv_i = ctx.dt("wv1", [D, 128], F32, "in")
    wq_i = ctx.dt("wq1", [D, 128], F32, "in")
    wsmd_i = ctx.dt("wsmd", [D, 3], F32, "in")
    wg_i = ctx.dt("wg", [D, 1], F32, "in")
    w0_i = ctx.dt("w0", [128, 128], F32, "in")
    w1_i = ctx.dt("w1", [128, 128], F32, "in")
    gam_i = ctx.dt("gam", [128, 1], F32, "in")
    o1_o = ctx.dt("o1h", [128, NP * TC], BF16, "int")

    sst = ctx.sb("sst", [128, 2 * TC], BF16)
    srt = ctx.sb("srt", [128, 2 * TC], BF16)
    kb = ctx.sb("kb", [128, TC], BF16)
    vb = ctx.sb("vb", [128, TC], BF16)
    qb = ctx.sb("qb", [128, TC], BF16)
    csb = ctx.sb("csb", [128, FC * NCH], BF16)
    sgf = ctx.sb("sgf", [128, TC], F32)
    spf = ctx.sb("spf", [128, TC], F32)
    scf = ctx.sb("scf", [128, TC], F32)
    s1b = ctx.sb("s1b", [128, TC], BF16)
    Gb = ctx.sb("Gb", [128, TC], BF16)
    Hb = ctx.sb("Hb", [128, TC], BF16)
    k_tok = ctx.sb("k_tok", [128, TC], BF16)
    s1_tok = ctx.sb("s1_tok", [128, TC], BF16)
    G_tok = ctx.sb("G_tok", [128, TC], BF16)
    H_tok = ctx.sb("H_tok", [128, TC], BF16)
    lr_bcf = ctx.sb("lr_bcf", [128, TC], F32)
    valsf = ctx.sb("valsf", [128, TC], F32)
    vsq = ctx.sb("vsq", [128, 1024], F32)
    sxb = ctx.sb("sxb", [128, CH], BF16)
    o1s = ctx.sb("o1s", [128, NP * TC], BF16)
    rows = ctx.sb("rowsm", [1, 2 * TC + 2 * 1024 + 128], F32)
    gmbc = ctx.sb("gmbc", [128, 4 * NCH], F32)
    gdbc = ctx.sb("gdbc", [128, 4 * NCH], F32)
    MU = ctx.sb("MU", [128, 512], F32)
    memcat = ctx.sb("memcat", [128, 256], F32)
    wmb = ctx.sb("wmb", [128, 256], BF16)
    wkb = ctx.sb("wkb", [128, FC * 128], BF16)
    wvb = ctx.sb("wvb", [128, FC * 128], BF16)
    wqb = ctx.sb("wqb", [128, FC * 128], BF16)
    wsmdb = ctx.sb("wsmdb", [128, FC * 3], BF16)
    wgb = ctx.sb("wgb", [128, FC], BF16)
    w0b = ctx.sb("w0b", [128, 128], BF16)
    w1b = ctx.sb("w1b", [128, 128], BF16)
    w1tb = ctx.sb("w1tb", [128, 128], BF16)
    gampl = ctx.sb("gampl", [128, 1], F32)
    ones1 = ctx.sb("ones1m", [1, 128], F32)
    cones = ctx.sb("cones", [1, 128], F32)
    onesc = ctx.sb("onescm", [128, 1], F32)
    consts = ctx.sb("constsm", [128, 2], F32)
    idb = ctx.sb("idb", [128, 128], BF16)
    psA = ctx.ps("psA", [128, 2048], F32)
    psB = ctx.ps("psB", [128, 2048], F32)

    lr_row = rows[0:1, 0:TC]
    gate_row = rows[0:1, TC:2 * TC]
    rstd_row = rows[0:1, 2 * TC:2 * TC + 1024]
    grow = rows[0:1, 2 * TC + 1024:2 * TC + 2048]
    gm_row = rows[0:1, 2 * TC + 2048:2 * TC + 2048 + 32]
    gd_row = rows[0:1, 2 * TC + 2048 + 64:2 * TC + 2048 + 96]

    lds = [ctx.sem("ldsA"), ctx.sem("ldsB")]
    ldr = [ctx.sem("ldrA"), ctx.sem("ldrB")]
    wld = ctx.sem("wldm")
    wls = ctx.sem("wlsm")
    ini = ctx.sem("inim")
    mmd = ctx.sem("mmdm")
    vec = ctx.sem("vecm")
    act = ctx.sem("actm")
    gp = ctx.sem("gpm")
    dt = ctx.sem("dtm")
    sres = ctx.sem("sres")
    st = ctx.sem("stm")

    nload = [0]
    nloadr = [0]

    with nc.Block() as block:
        ctx.block = block

        @block.gpsimd
        def _(gpsimd):
            wld.inc(gpsimd.dma_start(out=wkb[:, :].rearrange("p (c i) -> p c i", c=FC),
                                     in_=wk_i.rearrange("(c p) i -> p c i", p=128)), 16)
            wld.inc(gpsimd.dma_start(out=wvb[:, :].rearrange("p (c i) -> p c i", c=FC),
                                     in_=wv_i.rearrange("(c p) i -> p c i", p=128)), 16)
            wld.inc(gpsimd.dma_start(out=wqb[:, :].rearrange("p (c i) -> p c i", c=FC),
                                     in_=wq_i.rearrange("(c p) i -> p c i", p=128)), 16)
            wld.inc(gpsimd.dma_start(out=wsmdb[:, :].rearrange("p (c i) -> p c i", c=FC),
                                     in_=wsmd_i.rearrange("(c p) i -> p c i", p=128)), 16)
            with nc.allow_non_contiguous_dma(reason="tiny 4KB per-head gate weight column"):
                wld.inc(gpsimd.dma_start(out=wgb[:, :].rearrange("p (c i) -> p c i", c=FC),
                                         in_=wg_i.rearrange("(c p) i -> p c i", p=128)), 16)
            wld.inc(gpsimd.dma_start(out=w0b[:, :], in_=w0_i), 16)
            wld.inc(gpsimd.dma_start(out=w1b[:, :], in_=w1_i), 16)
            gpsimd.memset(ones1[:, :], 1.0)
            gpsimd.memset(cones[:, :], 2.0 * 0.01 / 128.0)
            gpsimd.memset(onesc[:, :], 1.0)
            gpsimd.memset(consts[:, 0:1], 1e-6)
            gpsimd.memset(MU[:, :], 0.0)
            gpsimd.memset(o1s[:, :], 0.0)
            ini.inc(gpsimd.iota(scf[:, 0:128], pattern=[[1, 128]], base=0,
                                channel_multiplier=-1,
                                allow_small_or_imprecise_dtypes=True), 1)
            gpsimd.wait_ge(ini.h, ini.n)
            ini.inc(gpsimd.tensor_single_scalar(idb[:, :], scf[:, 0:128], 0.0,
                                                ALU.is_equal), 1)

        @block.sync
        def _(sync):
            wls.inc(sync.dma_start(out=memcat[:, 0:128], in_=w0_i), 16)
            wls.inc(sync.dma_start(out=memcat[:, 128:256], in_=w1_i), 16)
            wls.inc(sync.dma_start(out=gampl[:, :], in_=gam_i), 16)

        @block.vector
        def _(vector):
            vector.wait_ge(wld.h, wld.n)
            vector.wait_ge(wls.h, wls.n)
            vec.inc(vector.tensor_scalar_add(gampl[:, :], gampl[:, :], 1.0), 1)

        @block.tensor
        def _(tensor):
            tensor.wait_ge(wld.h, wld.n)
            tensor.wait_ge(wls.h, wls.n)
            tensor.wait_ge(ini.h, ini.n)
            mmd.inc(tensor.transpose(psA[:, 0:64].bitcast(BF16), w1b[:, :], idb[:, :]), 1)

        @block.vector
        def _(vector):
            vector.wait_ge(mmd.h, mmd.n)
            vec.inc(vector.tensor_copy(w1tb[:, :], psA[:, 0:64].bitcast(BF16)), 1)

        sres_ready = 0

        for p in range(NP):
            batch_first = (p % 4 == 0)
            batch_last = (p % 4 == 3)
            pc = (p % 4) * NCH
            bb = dict(mmd=mmd.n, vec=vec.n, act=act.n, gp=gp.n)

            @block.sync
            def _(sync, p=p, bb=bb):
                for fcb in range(FC):
                    idx = nload[0]
                    par = idx % 2
                    if fcb >= 2:
                        sync.wait_ge(mmd.h, bb["mmd"] + fcb)
                        sync.wait_ge(gp.h, bb["gp"] + fcb)
                    lds[par].inc(sync.dma_start(
                        out=sst[:, par * TC:(par + 1) * TC],
                        in_=ssf[p * D + fcb * 128:p * D + (fcb + 1) * 128, :]), 16)
                    nload[0] += 1

            @block.tensor
            def _(tensor, p=p, bb=bb):
                tensor.wait_ge(vec.h, bb["vec"])
                tensor.wait_ge(act.h, bb["act"])
                if p == 0:
                    tensor.wait_ge(wld.h, wld.n)
                    tensor.wait_ge(ini.h, ini.n)
                for fcb in range(FC):
                    idx = nload[0] - FC + fcb
                    par = idx % 2
                    tensor.wait_ge(lds[par].h, 16 * (idx // 2 + 1))
                    tile = sst[:, par * TC:(par + 1) * TC]
                    for tt in range(4):
                        tensor.matmul(psA[:, tt * NT:(tt + 1) * NT],
                                      wkb[:, fcb * 128:(fcb + 1) * 128],
                                      tile[:, tt * NT:(tt + 1) * NT],
                                      start=(fcb == 0), stop=(fcb == FC - 1))
                    for tt in range(4):
                        mm = tensor.matmul(psB[0:1, tt * NT:(tt + 1) * NT],
                                           wsmdb[:, fcb * 3:fcb * 3 + 1],
                                           tile[:, tt * NT:(tt + 1) * NT],
                                           start=(fcb == 0), stop=(fcb == FC - 1))
                    mmd.inc(mm, 1)

            @block.vector
            def _(vector, p=p, bb=bb):
                for fcb in range(FC):
                    idx = nload[0] - FC + fcb
                    par = idx % 2
                    vector.wait_ge(lds[par].h, 16 * (idx // 2 + 1))
                    if fcb == 0 and p >= 1:
                        vector.wait_ge(mmd.h, bb["mmd"])
                    tile = sst[:, par * TC:(par + 1) * TC]
                    with nc.allow_low_precision(reason="chunk-mean in bf16 feeds sigmoid gates; well within tolerance"):
                        gp.inc(vector.tensor_reduce(
                            csb[:, fcb * NCH:(fcb + 1) * NCH],
                            tile[:, :].rearrange("q (m c) -> q m c", c=CH),
                            axis=AX.X, op=ALU.add), 1)

            @block.vector
            def _(vector, bb=bb):
                vector.wait_ge(mmd.h, bb["mmd"] + FC)
                vec.inc(vector.tensor_copy(kb[:, :], psA[:, 0:TC]), 1)

            @block.scalar
            def _(scalar, bb=bb):
                scalar.wait_ge(mmd.h, bb["mmd"] + FC)
                act.inc(scalar.activation(lr_row, psB[0:1, 0:TC], AF.Sigmoid), 1)

            b1 = dict(mmd=mmd.n, vec=vec.n, act=act.n, gp=gp.n)

            @block.sync
            def _(sync, p=p, b1=b1):
                for fcb in range(FC):
                    idx = nload[0]
                    par = idx % 2
                    if fcb < 2:
                        sync.wait_ge(mmd.h, b1["mmd"] - 1 + fcb)
                        sync.wait_ge(gp.h, b1["gp"] - 1 + fcb)
                    else:
                        sync.wait_ge(mmd.h, b1["mmd"] + fcb)
                    lds[par].inc(sync.dma_start(
                        out=sst[:, par * TC:(par + 1) * TC],
                        in_=ssf[p * D + fcb * 128:p * D + (fcb + 1) * 128, :]), 16)
                    nload[0] += 1

            @block.tensor
            def _(tensor, p=p, b1=b1):
                tensor.wait_ge(vec.h, b1["vec"])
                tensor.wait_ge(act.h, b1["act"])
                for fcb in range(FC):
                    idx = nload[0] - FC + fcb
                    par = idx % 2
                    tensor.wait_ge(lds[par].h, 16 * (idx // 2 + 1))
                    tile = sst[:, par * TC:(par + 1) * TC]
                    for tt in range(4):
                        mm = tensor.matmul(psA[:, tt * NT:(tt + 1) * NT],
                                           wvb[:, fcb * 128:(fcb + 1) * 128],
                                           tile[:, tt * NT:(tt + 1) * NT],
                                           start=(fcb == 0), stop=(fcb == FC - 1))
                    mmd.inc(mm, 1)
                tensor.wait_ge(gp.h, b1["gp"])
                for fcb in range(FC):
                    tensor.matmul(psB[0:1, 0:NCH],
                                  wsmdb[:, fcb * 3 + 1:fcb * 3 + 2],
                                  csb[:, fcb * NCH:(fcb + 1) * NCH],
                                  start=(fcb == 0), stop=(fcb == FC - 1))
                    mm = tensor.matmul(psB[0:1, 512:512 + NCH],
                                       wsmdb[:, fcb * 3 + 2:fcb * 3 + 3],
                                       csb[:, fcb * NCH:(fcb + 1) * NCH],
                                       start=(fcb == 0), stop=(fcb == FC - 1))
                mmd.inc(mm, 1)

            @block.vector
            def _(vector, b1=b1):
                vector.wait_ge(mmd.h, b1["mmd"] + FC)
                vec.inc(vector.tensor_copy(vb[:, :], psA[:, 0:TC]), 1)

            @block.scalar
            def _(scalar, b1=b1):
                scalar.wait_ge(mmd.h, b1["mmd"] + FC + 1)
                act.inc(scalar.activation(gm_row, psB[0:1, 0:NCH],
                                          AF.Sigmoid, scale=1.0 / CH), 1)
                scalar.wait_ge(act.h, act.n)
                act.inc(scalar.activation(gd_row, psB[0:1, 512:512 + NCH],
                                          AF.Sigmoid, scale=-1.0 / CH), 1)

            b2 = dict(mmd=mmd.n, vec=vec.n, act=act.n)

            @block.tensor
            def _(tensor, b2=b2):
                tensor.wait_ge(vec.h, b2["vec"])
                tensor.wait_ge(act.h, b2["act"])
                for tt in range(4):
                    tensor.matmul(psA[:, tt * NT:(tt + 1) * NT],
                                  cones[0:1, :], lr_row[0:1, tt * NT:(tt + 1) * NT],
                                  start=True, stop=True)
                tensor.matmul(psB[:, 0:NCH], ones1[0:1, :], gm_row[0:1, 0:NCH],
                              start=True, stop=True)
                mm = tensor.matmul(psB[:, 512:512 + NCH], ones1[0:1, :],
                                   gd_row[0:1, 0:NCH], start=True, stop=True)
                mmd.inc(mm, 1)

            @block.vector
            def _(vector, b2=b2, pc=pc):
                vector.wait_ge(mmd.h, b2["mmd"] + 1)
                vec.inc(vector.tensor_copy(lr_bcf[:, :], psA[:, 0:TC]), 1)
                vector.wait_ge(vec.h, vec.n)
                vec.inc(vector.tensor_copy(gmbc[:, pc:pc + NCH], psB[:, 0:NCH]), 1)
                vector.wait_ge(vec.h, vec.n)
                vec.inc(vector.tensor_copy(gdbc[:, pc:pc + NCH],
                                           psB[:, 512:512 + NCH]), 1)

            b3 = dict(mmd=mmd.n, vec=vec.n, act=act.n)

            @block.sync
            def _(sync, p=p, b3=b3, batch_last=batch_last):
                for fcb in range(FC):
                    idx = nloadr[0]
                    par = idx % 2
                    if fcb >= 2:
                        sync.wait_ge(mmd.h, b3["mmd"] + fcb)
                    ldr[par].inc(sync.dma_start(
                        out=srt[:, par * TC:par * TC + TC - 63],
                        in_=srf[p * D + fcb * 128:p * D + (fcb + 1) * 128, 63:TC]), 16)
                    if batch_last:
                        ldr[par].inc(sync.dma_start(
                            out=srt[:, par * TC + TC - 63:(par + 1) * TC],
                            in_=zpad[:, 0:63]), 16)
                    else:
                        ldr[par].inc(sync.dma_start(
                            out=srt[:, par * TC + TC - 63:(par + 1) * TC],
                            in_=srf[(p + 1) * D + fcb * 128:(p + 1) * D + (fcb + 1) * 128,
                                    0:63]), 16)
                    nloadr[0] += 1

            @block.tensor
            def _(tensor, p=p, b3=b3):
                tensor.wait_ge(vec.h, b3["vec"])
                for fcb in range(FC):
                    idx = nloadr[0] - FC + fcb
                    par = idx % 2
                    tensor.wait_ge(ldr[par].h, 32 * (idx // 2 + 1))
                    tile = srt[:, par * TC:(par + 1) * TC]
                    for tt in range(4):
                        tensor.matmul(psA[:, tt * NT:(tt + 1) * NT],
                                      wqb[:, fcb * 128:(fcb + 1) * 128],
                                      tile[:, tt * NT:(tt + 1) * NT],
                                      start=(fcb == 0), stop=(fcb == FC - 1))
                    for tt in range(4):
                        mm = tensor.matmul(psB[0:1, tt * NT:(tt + 1) * NT],
                                           wgb[:, fcb:fcb + 1],
                                           tile[:, tt * NT:(tt + 1) * NT],
                                           start=(fcb == 0), stop=(fcb == FC - 1))
                    mmd.inc(mm, 1)

            @block.vector
            def _(vector, b3=b3):
                vector.wait_ge(mmd.h, b3["mmd"] + FC)
                vec.inc(vector.tensor_copy(qb[:, :], psA[:, 0:TC]), 1)

            @block.scalar
            def _(scalar, b3=b3):
                scalar.wait_ge(mmd.h, b3["mmd"] + FC)
                act.inc(scalar.activation(gate_row, psB[0:1, 0:TC], AF.Sigmoid), 1)

            b4 = dict(mmd=mmd.n, vec=vec.n, act=act.n)

            @block.tensor
            def _(tensor, b4=b4):
                tensor.wait_ge(vec.h, b4["vec"])
                tensor.wait_ge(act.h, b4["act"])
                for tt in range(4):
                    mm = tensor.matmul(psB[:, tt * NT:(tt + 1) * NT],
                                       w0b[:, :], kb[:, tt * NT:(tt + 1) * NT],
                                       start=True, stop=True)
                mmd.inc(mm, 1)

            @block.scalar
            def _(scalar, b4=b4):
                scalar.wait_ge(mmd.h, b4["mmd"] + 1)
                act.inc(scalar.activation(sgf[:, :], psB[:, 0:TC], AF.Sigmoid), 1)

            @block.vector
            def _(vector, b4=b4):
                vector.wait_ge(act.h, b4["act"] + 1)
                vec.inc(vector.tensor_mul(s1b[:, :], sgf[:, :], psB[:, 0:TC]), 1)
                vector.wait_ge(vec.h, vec.n)
                vec.inc(vector.tensor_mul(scf[:, :], s1b[:, :], sgf[:, :]), 1)
                vector.wait_ge(vec.h, vec.n)
                vec.inc(vector.scalar_tensor_tensor(spf[:, :], scf[:, :], -1.0,
                                                    sgf[:, :], ALU.mult, ALU.add), 1)
                vector.wait_ge(vec.h, vec.n)
                vec.inc(vector.tensor_add(spf[:, :], spf[:, :], s1b[:, :]), 1)

            @block.tensor
            def _(tensor, b4=b4):
                tensor.wait_ge(vec.h, b4["vec"] + 1)
                for tt in range(4):
                    mm = tensor.matmul(psA[:, tt * NT:(tt + 1) * NT],
                                       w1b[:, :], s1b[:, tt * NT:(tt + 1) * NT],
                                       start=True, stop=True)
                mmd.inc(mm, 1)

            @block.vector
            def _(vector, b4=b4):
                vector.wait_ge(mmd.h, b4["mmd"] + 2)
                vec.inc(vector.tensor_sub(scf[:, :], psA[:, 0:TC], vb[:, :]), 1)
                vector.wait_ge(vec.h, vec.n)
                vec.inc(vector.tensor_mul(Gb[:, :], scf[:, :], lr_bcf[:, :]), 1)

            @block.tensor
            def _(tensor, b4=b4):
                tensor.wait_ge(vec.h, b4["vec"] + 6)
                tensor.wait_ge(act.h, b4["act"] + 1)
                for tt in range(4):
                    mm = tensor.matmul(psB[:, tt * NT:(tt + 1) * NT],
                                       w1tb[:, :], Gb[:, tt * NT:(tt + 1) * NT],
                                       start=True, stop=True)
                mmd.inc(mm, 1)

            @block.vector
            def _(vector, b4=b4):
                vector.wait_ge(mmd.h, b4["mmd"] + 3)
                vec.inc(vector.tensor_mul(Hb[:, :], psB[:, 0:TC], spf[:, :]), 1)
                vector.wait_ge(vec.h, vec.n)
                vec.inc(vector.tensor_copy(rows[0:1, 0:1], ones1[0:1, 0:1]), 1)

            @block.scalar
            def _(scalar, p=p, b4=b4):
                scalar.wait_ge(vec.h, b4["vec"] + 8)
                for m in range(16):
                    dt.inc(scalar.dma_start(out=k_tok[:, m * 128:(m + 1) * 128],
                                            in_=kb[:, m * 128:(m + 1) * 128],
                                            transpose=True), 16)
                    dt.inc(scalar.dma_start(out=s1_tok[:, m * 128:(m + 1) * 128],
                                            in_=s1b[:, m * 128:(m + 1) * 128],
                                            transpose=True), 16)
                    dt.inc(scalar.dma_start(out=G_tok[:, m * 128:(m + 1) * 128],
                                            in_=Gb[:, m * 128:(m + 1) * 128],
                                            transpose=True), 16)
                    dt.inc(scalar.dma_start(out=H_tok[:, m * 128:(m + 1) * 128],
                                            in_=Hb[:, m * 128:(m + 1) * 128],
                                            transpose=True), 16)

            if batch_first:
                @block.gpsimd
                def _(gpsimd, p=p):
                    if p >= 1:
                        gpsimd.wait_ge(vec.h, vec.n)
                    sres.inc(gpsimd.memset(MU[:, :], 0.0), 1)
                sres_ready += 1

            b5 = dict(mmd=mmd.n, vec=vec.n, act=act.n)

            for m in range(NCH):
                base = (m % 2) * 64
                t128 = m // 2
                tsl = slice(t128 * 128, (t128 + 1) * 128)
                csl = slice(m * CH, (m + 1) * CH)
                gcol = pc + m

                @block.tensor
                def _(tensor, p=p, m=m, base=base, tsl=tsl, csl=csl, b5=b5,
                      sres_ready=sres_ready):
                    if m == 0:
                        tensor.wait_ge(dt.h, 64 * 16 * (p + 1))
                        tensor.wait_ge(vec.h, b5["vec"])
                        tensor.wait_ge(sres.h, sres_ready)
                    if m >= 1:
                        tensor.wait_ge(vec.h, b5["vec"] + 3 * m)
                    mm = tensor.matmul(psA[:, 1024:1152],
                                       k_tok[base:base + 64, tsl],
                                       H_tok[base:base + 64, tsl],
                                       start=True, stop=True, skip_group_check=True)
                    mm = tensor.matmul(psA[:, 1152:1280],
                                       s1_tok[base:base + 64, tsl],
                                       G_tok[base:base + 64, tsl],
                                       start=True, stop=True, skip_group_check=True)
                    mmd.inc(mm, 1)
                    tensor.wait_ge(vec.h, b5["vec"] + 3 * m + 3)
                    if m >= 1:
                        tensor.wait_ge(act.h, b5["act"] + 3 * m)
                    mm = tensor.matmul(psB[:, 1024:1088], wmb[:, 0:128], qb[:, csl],
                                       start=True, stop=True)
                    mmd.inc(mm, 1)
                    tensor.wait_ge(act.h, b5["act"] + 3 * m + 2)
                    mm = tensor.matmul(psB[:, 1536:1600], wmb[:, 128:256], sxb[:, :],
                                       start=True, stop=True)
                    mmd.inc(mm, 1)

                @block.vector
                def _(vector, m=m, gcol=gcol, b5=b5, sres_ready=sres_ready):
                    vector.wait_ge(mmd.h, b5["mmd"] + 3 * m + 1)
                    if m == 0:
                        vector.wait_ge(sres.h, sres_ready)
                    vec.inc(vector.scalar_tensor_tensor(MU[:, 0:256], MU[:, 0:256],
                                                        gmbc[:, gcol:gcol + 1],
                                                        psA[:, 1024:1280],
                                                        ALU.mult, ALU.subtract), 1)
                    vector.wait_ge(vec.h, vec.n)
                    vec.inc(vector.scalar_tensor_tensor(MU[:, 256:512], MU[:, 256:512],
                                                        gdbc[:, gcol:gcol + 1],
                                                        MU[:, 0:256],
                                                        ALU.mult, ALU.add), 1)
                    vector.wait_ge(vec.h, vec.n)
                    vec.inc(vector.tensor_add(wmb[:, :], memcat[:, :],
                                              MU[:, 256:512]), 1)

                @block.scalar
                def _(scalar, m=m, b5=b5):
                    scalar.wait_ge(mmd.h, b5["mmd"] + 3 * m + 2)
                    act.inc(scalar.activation(sxb[:, :], psB[:, 1024:1088], AF.Silu), 1)
                    act.inc(scalar.copy(consts[0:1, 1:2], ones1[0:1, 0:1]), 1)
                    scalar.wait_ge(mmd.h, b5["mmd"] + 3 * m + 3)
                    act.inc(scalar.copy(valsf[:, m * CH:(m + 1) * CH],
                                        psB[:, 1536:1600]), 1)

            b6 = dict(mmd=mmd.n, vec=vec.n, act=act.n)

            for sc in range(2):
                vsl = slice(sc * 1024, sc * 1024 + 1024)
                ob = (p // 4) * 8192 + (p % 4) * 2048 + sc * 1024 + 63
                w = 961 if (batch_last and sc == 1) else 1024

                @block.scalar
                def _(scalar, sc=sc, vsl=vsl, b6=b6):
                    scalar.wait_ge(act.h, b6["act"] - 3 * NCH + 3 * (16 * (sc + 1)))
                    if sc == 1:
                        scalar.wait_ge(mmd.h, mmd.n)
                    act.inc(scalar.activation(vsq[:, :], valsf[:, vsl], AF.Square), 1)

                @block.tensor
                def _(tensor, sc=sc):
                    tensor.wait_ge(act.h, act.n)
                    tensor.wait_ge(vec.h, vec.n)
                    for u in range(2):
                        mm = tensor.matmul(psB[0:1, u * NT:(u + 1) * NT], onesc[:, :],
                                           vsq[:, u * NT:(u + 1) * NT],
                                           start=True, stop=True)
                    mmd.inc(mm, 1)

                @block.vector
                def _(vector):
                    vector.wait_ge(mmd.h, mmd.n)
                    vec.inc(vector.tensor_scalar_mul(rstd_row, psB[0:1, 0:1024],
                                                     1.0 / 128.0), 1)

                @block.scalar
                def _(scalar, sc=sc):
                    scalar.wait_ge(vec.h, vec.n)
                    act.inc(scalar.activation(rstd_row, rstd_row, AF.Ln,
                                              bias=consts[0:1, 0:1]), 1)
                    scalar.wait_ge(act.h, act.n)
                    act.inc(scalar.activation(rstd_row, rstd_row, AF.Exp, scale=-0.5), 1)

                @block.vector
                def _(vector, sc=sc):
                    vector.wait_ge(act.h, act.n)
                    vec.inc(vector.tensor_mul(grow, rstd_row,
                                              gate_row[0:1, sc * 1024:sc * 1024 + 1024]), 1)

                @block.tensor
                def _(tensor, sc=sc):
                    tensor.wait_ge(vec.h, vec.n)
                    for u in range(2):
                        mm = tensor.matmul(psB[:, u * NT:(u + 1) * NT], ones1[0:1, :],
                                           grow[0:1, u * NT:(u + 1) * NT],
                                           start=True, stop=True)
                    mmd.inc(mm, 1)

                @block.vector
                def _(vector, sc=sc, vsl=vsl, ob=ob, w=w):
                    vector.wait_ge(mmd.h, mmd.n)
                    vec.inc(vector.scalar_tensor_tensor(
                        o1s[:, ob:ob + w], valsf[:, sc * 1024:sc * 1024 + w],
                        gampl[:, 0:1], psB[:, 0:w], ALU.mult, ALU.mult), 1)
                    vector.wait_ge(vec.h, vec.n)
                    vec.inc(vector.tensor_copy(rows[0:1, 0:1], ones1[0:1, 0:1]), 1)

        @block.sync
        def _(sync):
            sync.wait_ge(vec.h, vec.n)
            st.inc(sync.dma_start(out=o1_o, in_=o1s[:, :]), 16)
            sync.wait_ge(st.h, st.n)
    return ctx



def build_post1(ctx):
    ctx.phase("q1")
    nc = ctx.nc
    TH = TC // 2
    x2_i = ctx.dt("x2", [D, TC], F32, "int")
    vals_i = ctx.dt("vals", [D, TC], BF16, "int")
    wcmb_i = ctx.dt("wcmb", [FC * D, 128], F32, "in")
    pwt_i = ctx.dt("pwt", [D, 64], F32, "in")
    u123_i = ctx.dt("u123", [64, 48], F32, "in")
    b2_i = ctx.dt("b2", [12, D], F32, "in")
    w1_i = ctx.dt("w1s", [8 * 12, D, NT], F32, "in")
    w2_i = ctx.dt("w2s", [12 * 32, 128, D], F32, "in")
    out_o = ctx.dt("out", [D, TC], F32, "out")
    zer_i = ctx.dt("zer", [TC, 80], BF16, "in")
    w_db = ctx.dt("w_db", [TC, 128], BF16, "int")
    wT_d = ctx.dt("wT_d", [128, TC], BF16, "int")

    xs = ctx.sb("xs", [128, FC * TC], F32)
    h_g = ctx.sb("h_g", [128, 32 * 1024], BF16)
    y3h = ctx.sb("y3h", [128, FC * TH], BF16)
    y3s = ctx.sb("y3s", [128, FC * TH], BF16)
    w1b = ctx.sb("w1b", [128, 2 * FC * NT], BF16)
    w2b = ctx.sb("w2b", [128, 2 * D], BF16)
    cmbb = ctx.sb("cmbb", [128, 2 * 1024], BF16)
    sq2 = ctx.sb("sq2", [128, TH], F32)
    rows = ctx.sb("rows", [1, 2 * TH], F32)
    bc1 = ctx.sb("bc1", [128, TH], BF16)
    xab = ctx.sb("xab", [64, TH], BF16)
    u123 = ctx.sb("u123s", [64, 48], BF16)
    b2b = ctx.sb("b2b", [44, FC * 128], BF16)
    pwtb = ctx.sb("pwtb", [128, FC * 64], BF16)
    wTs = ctx.sb("wTs", [128, TH], BF16)
    w48b = ctx.sb("w48b", [128, 96], BF16)
    rt_et = ctx.sb("rt_et", [128, 3 * 16], F32)
    rt_t8 = ctx.sb("rt_t8", [128, 8], F32)
    rt_kp = ctx.sb("rt_kp", [128, 16], F32)
    rt_cols = ctx.sb("rt_cols", [128, 4], F32)
    ones1 = ctx.sb("ones1", [1, 128], F32)
    onesc = ctx.sb("onesc", [128, 1], F32)
    consts = ctx.sb("consts", [128, 4], F32)
    psA = ctx.ps("psA", [128, 2048], F32)
    psB = ctx.ps("psB", [128, 2048], F32)

    ld = ctx.sem("ld")
    cli = ctx.sem("cli")
    cld = [ctx.sem("cldA"), ctx.sem("cldB")]
    w1l = [ctx.sem("w1lA"), ctx.sem("w1lB")]
    w2l = [ctx.sem("w2lA"), ctx.sem("w2lB")]
    bcl = ctx.sem("bcl")
    mmd = ctx.sem("mmd")
    vec = ctx.sem("vec")
    act = ctx.sem("act")
    gp = ctx.sem("gp")
    stw = [ctx.sem("stwA"), ctx.sem("stwB")]
    st = ctx.sem("st")

    with nc.Block() as block:
        ctx.block = block

        @block.sync
        def _(sync):
            for fcb in range(FC):
                ld.inc(sync.dma_start(out=xs[:, fcb * TC:(fcb + 1) * TC],
                                      in_=x2_i[fcb * 128:(fcb + 1) * 128, :]), 16)
            for fcb in range(FC):
                ld.inc(sync.dma_start(out=h_g[:, fcb * TC:(fcb + 1) * TC],
                                      in_=vals_i[fcb * 128:(fcb + 1) * 128, :]), 16)

        @block.gpsimd
        def _(gpsimd):
            cli.inc(gpsimd.dma_start(out=pwtb[:, :].rearrange("p (c o) -> p c o", c=FC),
                                     in_=pwt_i.rearrange("(c p) o -> p c o", p=128)), 16)
            cli.inc(gpsimd.dma_start(out=u123[:, :], in_=u123_i), 16)
            cli.inc(gpsimd.dma_start(out=b2b[32:44, :].rearrange("p (c o) -> p c o", c=FC),
                                     in_=b2_i.rearrange("l (c o) -> l c o", c=FC)), 16)
            gp.inc(gpsimd.memset(ones1[:, :], 1.0), 1)
            gpsimd.memset(onesc[:, :], 1.0)
            gpsimd.memset(consts[:, 0:1], 1e-5)
            gpsimd.memset(w48b[:, :], 0.0)
            gp.inc(gpsimd.memset(wTs[:, :], 0.0), 1)

        b0 = dict(mmd=mmd.n, vec=vec.n)

        @block.gpsimd
        def _(gpsimd, b0=b0):
            for oc in range(FC):
                if oc >= 2:
                    gpsimd.wait_ge(mmd.h, b0["mmd"] + oc)
                cld[oc % 2].inc(gpsimd.dma_start(
                    out=cmbb[:, (oc % 2) * 1024:(oc % 2 + 1) * 1024]
                        .rearrange("p (c o) -> p c o", c=FC),
                    in_=wcmb_i[oc * D:(oc + 1) * D, :].rearrange("(c p) o -> p c o", p=128)), 16)

        @block.tensor
        def _(tensor, b0=b0):
            tensor.wait_ge(ld.h, 16 * 16)
            for oc in range(FC):
                tensor.wait_ge(cld[oc % 2].h, 16 * (oc // 2 + 1))
                if oc >= 2:
                    tensor.wait_ge(vec.h, b0["vec"] + oc - 1)
                bank = psA if oc % 2 == 0 else psB
                for tt in range(4):
                    for hc in range(FC):
                        mm = tensor.matmul(
                            bank[:, tt * NT:(tt + 1) * NT],
                            cmbb[:, (oc % 2) * 1024 + hc * 128:(oc % 2) * 1024 + (hc + 1) * 128],
                            h_g[:, hc * TC + tt * NT:hc * TC + (tt + 1) * NT],
                            start=(hc == 0), stop=(hc == FC - 1))
                mmd.inc(mm, 1)

        @block.vector
        def _(vector, b0=b0):
            for oc in range(FC):
                vector.wait_ge(mmd.h, b0["mmd"] + oc + 1)
                bank = psA if oc % 2 == 0 else psB
                osl = slice(oc * TC, (oc + 1) * TC)
                vec.inc(vector.tensor_add(xs[:, osl], xs[:, osl], bank[:, 0:TC]), 1)
            vector.wait_ge(vec.h, vec.n)
            vec.inc(vector.tensor_copy(rows[0:1, 0:1], onesc[0:1, 0:1]), 1)

        for th in range(2):
            ho = th * TH

            bl = dict(mmd=mmd.n, vec=vec.n, act=act.n)

            @block.scalar
            def _(scalar, bl=bl):
                scalar.wait_ge(vec.h, bl["vec"])

            @block.tensor
            def _(tensor, bl=bl):
                tensor.wait_ge(vec.h, bl["vec"])
                tensor.wait_ge(gp.h, 2)

            emit_norm_fm(ctx, xs, y3h, sq2, rows, ones1, onesc, psA, psB,
                         mmd, vec, act, TH, kind="ln", eps_ap=consts[0:1, 0:1],
                         in_stride=TC, in_off=ho)

            b1 = dict(mmd=mmd.n, vec=vec.n, act=act.n)

            @block.tensor
            def _(tensor, b1=b1, ho=ho):
                tensor.wait_ge(vec.h, b1["vec"])
                tensor.wait_ge(cli.h, 48)
                for fcb in range(FC):
                    for tt in range(2):
                        mm = tensor.matmul(psA[0:64, tt * NT:(tt + 1) * NT],
                                           pwtb[:, fcb * 64:(fcb + 1) * 64],
                                           y3h[:, fcb * TH + tt * NT:fcb * TH + (tt + 1) * NT],
                                           start=(fcb == 0), stop=(fcb == FC - 1))
                mmd.inc(mm, 1)

            @block.vector
            def _(vector, b1=b1):
                vector.wait_ge(mmd.h, b1["mmd"] + 1)
                vec.inc(vector.tensor_copy(xab[:, :], psA[0:64, 0:TH]), 1)
                vector.wait_ge(vec.h, vec.n)
                vec.inc(vector.tensor_copy(rt_cols[:, 0:1], onesc[:, 0:1]), 1)

            b2r = dict(mmd=mmd.n, vec=vec.n, act=act.n, stw0=stw[0].n, stw1=stw[1].n,
                       st=st.n)
            NV = 22

            for i in range(8):
                @block.tensor
                def _(tensor, i=i, b2r=b2r):
                    if i == 0:
                        tensor.wait_ge(vec.h, b2r["vec"])
                    else:
                        tensor.wait_ge(act.h, b2r["act"] + 3 * i)
                    for r in range(3):
                        mm = tensor.matmul(psB[:, r * 512:r * 512 + 12],
                                           xab[:, i * 128:(i + 1) * 128],
                                           u123[:, r * 16:r * 16 + 12],
                                           start=True, stop=True)
                    mmd.inc(mm, 1)

                @block.vector
                def _(vector, i=i, b2r=b2r):
                    b = b2r["vec"] + NV * i
                    vector.wait_ge(mmd.h, b2r["mmd"] + i + 1)
                    vector.wait_ge(vec.h, b)
                    for r in range(3):
                        vec.inc(vector.reduce_max(rt_cols[:, r:r + 1],
                                                  psB[:, r * 512:r * 512 + 12],
                                                  axis=AX.X, negate=True), 1)
                        vector.wait_ge(vec.h, b + r + 1)

                @block.scalar
                def _(scalar, i=i, b2r=b2r):
                    scalar.wait_ge(vec.h, b2r["vec"] + NV * i + 3)
                    for r in range(3):
                        act.inc(scalar.activation(
                            rt_et[:, r * 16:r * 16 + 12], psB[:, r * 512:r * 512 + 12],
                            AF.Exp, bias=rt_cols[:, r:r + 1],
                            scale=1.0 / (1.0 + 1e-8)), 1)

                @block.vector
                def _(vector, i=i, b2r=b2r):
                    b = b2r["vec"] + NV * i + 3
                    vector.wait_ge(act.h, b2r["act"] + 3 * i + 3)
                    if i >= 2:
                        vector.wait_ge(stw[i % 2].h,
                                       (b2r["stw0"] if i % 2 == 0 else b2r["stw1"])
                                       + 16 * (i // 2))
                    nb = 0
                    for r in range(3):
                        ww = w48b[:, (i % 2) * 48 + r * 16:(i % 2) * 48 + r * 16 + 12]
                        vec.inc(vector.max(out=rt_t8[:, :],
                                           in_=rt_et[:, r * 16:r * 16 + 12]), 1)
                        nb += 1; vector.wait_ge(vec.h, b + nb)
                        vec.inc(vector.tensor_scalar(rt_kp[:, 0:12],
                                                     rt_et[:, r * 16:r * 16 + 12],
                                                     rt_t8[:, 2:3], None, ALU.is_ge), 1)
                        nb += 1; vector.wait_ge(vec.h, b + nb)
                        vec.inc(vector.tensor_mul(rt_kp[:, 0:12], rt_kp[:, 0:12],
                                                  rt_et[:, r * 16:r * 16 + 12]), 1)
                        nb += 1; vector.wait_ge(vec.h, b + nb)
                        vec.inc(vector.reduce_sum(rt_cols[:, 3:4], rt_kp[:, 0:12],
                                                  axis=AX.X), 1)
                        nb += 1; vector.wait_ge(vec.h, b + nb)
                        vec.inc(vector.reciprocal(rt_cols[:, 3:4], rt_cols[:, 3:4]), 1)
                        nb += 1; vector.wait_ge(vec.h, b + nb)
                        with nc.allow_low_precision(reason="router combine weights rounded to bf16"):
                            vec.inc(vector.tensor_scalar_mul(ww, rt_kp[:, 0:12],
                                                             rt_cols[:, 3:4]), 1)
                        nb += 1; vector.wait_ge(vec.h, b + nb)
                    vec.inc(vector.tensor_copy(rt_cols[:, 0:1], rt_cols[:, 0:1]), 1)

                @block.sync
                def _(sync, i=i, th=th, b2r=b2r):
                    sync.wait_ge(vec.h, b2r["vec"] + NV * (i + 1))
                    if th == 0 and i == 0:
                        st.inc(sync.dma_start(out=w_db[:, 48:128], in_=zer_i), 16)
                    stw[i % 2].inc(sync.dma_start(
                        out=w_db[th * TH + i * 128:th * TH + (i + 1) * 128, 0:48],
                        in_=w48b[:, (i % 2) * 48:(i % 2) * 48 + 48]), 16)

            @block.sync
            def _(sync, th=th, b2r=b2r):
                sync.wait_ge(stw[0].h, b2r["stw0"] + 16 * 4)
                sync.wait_ge(stw[1].h, b2r["stw1"] + 16 * 4)
                sync.wait_ge(st.h, st.n)
                st.inc(sync.dma_start(out=wTs[:, :],
                                      in_=w_db[th * TH:(th + 1) * TH, :],
                                      transpose=True), 16)
                sync.wait_ge(st.h, st.n)
                st.inc(sync.dma_start(out=wT_d[:, th * TH:(th + 1) * TH],
                                      in_=wTs[:, :]), 16)

            b3p = dict(mmd=mmd.n, vec=vec.n)

            @block.tensor
            def _(tensor, th=th, b3p=b3p):
                tensor.wait_ge(st.h, st.n)
                for Hc in range(FC):
                    bank = psA if Hc % 2 == 0 else psB
                    if Hc >= 2:
                        tensor.wait_ge(vec.h, b3p["vec"] + Hc - 1)
                    for tt in range(2):
                        mm = tensor.matmul(bank[:, tt * NT:(tt + 1) * NT],
                                           b2b[32:44, Hc * 128:(Hc + 1) * 128],
                                           wTs[32:44, tt * NT:(tt + 1) * NT],
                                           start=True, stop=True)
                    mmd.inc(mm, 1)

            @block.vector
            def _(vector, th=th, ho=ho, b3p=b3p):
                for Hc in range(FC):
                    vector.wait_ge(mmd.h, b3p["mmd"] + Hc + 1)
                    bank = psA if Hc % 2 == 0 else psB
                    osl = slice(Hc * TC + ho, Hc * TC + ho + TH)
                    vec.inc(vector.tensor_add(xs[:, osl], xs[:, osl], bank[:, 0:TH]), 1)
                vector.wait_ge(vec.h, vec.n)
                vec.inc(vector.tensor_copy(rt_cols[:, 0:1], rt_cols[:, 0:1]), 1)

            bw = dict(mmd=mmd.n, vec=vec.n, act=act.n, bcl=bcl.n,
                      w1l0=w1l[0].n, w1l1=w1l[1].n)

            @block.gpsimd
            def _(gpsimd, th=th, bw=bw):
                for hg in range(8):
                    for e in range(12):
                        u = hg * 12 + e
                        if u >= 2:
                            gpsimd.wait_ge(mmd.h, bw["mmd"] + u)
                        w1l[u % 2].inc(gpsimd.dma_start(
                            out=w1b[:, (u % 2) * FC * NT:(u % 2 + 1) * FC * NT]
                                .rearrange("p (c o) -> p c o", c=FC),
                            in_=w1_i[hg * 12 + e].rearrange("(c p) o -> p c o", p=128)), 16)

            @block.sync
            def _(sync, th=th, bw=bw):
                for hg in range(8):
                    for e in range(12):
                        u = hg * 12 + e
                        if u >= 1:
                            sync.wait_ge(vec.h, bw["vec"] + 9 * u)
                        else:
                            sync.wait_ge(st.h, st.n)
                        bcl.inc(sync.dma_start(
                            out=bc1[:, :],
                            in_=wT_d[e:e + 1, th * TH:(th + 1) * TH]
                                .broadcast_to((128, TH))), 16)

            @block.vector
            def _(vector, th=th, bw=bw):
                for hg in range(8):
                    for e in range(12):
                        u = hg * 12 + e
                        vector.wait_ge(bcl.h, bw["bcl"] + 16 * (u + 1))
                        if u >= 1:
                            vector.wait_ge(mmd.h, bw["mmd"] + u)
                        for fcb in range(FC):
                            vec.inc(vector.tensor_mul(y3s[:, fcb * TH:(fcb + 1) * TH],
                                                      y3h[:, fcb * TH:(fcb + 1) * TH],
                                                      bc1[:, :]), 1)
                        vector.wait_ge(vec.h, vec.n)
                        vec.inc(vector.tensor_copy(rt_cols[:, 0:1], rt_cols[:, 0:1]), 1)

            @block.tensor
            def _(tensor, th=th, bw=bw):
                tensor.wait_ge(vec.h, bw["vec"])
                tensor.wait_ge(act.h, bw["act"])
                for hg in range(8):
                    for e in range(12):
                        u = hg * 12 + e
                        if hg >= 1 and e == 0:
                            tensor.wait_ge(act.h, bw["act"] + 4 * hg)
                        tensor.wait_ge(w1l[u % 2].h,
                                       (bw["w1l0"] if u % 2 == 0 else bw["w1l1"])
                                       + 16 * (u // 2 + 1))
                        tensor.wait_ge(vec.h, bw["vec"] + 9 * (u + 1))
                        wsl = (u % 2) * FC * NT
                        for hc in range(4):
                            ob = psA if hc < 2 else psB
                            osl = (hc % 2) * 1024
                            for fcb in range(FC):
                                for tt in range(2):
                                    mm = tensor.matmul(
                                        ob[:, osl + tt * NT:osl + (tt + 1) * NT],
                                        w1b[:, wsl + fcb * NT + hc * 128:wsl + fcb * NT + (hc + 1) * 128],
                                        y3s[:, fcb * TH + tt * NT:fcb * TH + (tt + 1) * NT],
                                        start=(e == 0 and fcb == 0),
                                        stop=(e == 11 and fcb == FC - 1))
                        mmd.inc(mm, 1)

            @block.scalar
            def _(scalar, th=th, bw=bw):
                for hg in range(8):
                    scalar.wait_ge(mmd.h, bw["mmd"] + 12 * (hg + 1))
                    for hc in range(4):
                        ob = psA if hc < 2 else psB
                        osl = (hc % 2) * 1024
                        act.inc(scalar.activation(
                            h_g[:, (hg * 4 + hc) * 1024:(hg * 4 + hc + 1) * 1024],
                            ob[:, osl:osl + 1024], AF.Gelu), 1)

            b5 = dict(mmd=mmd.n, vec=vec.n, act=act.n, bcl=bcl.n,
                      w2l0=w2l[0].n, w2l1=w2l[1].n)

            @block.gpsimd
            def _(gpsimd, th=th, b5=b5):
                for hh in range(2):
                    for e in range(12):
                        for hidc in range(32):
                            u = (hh * 12 + e) * 32 + hidc
                            if u >= 2:
                                gpsimd.wait_ge(mmd.h, b5["mmd"] + u)
                            w2l[u % 2].inc(gpsimd.dma_start(
                                out=w2b[:, (u % 2) * D:(u % 2) * D + 512],
                                in_=w2_i[e * 32 + hidc, :, hh * 512:(hh + 1) * 512]), 16)

            @block.sync
            def _(sync, th=th, b5=b5):
                for hh in range(2):
                    for e in range(12):
                        u = hh * 12 + e
                        if u >= 1:
                            sync.wait_ge(vec.h, b5["vec"] + 9 * u)
                        else:
                            sync.wait_ge(vec.h, b5["vec"])
                        bcl.inc(sync.dma_start(
                            out=bc1[:, :],
                            in_=wT_d[16 + e:17 + e, th * TH:(th + 1) * TH]
                                .broadcast_to((128, TH))), 16)

            @block.tensor
            def _(tensor, th=th, b5=b5):
                tensor.wait_ge(act.h, b5["act"])
                tensor.wait_ge(vec.h, b5["vec"])
                for hh in range(2):
                    for e in range(12):
                        ue = hh * 12 + e
                        if ue >= 1:
                            tensor.wait_ge(vec.h, b5["vec"] + 9 * ue)
                        for hidc in range(32):
                            u = ue * 32 + hidc
                            tensor.wait_ge(w2l[u % 2].h,
                                           (b5["w2l0"] if u % 2 == 0 else b5["w2l1"])
                                           + 16 * (u // 2 + 1))
                            for Hc in range(4):
                                ob = psA if Hc < 2 else psB
                                osl = (Hc % 2) * 1024
                                for tt in range(2):
                                    mm = tensor.matmul(
                                        ob[:, osl + tt * NT:osl + (tt + 1) * NT],
                                        w2b[:, (u % 2) * D + Hc * 128:(u % 2) * D + (Hc + 1) * 128],
                                        h_g[:, hidc * 1024 + tt * NT:hidc * 1024 + (tt + 1) * NT],
                                        start=(hidc == 0), stop=(hidc == 31))
                            mmd.inc(mm, 1)

            @block.vector
            def _(vector, th=th, ho=ho, b5=b5):
                for hh in range(2):
                    for e in range(12):
                        ue = hh * 12 + e
                        vector.wait_ge(mmd.h, b5["mmd"] + 32 * (ue + 1))
                        vector.wait_ge(bcl.h, b5["bcl"] + 16 * (ue + 1))
                        for Hc in range(4):
                            ob = psA if Hc < 2 else psB
                            osl = (Hc % 2) * 1024
                            xsl = slice((hh * 4 + Hc) * TC + ho, (hh * 4 + Hc) * TC + ho + TH)
                            vec.inc(vector.tensor_mul(sq2[:, 0:TH], ob[:, osl:osl + 1024],
                                                      bc1[:, :]), 1)
                            vector.wait_ge(vec.h, vec.n)
                            vec.inc(vector.tensor_add(xs[:, xsl], xs[:, xsl],
                                                      sq2[:, 0:TH]), 1)
                            vector.wait_ge(vec.h, vec.n)
                        vec.inc(vector.tensor_copy(rt_cols[:, 0:1], rt_cols[:, 0:1]), 1)

        @block.sync
        def _(sync):
            sync.wait_ge(vec.h, vec.n)
            for fcb in range(FC):
                st.inc(sync.dma_start(out=out_o[fcb * 128:(fcb + 1) * 128, :],
                                      in_=xs[:, fcb * TC:(fcb + 1) * TC]), 16)
            sync.wait_ge(st.h, st.n)
    return ctx




def emit_drain(ctx):
    nc = ctx.nc
    sems = [s for s in ctx.sems.values() if s.n > 0]
    with nc.Block() as blk:
        @blk.sync
        def _(sync):
            for s in sems:
                sync.wait_ge(s.h, s.n)


def emit_ag(ctx, tag, src, dst):
    nc = ctx.nc
    ctx.phase(tag)
    cc = ctx.sem("cc")
    with nc.Block() as blk:
        @blk.gpsimd
        def _(g):
            cc.inc(g.collective_compute(
                "AllGather", ALU.bypass,
                replica_groups=[list(range(NCORES))],
                ins=[src.opt()], outs=[dst.opt()]), 1)
            g.wait_ge(cc.h, cc.n)


def emit_a2a(ctx, tag, src, rname, dst):
    nc = ctx.nc
    ctx.phase(tag)
    rr = ctx.dt(rname, [NPANEL * 128, TC], BF16, "int")
    cp = ctx.sem("cp")
    cc = ctx.sem("cc")
    with nc.Block() as blk:
        @blk.sync
        def _(sync):
            for p in range(NPANEL):
                cp.inc(sync.dma_start(out=rr[p * 128:(p + 1) * 128, :],
                                      in_=src[:, p * TC:(p + 1) * TC]), 16)
        @blk.gpsimd
        def _(g):
            g.wait_ge(cp.h, cp.n)
            cc.inc(g.collective_compute(
                "AllToAll", ALU.bypass,
                replica_groups=[list(range(NCORES))],
                ins=[rr.opt()], outs=[dst.opt()]), 1)
            g.wait_ge(cc.h, cc.n)


def build_fused():
    ctx = Ctx("fused", fused=True)
    nc = ctx.nc
    with nc.cleanup_on_exit():
        build_pre0(ctx)
        emit_drain(ctx)
    with nc.cleanup_on_exit():
        y0f = ctx.dt("y0f", [NPANEL * D, TC], BF16, "int", shared=True)
        emit_ag(ctx, "cy0", ctx.dram["y0"], y0f)
    with nc.cleanup_on_exit():
        build_mid0(ctx)
        emit_drain(ctx)
    with nc.cleanup_on_exit():
        o_l = ctx.dt("o", [D, TC], BF16, "int")
        emit_a2a(ctx, "co", ctx.dram["oh"], "o_r", o_l)
    with nc.cleanup_on_exit():
        build_l0post(ctx)
        emit_drain(ctx)
    with nc.cleanup_on_exit():
        ssf = ctx.dt("ssf", [NPANEL * D, TC], BF16, "int", shared=True)
        emit_ag(ctx, "css", ctx.dram["ss"], ssf)
        srf = ctx.dt("srf", [NPANEL * D, TC], BF16, "int", shared=True)
        emit_ag(ctx, "csr", ctx.dram["sr"], srf)
    with nc.cleanup_on_exit():
        build_mid1(ctx)
        emit_drain(ctx)
    with nc.cleanup_on_exit():
        vals = ctx.dt("vals", [D, TC], BF16, "int")
        emit_a2a(ctx, "cv", ctx.dram["o1h"], "o1_r", vals)
    with nc.cleanup_on_exit():
        build_post1(ctx)
        emit_drain(ctx)
    return ctx



def fused_in_maps(inputs):
    xf = np.asarray(inputs["L0_attn_hidden_states"], np.float32).reshape(B * T, D)
    gw = np.asarray(inputs["L0_attn_g_norm_weight"], np.float32)
    Wo = np.asarray(inputs["L0_attn_Wo"], np.float32)
    wot_blocks = np.stack([Wo.T[:, oc * 128:(oc + 1) * 128] for oc in range(FC)])
    Wc = np.asarray(inputs["L0_ffn_classifier_weight"], np.float32)
    bias = np.asarray(inputs["L0_ffn_e_score_correction_bias"], np.float32)
    sg = np.asarray(inputs["L1_attn_store_norm_g"], np.float32)
    rg = np.asarray(inputs["L1_attn_retrieve_norm_g"], np.float32)
    gup = np.asarray(inputs["L0_ffn_gate_up_proj"], np.float32)[:32]
    dnp = np.asarray(inputs["L0_ffn_down_proj"], np.float32)
    l0wgu = np.ascontiguousarray(np.transpose(gup, (0, 2, 1)))
    l0wdn = np.ascontiguousarray(np.transpose(dnp, (0, 2, 1)))
    wkv = np.asarray(inputs["L1_attn_w_kv"], np.float32)
    wq1 = np.asarray(inputs["L1_attn_w_q"], np.float32)
    ws = np.asarray(inputs["L1_attn_w_step"], np.float32)
    wm = np.asarray(inputs["L1_attn_w_mom"], np.float32)
    wd = np.asarray(inputs["L1_attn_w_decay"], np.float32)
    wg1 = np.asarray(inputs["L1_attn_w_gate"], np.float32)
    gam = np.asarray(inputs["L1_attn_mh_gamma"], np.float32)
    Wcm = np.asarray(inputs["L1_attn_w_combine"], np.float32)
    wcmb_blocks = np.stack([Wcm[:, oc * 128:(oc + 1) * 128] for oc in range(FC)])
    Pw = np.asarray(inputs["L1_ffn_P_w"], np.float32)
    U1 = np.asarray(inputs["L1_ffn_U1"], np.float32)
    U2 = np.asarray(inputs["L1_ffn_U2"], np.float32)
    U3 = np.asarray(inputs["L1_ffn_U3"], np.float32)
    u123 = np.zeros((64, 48), np.float32)
    u123[:, 0:12] = U1.T; u123[:, 16:28] = U2.T; u123[:, 32:44] = U3.T
    W1 = np.asarray(inputs["L1_ffn_W1"], np.float32)
    W2 = np.asarray(inputs["L1_ffn_W2"], np.float32)
    w1s = np.empty((96, D, NT), np.float32)
    for hg in range(8):
        for e in range(12):
            w1s[hg * 12 + e] = W1[e].T[:, hg * 512:(hg + 1) * 512]
    w2s = np.empty((384, 128, D), np.float32)
    for e in range(12):
        for hidc in range(32):
            w2s[e * 32 + hidc] = W2[e].T[hidc * 128:(hidc + 1) * 128, :]
    common = {
        "wot": np.ascontiguousarray(wot_blocks.reshape(FC * D, 128)),
        "wg1t": np.ascontiguousarray(np.asarray(inputs["L0_attn_Wg1"], np.float32).T),
        "wg2t": np.ascontiguousarray(np.asarray(inputs["L0_attn_Wg2"], np.float32).T),
        "gnorm": np.ascontiguousarray(gw.reshape(FC, 128).T),
        "wct": np.ascontiguousarray(Wc.T),
        "biasbc": np.ascontiguousarray(np.broadcast_to(bias[None, :], (128, 40))),
        "sgain": np.ascontiguousarray(sg.reshape(FC, 128).T),
        "rgain": np.ascontiguousarray(rg.reshape(FC, 128).T),
        "l0wgu": l0wgu,
        "l0wdn": l0wdn,
        "zpad": np.zeros((128, 64), BFD),
        "wsmd0": None,
        "w0": np.asarray(inputs["L1_attn_mem_w0"], np.float32),
        "w1": np.asarray(inputs["L1_attn_mem_w1"], np.float32),
        "zer": np.zeros((TC, 80), BFD),
        "wcmb": np.ascontiguousarray(wcmb_blocks.reshape(FC * D, 128)),
        "pwt": np.ascontiguousarray(Pw.T),
        "u123": u123,
        "b2": np.asarray(inputs["L1_ffn_b2"], np.float32),
        "w1s": w1s, "w2s": w2s,
    }
    del common["wsmd0"]
    in_maps = []
    for h in range(NCORES):
        m = dict(common)
        m["x"] = np.ascontiguousarray(xf[h * TC:(h + 1) * TC].T)
        m["wq"] = np.ascontiguousarray(
            np.asarray(inputs["L0_attn_Wq"], np.float32)[h * 128:(h + 1) * 128, :].T)
        m["wk"] = np.ascontiguousarray(
            np.asarray(inputs["L0_attn_Wk"], np.float32)[h * 128:(h + 1) * 128, :].T)
        m["wv"] = np.ascontiguousarray(
            np.asarray(inputs["L0_attn_Wv"], np.float32)[h * 128:(h + 1) * 128, :].T)
        m["wk1"] = np.ascontiguousarray(wkv[:, h * 128:(h + 1) * 128])
        m["wv1"] = np.ascontiguousarray(wkv[:, 1024 + h * 128:1024 + (h + 1) * 128])
        m["wq1"] = np.ascontiguousarray(wq1[:, h * 128:(h + 1) * 128])
        m["wsmd"] = np.ascontiguousarray(
            np.stack([ws[:, h], wm[:, h], wd[:, h]], axis=1))
        m["wg"] = np.ascontiguousarray(wg1[:, h:h + 1])
        m["gam"] = np.ascontiguousarray(gam[h, 0, :].reshape(128, 1))
        in_maps.append(m)
    return in_maps


def kernel(**inputs):
    ctx = build_fused()
    in_maps = fused_in_maps(inputs)
    res = run_bass_kernel_spmd(ctx.nc, in_maps, core_ids=list(range(NCORES)))
    full = np.concatenate(
        [np.asarray(res.results[c]["out"], np.float32).T for c in range(NCORES)],
        axis=0)
    return full.reshape(B, T, D)


# version 03 (final)
# speedup vs baseline: 1.5409x; 1.5409x over previous
import os
os.environ.setdefault("TRN_ALLOW_ICI_ALL_TO_ALL", "1")
import numpy as np
import ml_dtypes
import contextlib
import concourse.bass as bass
import concourse.mybir as mybir
from concourse.bass_utils import run_bass_kernel_spmd

F32 = mybir.dt.float32
BF16 = mybir.dt.bfloat16
AF = mybir.ActivationFunctionType
ALU = mybir.AluOpType
AX = mybir.AxisListType
BFD = np.dtype(ml_dtypes.bfloat16)

NCORES = 8
B, T, D = 2, 8192, 1024
TC = B * T // NCORES
FC = D // 128
NT = 512
NPANEL = 8


class Sem:
    def __init__(self, h):
        self.h = h
        self.n = 0

    def inc(self, instr, k=1):
        instr.then_inc(self.h, k)
        self.n += k
        return self.n


MODULE_FINAL_OUT = "out"

class Ctx:
    def __init__(self, name, fused=False):
        self.nc = bass.Bass("TRN2", target_bir_lowering=False, debug=False)
        self.es = contextlib.ExitStack()
        self.block = None
        self.fused = fused
        self.tag = ""
        self.dram = {}
        self.sems = {}
        self.sbuf_base0 = self.nc.sbuf_base
        self.psum_base0 = self.nc.psum_base

    def phase(self, tag):
        self.tag = tag + "_"
        self.sems = {}
        self.nc.sbuf_base = self.sbuf_base0
        self.nc.psum_base = self.psum_base0

    def sem(self, name):
        if name not in self.sems:
            self.sems[name] = Sem(self.nc.alloc_semaphore(name=self.tag + name))
        return self.sems[name]

    def sb(self, name, shape, dtype):
        return self.nc.alloc_sbuf_tensor(self.tag + name, shape, dtype)

    def sba(self, name, shape, dtype, offset):
        return self.nc.alloc_sbuf_tensor_at(self.tag + name, shape, dtype,
                                            offset=offset)

    def ps(self, name, shape, dtype):
        return self.nc.alloc_psum_tensor(self.tag + name, shape, dtype)

    def dt(self, name, shape, dtype, io, shared=False):
        if name in self.dram:
            return self.dram[name]
        kind = {"in": "ExternalInput", "int": "Internal",
                "out": "ExternalOutput"}[io]
        kw = {"addr_space": "Shared"} if shared else {}
        ap = self.nc.dram_tensor(name, shape, dtype, kind=kind, **kw).ap()
        self.dram[name] = ap
        return ap



def emit_norm_fm(ctx, xs, yout, sq, rows, ones1, onesc, psA, psB,
                 mmd, vec, act, ntok, kind="ln", eps_ap=None, gcol=None,
                 in_stride=None, in_off=0):
    nc, block = ctx.nc, ctx.block
    ntt = ntok // NT
    ist = in_stride if in_stride is not None else ntok
    def xv_of(fcb):
        return xs[:, fcb * ist + in_off:fcb * ist + in_off + ntok]
    mean = rows[0:1, 0:ntok]
    work = rows[0:1, ntok:2 * ntok]
    base_act = act.n
    base_mmd = mmd.n

    @block.scalar
    def _(scalar):
        for fcb in range(FC):
            if fcb >= 1:
                scalar.wait_ge(mmd.h, base_mmd + fcb)
            act.inc(scalar.activation(sq[:, 0:ntok], xv_of(fcb), AF.Square), 1)

    @block.tensor
    def _(tensor):
        for fcb in range(FC):
            tensor.wait_ge(act.h, base_act + fcb + 1)
            for tt in range(ntt):
                sl = slice(tt * NT, (tt + 1) * NT)
                if kind == "ln":
                    tensor.matmul(psA[0:1, sl], onesc[:, :],
                                  xv_of(fcb)[:, tt * NT:(tt + 1) * NT],
                                  start=(fcb == 0), stop=(fcb == FC - 1))
                mm2 = tensor.matmul(psB[0:1, sl], onesc[:, :], sq[:, sl],
                                    start=(fcb == 0), stop=(fcb == FC - 1))
            mmd.inc(mm2, 1)

    @block.vector
    def _(vector):
        vector.wait_ge(mmd.h, mmd.n)
        if kind == "ln":
            vector.tensor_scalar_mul(work, psB[0:1, 0:ntok], 1.0 / D)
            vec.inc(vector.tensor_scalar_mul(mean, psA[0:1, 0:ntok], 1.0 / D), 1)
            vector.wait_ge(vec.h, vec.n)
        else:
            vec.inc(vector.tensor_scalar_mul(work, psB[0:1, 0:ntok], 1.0 / D), 1)

    if kind == "ln":
        @block.scalar
        def _(scalar):
            scalar.wait_ge(vec.h, vec.n)
            act.inc(scalar.activation(psA[0:1, 0:ntok], mean, AF.Square), 1)

        @block.vector
        def _(vector):
            vector.wait_ge(act.h, act.n)
            vec.inc(vector.tensor_sub(work, work, psA[0:1, 0:ntok]), 1)

    @block.scalar
    def _(scalar):
        scalar.wait_ge(vec.h, vec.n)
        act.inc(scalar.activation(work, work, AF.Ln, bias=eps_ap), 1)
        scalar.wait_ge(act.h, act.n)
        act.inc(scalar.activation(work, work, AF.Exp, scale=-0.5), 1)

    if kind == "ln":
        @block.vector
        def _(vector):
            vector.wait_ge(act.h, act.n)
            vec.inc(vector.scalar_tensor_tensor(mean, mean, -1.0, work,
                                                ALU.mult, ALU.mult), 1)

    @block.tensor
    def _(tensor):
        if kind == "ln":
            tensor.wait_ge(vec.h, vec.n)
        else:
            tensor.wait_ge(act.h, act.n)
        for tt in range(ntt):
            sl = slice(tt * NT, (tt + 1) * NT)
            mm = tensor.matmul(psA[:, sl], ones1[0:1, :], work[0:1, sl],
                               start=True, stop=True)
            if kind == "ln":
                mm = tensor.matmul(psB[:, sl], ones1[0:1, :], mean[0:1, sl],
                                   start=True, stop=True)
        mmd.inc(mm, 1)

    @block.vector
    def _(vector):
        vector.wait_ge(mmd.h, mmd.n)
        for fcb in range(FC):
            xv = xv_of(fcb)
            yv = yout[:, fcb * ntok:(fcb + 1) * ntok]
            buf = sq[:, 0:ntok]
            if fcb > 0:
                vector.wait_ge(vec.h, vec.n)
            if kind == "ln":
                vec.inc(vector.tensor_mul(buf, xv, psA[:, 0:ntok]), 1)
                vector.wait_ge(vec.h, vec.n)
                if gcol is None:
                    vec.inc(vector.tensor_add(yv, buf, psB[:, 0:ntok]), 1)
                else:
                    vec.inc(vector.tensor_add(buf, buf, psB[:, 0:ntok]), 1)
                    vector.wait_ge(vec.h, vec.n)
                    vec.inc(vector.tensor_scalar_mul(yv, buf, gcol[:, fcb:fcb + 1]), 1)
            else:
                if gcol is None:
                    vec.inc(vector.tensor_mul(yv, xv, psA[:, 0:ntok]), 1)
                else:
                    vec.inc(vector.tensor_mul(buf, xv, psA[:, 0:ntok]), 1)
                    vector.wait_ge(vec.h, vec.n)
                    vec.inc(vector.tensor_scalar_mul(yv, buf, gcol[:, fcb:fcb + 1]), 1)
        vector.wait_ge(vec.h, vec.n)
        vec.inc(vector.tensor_copy(rows[0:1, 0:1], onesc[0:1, 0:1]), 1)



def build_pre0(ctx):
    ctx.phase("p0")
    nc = ctx.nc
    x = ctx.dt("x", [D, TC], F32, "in")
    y0o = ctx.dt("y0", [D, TC], BF16, "int")

    xs = ctx.sb("xs", [128, FC * TC], F32)
    y0 = ctx.sb("y0s", [128, FC * TC], BF16)
    sq2 = ctx.sb("sq2", [128, TC], F32)
    rows = ctx.sb("rows", [1, 2 * TC], F32)
    ones1 = ctx.sb("ones1", [1, 128], F32)
    onesc = ctx.sb("onesc", [128, 1], F32)
    consts = ctx.sb("consts", [128, 4], F32)
    psA = ctx.ps("psA", [128, 2048], F32)
    psB = ctx.ps("psB", [128, 2048], F32)

    ld = ctx.sem("ld"); mmd = ctx.sem("mmd"); vec = ctx.sem("vec")
    act = ctx.sem("act"); st = ctx.sem("st"); ig = ctx.sem("ig")

    with nc.Block() as block:
        ctx.block = block

        @block.sync
        def _(sync):
            for fcb in range(FC):
                ld.inc(sync.dma_start(out=xs[:, fcb * TC:(fcb + 1) * TC],
                                      in_=x[fcb * 128:(fcb + 1) * 128, :]), 16)

        @block.gpsimd
        def _(gpsimd):
            gpsimd.memset(ones1[:, :], 1.0)
            gpsimd.memset(consts[:, 0:1], 1e-5)
            ig.inc(gpsimd.memset(onesc[:, :], 1.0), 1)

        @block.scalar
        def _(scalar):
            scalar.wait_ge(ld.h, ld.n)
            scalar.wait_ge(ig.h, ig.n)

        @block.tensor
        def _(tensor):
            tensor.wait_ge(ld.h, ld.n)
            tensor.wait_ge(ig.h, ig.n)

        emit_norm_fm(ctx, xs, y0, sq2, rows, ones1, onesc, psA, psB,
                     mmd, vec, act, TC, kind="ln", eps_ap=consts[0:1, 0:1])

        @block.sync
        def _(sync):
            sync.wait_ge(vec.h, vec.n)
            for fcb in range(FC):
                st.inc(sync.dma_start(out=y0o[fcb * 128:(fcb + 1) * 128, :],
                                      in_=y0[:, fcb * TC:(fcb + 1) * TC]), 16)
            sync.wait_ge(st.h, st.n)
    return ctx



def build_mid0(ctx):
    ctx.phase("m0")
    nc = ctx.nc
    NP = NPANEL
    y0f = ctx.dt("y0f", [NP * D, TC], BF16, "int")
    wq = ctx.dt("wq", [D, 128], F32, "in")
    wk = ctx.dt("wk", [D, 128], F32, "in")
    wv = ctx.dt("wv", [D, 128], F32, "in")
    oo = ctx.dt("oh", [128, NP * TC], BF16, "int")

    y0p = ctx.sb("y0p", [128, 2 * FC * TC], BF16)
    wqt = ctx.sb("wqt", [128, FC * 128], BF16)
    wkt = ctx.sb("wkt", [128, FC * 128], BF16)
    wvt = ctx.sb("wvt", [128, FC * 128], BF16)
    qsil = ctx.sb("qsil", [128, TC], F32)
    Wfull = ctx.sb("Wfull", [128, TC], F32)
    cum = ctx.sb("cum", [128, TC], F32)
    scr = ctx.sb("scr", [128, TC], F32)
    qn = ctx.sb("qn", [128, TC], BF16)
    W_bf = ctx.sb("W_bf", [128, TC], BF16)
    v_bf = ctx.sb("v_bf", [128, TC], BF16)
    W_tok = ctx.sb("W_tok", [128, TC], BF16)
    v_tok = ctx.sb("v_tok", [128, TC], BF16)
    o_h = ctx.sb("o_h", [128, NP * TC], BF16)
    Am = ctx.sb("Am", [128, 128], BF16)
    tri = ctx.sb("tri", [128, 128], F32)
    onesbig = ctx.sb("onesbig", [128, TC], F32)
    Tst_f = ctx.sb("Tst_f", [128, 128], F32)
    Tst_bf = ctx.sb("Tst_bf", [128, 128], BF16)
    carry = ctx.sb("carry", [128, 1], F32)
    psA = ctx.ps("psA", [128, 2048], F32)
    psB = ctx.ps("psB", [128, 2048], F32)

    ld = [ctx.sem("ldA"), ctx.sem("ldB")]
    wld = ctx.sem("wld")
    ini = ctx.sem("ini")
    mmp = ctx.sem("mmp")
    evq = ctx.sem("evq")
    evv = ctx.sem("evv")
    rec = ctx.sem("rec")
    vsc = ctx.sem("vsc")
    dt = ctx.sem("dt")
    sA = ctx.sem("sA")
    sAm = ctx.sem("sAm")
    sOc = ctx.sem("sOc")
    sOe = ctx.sem("sOe")
    sTd = ctx.sem("sTd")
    sSt = ctx.sem("sSt")
    st = ctx.sem("st")

    proj_banks = [psA[:, 1024:1536], psA[:, 1536:2048],
                  psB[:, 1024:1536], psB[:, 1536:2048]]

    with nc.Block() as block:
        ctx.block = block

        @block.gpsimd
        def _(gpsimd):
            wld.inc(gpsimd.dma_start(out=wqt[:, :].rearrange("p (c i) -> p c i", c=FC),
                             in_=wq.rearrange("(c p) i -> p c i", p=128)), 16)
            wld.inc(gpsimd.dma_start(out=wkt[:, :].rearrange("p (c i) -> p c i", c=FC),
                             in_=wk.rearrange("(c p) i -> p c i", p=128)), 16)
            wld.inc(gpsimd.dma_start(out=wvt[:, :].rearrange("p (c i) -> p c i", c=FC),
                                     in_=wv.rearrange("(c p) i -> p c i", p=128)), 16)
            gpsimd.memset(onesbig[:, :], 1.0)
            ini.inc(gpsimd.iota(scr[:, 0:128], pattern=[[1, 128]], base=0,
                        channel_multiplier=-1, allow_small_or_imprecise_dtypes=True), 1)
            gpsimd.wait_ge(ini.h, ini.n)
            ini.inc(gpsimd.tensor_single_scalar(tri[:, :], scr[:, 0:128], 0.0, ALU.is_ge), 1)
            gpsimd.memset(Tst_f[:, :], 0.0)
            sSt.inc(gpsimd.memset(Tst_bf[:, :], 0.0), 1)
            ini.inc(gpsimd.memset(carry[:, :], 0.0), 1)

        state_ready = 1

        for p in range(NP):
            pb = (p % 2) * FC * TC

            @block.sync
            def _(sync, p=p, pb=pb):
                if p >= 2:
                    sync.wait_ge(mmp.h, 12 * (p - 1) + 1)
                for fcb in range(FC):
                    ld[p % 2].inc(sync.dma_start(
                        out=y0p[:, pb + fcb * TC:pb + (fcb + 1) * TC],
                        in_=y0f[p * D + fcb * 128:p * D + (fcb + 1) * 128, :]), 16)

            @block.tensor
            def _(tensor, p=p, pb=pb, state_ready=state_ready):
                tensor.wait_ge(ld[p % 2].h, 128 * (p // 2 + 1))
                if p == 0:
                    tensor.wait_ge(wld.h, 48)
                    tensor.wait_ge(ini.h, 1)
                if p >= 1:
                    tensor.wait_ge(sOe.h, 16 * p)
                    tensor.wait_ge(sSt.h, state_ready)
                for pi, wt in enumerate((wqt, wkt, wvt)):
                    for tt in range(4):
                        if pi == 1:
                            tensor.wait_ge(evq.h, 8 * p + tt + 1)
                        elif pi == 2:
                            tensor.wait_ge(evq.h, 8 * p + 4 + tt + 1)
                        bank = proj_banks[tt]
                        for fcb in range(FC):
                            mm = tensor.matmul(
                                bank[:, 0:NT],
                                wt[:, fcb * 128:(fcb + 1) * 128],
                                y0p[:, pb + fcb * TC + tt * NT:pb + fcb * TC + (tt + 1) * NT],
                                start=(fcb == 0), stop=(fcb == FC - 1))
                        mmp.inc(mm, 1)

            @block.scalar
            def _(scalar, p=p):
                for tt in range(4):
                    scalar.wait_ge(mmp.h, p * 12 + tt + 1)
                    evq.inc(scalar.activation(qsil[:, tt * NT:(tt + 1) * NT],
                                              proj_banks[tt][:, 0:NT], AF.Silu), 1)
                for tt in range(4):
                    scalar.wait_ge(mmp.h, p * 12 + 4 + tt + 1)
                    evq.inc(scalar.activation(Wfull[:, tt * NT:(tt + 1) * NT],
                                              proj_banks[tt][:, 0:NT], AF.Exp), 1)

            @block.vector
            def _(vector, p=p):
                vector.wait_ge(dt.h, p * 32 * 16)
                for tt in range(4):
                    vector.wait_ge(mmp.h, p * 12 + 8 + tt + 1)
                    evv.inc(vector.tensor_copy(v_bf[:, tt * NT:(tt + 1) * NT],
                                               proj_banks[tt][:, 0:NT]), 1)

            @block.gpsimd
            def _(gpsimd, p=p):
                if p == 4:
                    gpsimd.wait_ge(vsc.h, 4 * p)
                    ini.inc(gpsimd.memset(carry[:, :], 0.0), 1)

            @block.vector
            def _(vector, p=p):
                vector.wait_ge(evq.h, p * 8 + 8)
                if p == 0 or p == 4:
                    vector.wait_ge(ini.h, ini.n)
                if p >= 1:
                    vector.wait_ge(sTd.h, 16 * p)
                sc = vector.tensor_tensor_scan(cum[:, :], onesbig[:, :], Wfull[:, :],
                                               carry[:, 0:1], ALU.mult, ALU.add)
                vsc.inc(sc, 1)
                vector.wait_ge(vsc.h, vsc.n)
                vsc.inc(vector.tensor_copy(carry[:, 0:1], cum[:, TC - 1:TC]), 1)
                vsc.inc(vector.tensor_copy(W_bf[:, :], Wfull[:, :]), 1)

            @block.scalar
            def _(scalar, p=p):
                scalar.wait_ge(vsc.h, 4 * p + 1)
                rec.inc(scalar.activation(scr[:, :], cum[:, :], AF.Ln), 1)
                scalar.wait_ge(rec.h, rec.n)
                rec.inc(scalar.activation(scr[:, :], scr[:, :], AF.Exp, scale=-1.0), 1)

            @block.vector
            def _(vector, p=p):
                vector.wait_ge(rec.h, 2 * (p + 1))
                vsc.inc(vector.scalar_tensor_tensor(qn[:, :], qsil[:, :],
                                                    0.08838834764831845,
                                                    scr[:, :], ALU.mult, ALU.mult), 1)

            @block.scalar
            def _(scalar, p=p):
                scalar.wait_ge(vsc.h, 4 * p + 3)
                scalar.wait_ge(evv.h, 4 * p + 4)
                for m in range(16):
                    dt.inc(scalar.dma_start(out=W_tok[:, m * 128:(m + 1) * 128],
                                            in_=W_bf[:, m * 128:(m + 1) * 128],
                                            transpose=True), 16)
                    dt.inc(scalar.dma_start(out=v_tok[:, m * 128:(m + 1) * 128],
                                            in_=v_bf[:, m * 128:(m + 1) * 128],
                                            transpose=True), 16)

            for m in range(16):
                g = p * 16 + m
                msl = slice(m * 128, (m + 1) * 128)
                ocol = slice(g * 128, g * 128 + 128)
                ob = psB[:, 0:128] if g % 2 == 0 else psB[:, 512:640]

                @block.tensor
                def _(tensor, p=p, m=m, g=g, msl=msl, ob=ob, ocol=ocol,
                      state_ready=state_ready):
                    if m == 0:
                        tensor.wait_ge(vsc.h, 4 * p + 4)
                        tensor.wait_ge(dt.h, 512 * (p + 1))
                    if g >= 1:
                        tensor.wait_ge(sAm.h, g)
                    sA.inc(tensor.matmul(psA[:, 0:128], W_bf[:, msl], qn[:, msl],
                                         start=True, stop=True), 1)
                    tensor.wait_ge(sAm.h, g + 1)
                    tensor.wait_ge(sSt.h, state_ready)
                    if g >= 2:
                        tensor.wait_ge(sOe.h, g - 1)
                    tensor.matmul(ob, v_tok[:, msl], Am[:, :], start=True, stop=False)
                    sOc.inc(tensor.matmul(ob, Tst_bf[:, :], qn[:, msl],
                                          start=False, stop=True), 1)
                    sTd.inc(tensor.matmul(psA[:, 512:640], W_tok[:, msl], v_tok[:, msl],
                                          start=True, stop=True), 1)

                @block.vector
                def _(vector, g=g):
                    vector.wait_ge(sA.h, g + 1)
                    sAm.inc(vector.tensor_mul(Am[:, :], psA[:, 0:128], tri[:, :]), 1)

                @block.scalar
                def _(scalar, g=g, ocol=ocol, ob=ob):
                    scalar.wait_ge(sOc.h, g + 1)
                    sOe.inc(scalar.copy(o_h[:, ocol], ob), 1)

                @block.vector
                def _(vector, g=g):
                    vector.wait_ge(sTd.h, g + 1)
                    sSt.inc(vector.tensor_add(Tst_f[:, :], Tst_f[:, :], psA[:, 512:640]), 1)
                    vector.wait_ge(sSt.h, sSt.n)
                    sSt.inc(vector.tensor_copy(Tst_bf[:, :], Tst_f[:, :]), 1)
                state_ready += 2

            if p == 3:
                @block.gpsimd
                def _(gpsimd, state_ready=state_ready):
                    gpsimd.wait_ge(sSt.h, state_ready)
                    gpsimd.memset(Tst_f[:, :], 0.0)
                    sSt.inc(gpsimd.memset(Tst_bf[:, :], 0.0), 1)
                state_ready += 1

        @block.sync
        def _(sync):
            sync.wait_ge(sOe.h, 128)
            st.inc(sync.dma_start(out=oo, in_=o_h[:, :]), 16)
            sync.wait_ge(st.h, st.n)
    return ctx



def build_l0post(ctx, n_experts=32):
    ctx.phase("lp")
    nc = ctx.nc
    x_i = ctx.dt("x", [D, TC], F32, "in")
    o_i = ctx.dt("o", [D, TC], BF16, "int")
    y0_i = ctx.dt("y0", [D, TC], BF16, "int")
    wot_i = ctx.dt("wot", [FC * D, 128], F32, "in")
    wg1t_i = ctx.dt("wg1t", [D, 128], F32, "in")
    wg2t_i = ctx.dt("wg2t", [128, D], F32, "in")
    gnorm_i = ctx.dt("gnorm", [128, FC], F32, "in")
    wct_i = ctx.dt("wct", [D, 40], F32, "in")
    biasbc_i = ctx.dt("biasbc", [128, 40], F32, "in")
    sgain_i = ctx.dt("sgain", [128, FC], F32, "in")
    rgain_i = ctx.dt("rgain", [128, FC], F32, "in")
    l0wgu_i = ctx.dt("l0wgu", [32, D, D], F32, "in")
    l0wdn_i = ctx.dt("l0wdn", [32, 512, D], F32, "in")
    x2_o = ctx.dt("x2", [D, TC], F32, "int")
    ss_o = ctx.dt("ss", [D, TC], BF16, "int")
    sr_o = ctx.dt("sr", [D, TC], BF16, "int")
    wT_d = ctx.dt("wT_d", [128, TC], BF16, "int")
    w_db = ctx.dt("w_db", [TC, 128], BF16, "int")

    xs = ctx.sb("xs", [128, FC * TC], F32)
    bufB = ctx.sb("bufB", [128, FC * TC], BF16)
    wgu = ctx.sb("wgu", [128, 2 * 8 * 1024], BF16)
    wdn = ctx.sb("wdn", [128, 4 * 1024], BF16)
    sq2 = ctx.sb("sq2", [128, TC], F32)
    rows = ctx.sb("rows", [1, 2 * TC], F32)
    hI = ctx.sb("hI", [128, 4 * TC], BF16)
    bc_sb = ctx.sb("bc_sb", [128, TC], BF16)
    y0t = ctx.sb("y0t", [128, 2 * NT], BF16)
    g1 = ctx.sb("g1s", [128, TC], BF16)
    wot = ctx.sb("wots", [128, 2 * 1024], BF16)
    sgt = ctx.sb("sgt", [128, TC], BF16)
    wg1t = ctx.sb("wg1ts", [128, FC * 128], BF16)
    wg2t = ctx.sb("wg2ts", [128, 1024], BF16)
    wct = ctx.sb("wcts", [128, FC * 40], BF16)
    biasbc = ctx.sb("biasbcs", [128, 40], F32)
    gnorm = ctx.sb("gnorms", [128, FC], F32)
    sgain = ctx.sb("sgains", [128, FC], F32)
    rgain = ctx.sb("rgains", [128, FC], F32)
    w48b = ctx.sb("w48b", [128, 96], BF16)
    wTs = ctx.sb("wTs", [128, TC], BF16)
    rt_et = ctx.sb("rt_et", [128, 40], F32)
    rt_sc = ctx.sb("rt_sc", [128, 40], F32)
    rt_sb = ctx.sb("rt_sb", [128, 40], F32)
    rt_kp = ctx.sb("rt_kp", [128, 40], F32)
    rt_t8 = ctx.sb("rt_t8", [128, 8], F32)
    rt_cols = ctx.sb("rt_cols", [128, 2], F32)
    ones1 = ctx.sb("ones1", [1, 128], F32)
    onesc = ctx.sb("onesc", [128, 1], F32)
    consts = ctx.sb("consts", [128, 4], F32)
    psA = ctx.ps("psA", [128, 2048], F32)
    psB = ctx.ps("psB", [128, 2048], F32)
    stg = ctx.sba("stg", [128, 2 * TC], BF16, nc.lookup_mloc(y0t).addr)

    ld = ctx.sem("ld")
    ldg = ctx.sem("ldg")
    yld = ctx.sem("yld")
    wl = [ctx.sem("wlA"), ctx.sem("wlB")]
    wld2 = ctx.sem("wld2")
    bcl = ctx.sem("bcl")
    wol = [ctx.sem("wolA"), ctx.sem("wolB")]
    mmd = ctx.sem("mmd")
    vec = ctx.sem("vec")
    act = ctx.sem("act")
    gp = ctx.sem("gp")
    rtm = ctx.sem("rtm")
    rtr = ctx.sem("rtr")
    st = ctx.sem("st")
    stw = [ctx.sem("stwA"), ctx.sem("stwB")]

    with nc.Block() as block:
        ctx.block = block

        @block.sync
        def _(sync):
            for fcb in range(FC):
                ld.inc(sync.dma_start(out=xs[:, fcb * TC:(fcb + 1) * TC],
                                      in_=x_i[fcb * 128:(fcb + 1) * 128, :]), 16)
            for fcb in range(FC):
                ld.inc(sync.dma_start(out=bufB[:, fcb * TC:(fcb + 1) * TC],
                                      in_=o_i[fcb * 128:(fcb + 1) * 128, :]), 16)
            ld.inc(sync.dma_start(out=gnorm[:, :], in_=gnorm_i), 16)
            ld.inc(sync.dma_start(out=sgain[:, :], in_=sgain_i), 16)
            ld.inc(sync.dma_start(out=rgain[:, :], in_=rgain_i), 16)
            ld.inc(sync.dma_start(out=biasbc[:, :], in_=biasbc_i), 16)

        @block.gpsimd
        def _(gpsimd):
            ldg.inc(gpsimd.dma_start(out=wct[:, :].rearrange("p (c o) -> p c o", c=FC),
                                    in_=wct_i.rearrange("(c p) o -> p c o", p=128)), 16)
            ldg.inc(gpsimd.dma_start(out=wg1t[:, :].rearrange("p (c o) -> p c o", c=FC),
                                    in_=wg1t_i.rearrange("(c p) o -> p c o", p=128)), 16)
            ldg.inc(gpsimd.dma_start(out=wg2t[:, :], in_=wg2t_i), 16)
            gpsimd.memset(ones1[:, :], 1.0)
            gpsimd.memset(onesc[:, :], 1.0)
            gpsimd.memset(consts[:, 0:1], 1e-5)
            gpsimd.memset(consts[:, 1:2], 1e-6)
            gpsimd.memset(w48b[:, 41:48], 0.0)
            gpsimd.memset(w48b[:, 89:96], 0.0)
            gp.inc(gpsimd.memset(wTs[:, :], 0.0), 1)

        @block.scalar
        def _(scalar):
            scalar.wait_ge(ld.h, ld.n)
            scalar.wait_ge(ldg.h, ldg.n)
            scalar.wait_ge(gp.h, 1)

        @block.tensor
        def _(tensor):
            tensor.wait_ge(ld.h, ld.n)
            tensor.wait_ge(ldg.h, ldg.n)
            tensor.wait_ge(gp.h, 1)

        emit_norm_fm(ctx, bufB, bufB, sq2, rows, ones1, onesc, psA, psB,
                     mmd, vec, act, TC, kind="rms", eps_ap=consts[0:1, 0:1],
                     gcol=gnorm)

        b_mmd = mmd.n

        @block.sync
        def _(sync):
            for fcb in range(FC):
                for tt in range(4):
                    idx = fcb * 4 + tt
                    if idx >= 1:
                        sync.wait_ge(mmd.h, b_mmd + idx)
                    yld.inc(sync.dma_start(
                        out=y0t[:, (idx % 2) * NT:(idx % 2 + 1) * NT],
                        in_=y0_i[fcb * 128:(fcb + 1) * 128, tt * NT:(tt + 1) * NT]), 16)

        b_vec_g1 = vec.n

        @block.tensor
        def _(tensor):
            tensor.wait_ge(vec.h, b_vec_g1)
            for fcb in range(FC):
                for tt in range(4):
                    idx = fcb * 4 + tt
                    tensor.wait_ge(yld.h, 16 * (idx + 1))
                    mm = tensor.matmul(psA[:, tt * NT:(tt + 1) * NT],
                                       wg1t[:, fcb * 128:(fcb + 1) * 128],
                                       y0t[:, (idx % 2) * NT:(idx % 2 + 1) * NT],
                                       start=(fcb == 0), stop=(fcb == FC - 1))
                    mmd.inc(mm, 1)

        @block.vector
        def _(vector):
            vector.wait_ge(mmd.h, mmd.n)
            vec.inc(vector.tensor_copy(g1[:, :], psA[:, 0:TC]), 1)

        b2_mmd, b2_act, b2_vec = mmd.n, act.n, vec.n

        @block.tensor
        def _(tensor):
            tensor.wait_ge(vec.h, b2_vec)
            for oc in range(FC):
                if oc >= 2:
                    tensor.wait_ge(act.h, b2_act + oc - 1)
                bank = psB if oc % 2 == 0 else psA
                for tt in range(4):
                    mm = tensor.matmul(bank[:, tt * NT:(tt + 1) * NT],
                                       wg2t[:, oc * 128:(oc + 1) * 128],
                                       g1[:, tt * NT:(tt + 1) * NT],
                                       start=True, stop=True)
                mmd.inc(mm, 1)

        @block.scalar
        def _(scalar):
            for oc in range(FC):
                scalar.wait_ge(mmd.h, b2_mmd + oc + 1)
                if oc >= 1:
                    scalar.wait_ge(vec.h, b2_vec + oc)
                bank = psB if oc % 2 == 0 else psA
                act.inc(scalar.activation(sgt[:, :], bank[:, 0:TC], AF.Silu), 1)

        @block.vector
        def _(vector):
            for oc in range(FC):
                vector.wait_ge(act.h, b2_act + oc + 1)
                osl = slice(oc * TC, (oc + 1) * TC)
                vec.inc(vector.tensor_mul(bufB[:, osl], bufB[:, osl], sgt[:, :]), 1)
            vector.wait_ge(vec.h, vec.n)
            vec.inc(vector.tensor_copy(rows[0:1, 0:1], onesc[0:1, 0:1]), 1)

        b3_mmd, b3_vec = mmd.n, vec.n

        @block.gpsimd
        def _(gpsimd):
            for oc in range(FC):
                if oc >= 2:
                    gpsimd.wait_ge(mmd.h, b3_mmd + oc)
                wol[oc % 2].inc(gpsimd.dma_start(
                    out=wot[:, (oc % 2) * 1024:(oc % 2 + 1) * 1024]
                        .rearrange("p (c o) -> p c o", c=FC),
                    in_=wot_i[oc * D:(oc + 1) * D, :].rearrange("(c p) o -> p c o", p=128)), 16)

        @block.tensor
        def _(tensor):
            tensor.wait_ge(vec.h, b3_vec)
            for oc in range(FC):
                tensor.wait_ge(wol[oc % 2].h, 16 * (oc // 2 + 1))
                if oc >= 2:
                    tensor.wait_ge(vec.h, b3_vec + oc - 1)
                bank = psA if oc % 2 == 0 else psB
                for tt in range(4):
                    for hc in range(FC):
                        mm = tensor.matmul(
                            bank[:, tt * NT:(tt + 1) * NT],
                            wot[:, (oc % 2) * 1024 + hc * 128:(oc % 2) * 1024 + (hc + 1) * 128],
                            bufB[:, hc * TC + tt * NT:hc * TC + (tt + 1) * NT],
                            start=(hc == 0), stop=(hc == FC - 1))
                mmd.inc(mm, 1)

        @block.vector
        def _(vector):
            for oc in range(FC):
                vector.wait_ge(mmd.h, b3_mmd + oc + 1)
                bank = psA if oc % 2 == 0 else psB
                osl = slice(oc * TC, (oc + 1) * TC)
                vec.inc(vector.tensor_add(xs[:, osl], xs[:, osl], bank[:, 0:TC]), 1)

        @block.scalar
        def _(scalar):
            scalar.wait_ge(vec.h, vec.n)

        @block.tensor
        def _(tensor):
            tensor.wait_ge(vec.h, vec.n)

        emit_norm_fm(ctx, xs, bufB, sq2, rows, ones1, onesc, psA, psB,
                     mmd, vec, act, TC, kind="ln", eps_ap=consts[0:1, 0:1])
        y1 = bufB

        b4_mmd, b4_vec, b4_act, b4_rtm, b4_rtr = \
            mmd.n, vec.n, act.n, rtm.n, rtr.n

        for i in range(16):
            @block.tensor
            def _(tensor, i=i):
                if i == 0:
                    tensor.wait_ge(vec.h, b4_vec)
                else:
                    tensor.wait_ge(act.h, b4_act + i)
                for fcb in range(FC):
                    mm = tensor.matmul(psA[:, 0:40],
                                       y1[:, fcb * TC + i * 128:fcb * TC + (i + 1) * 128],
                                       wct[:, fcb * 40:(fcb + 1) * 40],
                                       start=(fcb == 0), stop=(fcb == FC - 1))
                rtm.inc(mm, 1)

            @block.vector
            def _(vector, i=i):
                vector.wait_ge(rtm.h, b4_rtm + i + 1)
                vec.inc(vector.reduce_max(rt_cols[:, 0:1], psA[:, 0:40], axis=AX.X,
                                          negate=True), 1)

            @block.scalar
            def _(scalar, i=i):
                scalar.wait_ge(vec.h, b4_vec + 9 * i + 1)
                act.inc(scalar.activation(rt_et[:, :], psA[:, 0:40], AF.Exp,
                                          bias=rt_cols[:, 0:1]), 1)

            @block.vector
            def _(vector, i=i):
                b = b4_vec + 9 * i
                vector.wait_ge(act.h, b4_act + i + 1)
                vec.inc(vector.reduce_sum(rt_cols[:, 1:2], rt_et[:, :], axis=AX.X), 1)
                vector.wait_ge(vec.h, b + 2)
                vec.inc(vector.reciprocal(rt_cols[:, 1:2], rt_cols[:, 1:2]), 1)
                vector.wait_ge(vec.h, b + 3)
                vec.inc(vector.tensor_scalar_mul(rt_sc[:, :], rt_et[:, :], rt_cols[:, 1:2]), 1)
                vector.wait_ge(vec.h, b + 4)
                vec.inc(vector.tensor_add(rt_sb[:, :], rt_sc[:, :], biasbc[:, :]), 1)
                vector.wait_ge(vec.h, b + 5)
                vec.inc(vector.max(out=rt_t8[:, :], in_=rt_sb[:, :]), 1)
                vector.wait_ge(vec.h, b + 6)
                if i >= 2:
                    vector.wait_ge(stw[i % 2].h, 16 * (i // 2))
                vec.inc(vector.tensor_scalar(rt_kp[:, :], rt_sb[:, :], rt_t8[:, 5:6],
                                             None, ALU.is_ge), 1)
                vector.wait_ge(vec.h, b + 7)
                wsl = w48b[:, (i % 2) * 48:(i % 2) * 48 + 48]
                vec.inc(vector.scalar_tensor_tensor(wsl[:, 0:40], rt_sc[:, :], 2.5,
                                                    rt_kp[:, :], ALU.mult, ALU.mult), 1)
                vector.wait_ge(vec.h, b + 8)
                with nc.allow_low_precision(reason="router zero_w sum output rounded to bf16 like all routing weights"):
                    vec.inc(vector.reduce_sum(wsl[:, 40:41], wsl[:, 32:40], axis=AX.X), 1)

            @block.sync
            def _(sync, i=i):
                if i == 0:
                    sync.wait_ge(gp.h, 1)
                    st.inc(sync.dma_start(
                        out=w_db[:, 48:128].rearrange("(i p) c -> p i c", p=128),
                        in_=wTs[:, 0:16 * 80].rearrange("p (i c) -> p i c", i=16)), 16)
                if i < 15:
                    sync.wait_ge(vec.h, b4_vec + 9 * (i + 1) + 1)
                else:
                    sync.wait_ge(vec.h, b4_vec + 9 * 16 + 1)
                stw[i % 2].inc(sync.dma_start(out=w_db[i * 128:(i + 1) * 128, 0:48],
                                       in_=w48b[:, (i % 2) * 48:(i % 2) * 48 + 48]), 16)

        @block.vector
        def _(vector):
            vector.wait_ge(vec.h, vec.n)
            vec.inc(vector.tensor_copy(rt_cols[:, 0:1], rt_cols[:, 0:1]), 1)

        @block.sync
        def _(sync):
            sync.wait_ge(stw[0].h, 16 * 8)
            sync.wait_ge(stw[1].h, 16 * 8)
            sync.wait_ge(st.h, st.n)
            stw[0].inc(sync.dma_start(out=wTs[:, :], in_=w_db, transpose=True), 16)
            sync.wait_ge(stw[0].h, 16 * 9)
            stw[0].inc(sync.dma_start(out=wT_d, in_=wTs[:, :]), 16)

        b5_vec, b5_bcl = vec.n, bcl.n

        @block.sync
        def _(sync):
            sync.wait_ge(stw[0].h, 16 * 10)
            bcl.inc(sync.dma_start(out=bc_sb[:, :],
                                   in_=wT_d[40:41, :].broadcast_to((128, TC))), 16)

        @block.vector
        def _(vector):
            vector.wait_ge(bcl.h, b5_bcl + 16)
            for fcb in range(FC):
                osl = slice(fcb * TC, (fcb + 1) * TC)
                if fcb >= 1:
                    vector.wait_ge(vec.h, vec.n)
                vec.inc(vector.tensor_mul(sq2[:, 0:TC], y1[:, osl], bc_sb[:, :]), 1)
                vector.wait_ge(vec.h, vec.n)
                vec.inc(vector.tensor_add(xs[:, osl], xs[:, osl], sq2[:, 0:TC]), 1)

        b6 = dict(mmd=mmd.n, vec=vec.n, act=act.n, bcl=bcl.n)

        @block.gpsimd
        def _(gpsimd):
            for e in range(n_experts):
                if e >= 2:
                    gpsimd.wait_ge(mmd.h, b6["mmd"] + 16 * (e - 2) + 9)
                wl[e % 2].inc(gpsimd.dma_start(
                    out=wgu[:, (e % 2) * 8 * 1024:((e % 2) + 1) * 8 * 1024]
                        .rearrange("p (c o) -> p c o", c=FC),
                    in_=l0wgu_i[e].rearrange("(c p) o -> p c o", p=128)), 16)
                if e >= 1:
                    gpsimd.wait_ge(mmd.h, b6["mmd"] + 16 * e + 1)
                    gpsimd.wait_ge(wld2.h, 16 * e)
                wld2.inc(gpsimd.dma_start(
                    out=wdn[:, :].rearrange("p (c o) -> p c o", c=4),
                    in_=l0wdn_i[e].rearrange("(c p) o -> p c o", p=128)), 16)

        @block.sync
        def _(sync):
            for e in range(n_experts):
                if e >= 1:
                    sync.wait_ge(vec.h, b6["vec"] + 18 * e - 9)
                else:
                    sync.wait_ge(vec.h, b6["vec"])
                bcl.inc(sync.dma_start(out=bc_sb[:, :],
                                       in_=wT_d[e:e + 1, :].broadcast_to((128, TC))), 16)

        for e in range(n_experts):
            wgub = wgu[:, (e % 2) * 8 * 1024:]

            @block.tensor
            def _(tensor, e=e, wgub=wgub):
                tensor.wait_ge(wl[e % 2].h, 16 * (e // 2 + 1))
                if e == 0:
                    tensor.wait_ge(vec.h, vec.n)
                for j in range(4):
                    if j >= 1:
                        tensor.wait_ge(act.h, b6["act"] + 4 * e + j)
                        tensor.wait_ge(vec.h, b6["vec"] + 18 * e + 2 * j)
                    elif e >= 1:
                        tensor.wait_ge(act.h, b6["act"] + 4 * e)
                        tensor.wait_ge(vec.h, b6["vec"] + 18 * e)
                    for fcb in range(FC):
                        for tt in range(4):
                            mm = tensor.matmul(
                                psA[:, tt * NT:(tt + 1) * NT],
                                wgub[:, fcb * 1024 + j * 128:fcb * 1024 + (j + 1) * 128],
                                y1[:, fcb * TC + tt * NT:fcb * TC + (tt + 1) * NT],
                                start=(fcb == 0), stop=(fcb == FC - 1))
                    mmd.inc(mm, 1)
                    for fcb in range(FC):
                        for tt in range(4):
                            mm = tensor.matmul(
                                psB[:, tt * NT:(tt + 1) * NT],
                                wgub[:, fcb * 1024 + (4 + j) * 128:fcb * 1024 + (5 + j) * 128],
                                y1[:, fcb * TC + tt * NT:fcb * TC + (tt + 1) * NT],
                                start=(fcb == 0), stop=(fcb == FC - 1))
                    mmd.inc(mm, 1)
                tensor.wait_ge(wld2.h, 16 * (e + 1))
                for hcb in range(FC):
                    if hcb == 0:
                        tensor.wait_ge(vec.h, b6["vec"] + 18 * e + 9)
                        tensor.wait_ge(act.h, b6["act"] + 4 * e + 4)
                    if hcb >= 2:
                        tensor.wait_ge(vec.h, b6["vec"] + 18 * e + 9 + hcb - 1)
                    bank = psA if hcb % 2 == 0 else psB
                    for ic in range(4):
                        for tt in range(4):
                            mm = tensor.matmul(
                                bank[:, tt * NT:(tt + 1) * NT],
                                wdn[:, ic * 1024 + hcb * 128:ic * 1024 + (hcb + 1) * 128],
                                hI[:, ic * TC + tt * NT:ic * TC + (tt + 1) * NT],
                                start=(ic == 0), stop=(ic == 3))
                    mmd.inc(mm, 1)

            @block.scalar
            def _(scalar, e=e):
                for j in range(4):
                    scalar.wait_ge(mmd.h, b6["mmd"] + e * 16 + 2 * j + 1)
                    if j >= 1:
                        scalar.wait_ge(vec.h, b6["vec"] + 18 * e + 2 * j)
                    elif e >= 1:
                        scalar.wait_ge(vec.h, b6["vec"] + 18 * e)
                    act.inc(scalar.activation(sgt[:, :], psA[:, 0:TC], AF.Silu), 1)

            @block.vector
            def _(vector, e=e):
                for j in range(4):
                    vector.wait_ge(act.h, b6["act"] + 4 * e + j + 1)
                    vector.wait_ge(mmd.h, b6["mmd"] + e * 16 + 2 * j + 2)
                    if j == 0:
                        vector.wait_ge(bcl.h, b6["bcl"] + 16 * (e + 1))
                    hsl = slice(j * TC, (j + 1) * TC)
                    vec.inc(vector.tensor_mul(hI[:, hsl], sgt[:, :], psB[:, 0:TC]), 1)
                    vector.wait_ge(vec.h, b6["vec"] + 18 * e + 2 * j + 1)
                    vec.inc(vector.tensor_mul(hI[:, hsl], hI[:, hsl], bc_sb[:, :]), 1)
                vector.wait_ge(vec.h, b6["vec"] + 18 * e + 8)
                vec.inc(vector.tensor_copy(rt_cols[:, 0:1], rt_cols[:, 0:1]), 1)
                for hcb in range(FC):
                    vector.wait_ge(mmd.h, b6["mmd"] + e * 16 + 8 + hcb + 1)
                    bank = psA if hcb % 2 == 0 else psB
                    osl = slice(hcb * TC, (hcb + 1) * TC)
                    vec.inc(vector.tensor_add(xs[:, osl], xs[:, osl], bank[:, 0:TC]), 1)
                vector.wait_ge(vec.h, b6["vec"] + 18 * e + 17)
                vec.inc(vector.tensor_copy(rt_cols[:, 0:1], rt_cols[:, 0:1]), 1)

        @block.sync
        def _(sync):
            sync.wait_ge(vec.h, vec.n)
            for fcb in range(FC):
                st.inc(sync.dma_start(out=x2_o[fcb * 128:(fcb + 1) * 128, :],
                                      in_=xs[:, fcb * TC:(fcb + 1) * TC]), 16)

        @block.scalar
        def _(scalar):
            scalar.wait_ge(vec.h, vec.n)

        @block.tensor
        def _(tensor):
            tensor.wait_ge(vec.h, vec.n)

        emit_norm_fm(ctx, xs, bufB, sq2, rows, ones1, onesc, psA, psB,
                     mmd, vec, act, TC, kind="ln", eps_ap=consts[0:1, 0:1])
        y2 = bufB

        @block.scalar
        def _(scalar):
            scalar.wait_ge(vec.h, vec.n)

        @block.tensor
        def _(tensor):
            tensor.wait_ge(vec.h, vec.n)

        b8_mmd, b8_act, b8_vec, b8_st = mmd.n, act.n, vec.n, st.n

        @block.scalar
        def _(scalar):
            for fcb in range(FC):
                if fcb >= 1:
                    scalar.wait_ge(mmd.h, b8_mmd + fcb)
                act.inc(scalar.activation(sq2[:, 0:TC], y2[:, fcb * TC:(fcb + 1) * TC],
                                          AF.Square), 1)

        @block.tensor
        def _(tensor):
            for fcb in range(FC):
                tensor.wait_ge(act.h, b8_act + fcb + 1)
                for tt in range(4):
                    sl = slice(tt * NT, (tt + 1) * NT)
                    mm2 = tensor.matmul(psB[0:1, sl], onesc[:, :], sq2[:, sl],
                                        start=(fcb == 0), stop=(fcb == FC - 1))
                mmd.inc(mm2, 1)

        work2 = rows[0:1, TC:2 * TC]

        @block.vector
        def _(vector):
            vector.wait_ge(mmd.h, mmd.n)
            vec.inc(vector.tensor_scalar_mul(work2, psB[0:1, 0:TC], 1.0 / D), 1)

        @block.scalar
        def _(scalar):
            scalar.wait_ge(vec.h, vec.n)
            act.inc(scalar.activation(work2, work2, AF.Ln, bias=consts[0:1, 1:2]), 1)
            scalar.wait_ge(act.h, act.n)
            act.inc(scalar.activation(work2, work2, AF.Exp, scale=-0.5), 1)

        @block.tensor
        def _(tensor):
            tensor.wait_ge(act.h, act.n)
            for tt in range(4):
                sl = slice(tt * NT, (tt + 1) * NT)
                mm = tensor.matmul(psA[:, sl], ones1[0:1, :], work2[0:1, sl],
                                   start=True, stop=True)
            mmd.inc(mm, 1)

        b9_vec, b9_st = vec.n, st.n

        @block.vector
        def _(vector):
            vector.wait_ge(mmd.h, mmd.n)
            for fcb in range(FC):
                osl = slice(fcb * TC, (fcb + 1) * TC)
                if fcb >= 1:
                    vector.wait_ge(st.h, b9_st + 32 * fcb)
                vec.inc(vector.tensor_mul(sq2[:, 0:TC], y2[:, osl], psA[:, 0:TC]), 1)
                vector.wait_ge(vec.h, vec.n)
                vec.inc(vector.tensor_scalar_mul(stg[:, 0:TC], sq2[:, 0:TC],
                                                 rgain[:, fcb:fcb + 1]), 1)
                vector.wait_ge(vec.h, vec.n)
                vec.inc(vector.tensor_scalar_mul(stg[:, TC:2 * TC], sq2[:, 0:TC],
                                                 sgain[:, fcb:fcb + 1]), 1)

        @block.sync
        def _(sync):
            for fcb in range(FC):
                sync.wait_ge(vec.h, b9_vec + 3 * (fcb + 1))
                st.inc(sync.dma_start(out=sr_o[fcb * 128:(fcb + 1) * 128, :],
                                      in_=stg[:, 0:TC]), 16)
                st.inc(sync.dma_start(out=ss_o[fcb * 128:(fcb + 1) * 128, :],
                                      in_=stg[:, TC:2 * TC]), 16)
            sync.wait_ge(st.h, st.n)
    return ctx



def build_mid1(ctx):
    ctx.phase("m1")
    nc = ctx.nc
    NP = NPANEL
    CH = 64
    NCH = TC // CH
    ssf = ctx.dt("ssf", [NP * D, TC], BF16, "int")
    srf = ctx.dt("srf", [NP * D, TC], BF16, "int")
    zpad = ctx.dt("zpad", [128, CH], BF16, "in")
    wk_i = ctx.dt("wk1", [D, 128], F32, "in")
    wv_i = ctx.dt("wv1", [D, 128], F32, "in")
    wq_i = ctx.dt("wq1", [D, 128], F32, "in")
    wsmd_i = ctx.dt("wsmd", [D, 3], F32, "in")
    wg_i = ctx.dt("wg", [D, 1], F32, "in")
    w0_i = ctx.dt("w0", [128, 128], F32, "in")
    w1_i = ctx.dt("w1", [128, 128], F32, "in")
    gam_i = ctx.dt("gam", [128, 1], F32, "in")
    o1_o = ctx.dt("o1h", [128, NP * TC], BF16, "int")

    sst = ctx.sb("sst", [128, 2 * TC], BF16)
    srt = ctx.sb("srt", [128, 2 * TC], BF16)
    kb = ctx.sb("kb", [128, TC], BF16)
    vb = ctx.sb("vb", [128, TC], BF16)
    qb = ctx.sb("qb", [128, TC], BF16)
    csb = ctx.sb("csb", [128, FC * NCH], BF16)
    sgf = ctx.sb("sgf", [128, TC], F32)
    spf = ctx.sb("spf", [128, TC], F32)
    scf = ctx.sb("scf", [128, TC], F32)
    s1b = ctx.sb("s1b", [128, TC], BF16)
    Gb = ctx.sb("Gb", [128, TC], BF16)
    Hb = ctx.sb("Hb", [128, TC], BF16)
    k_tok = ctx.sb("k_tok", [128, TC], BF16)
    s1_tok = ctx.sb("s1_tok", [128, TC], BF16)
    G_tok = ctx.sb("G_tok", [128, TC], BF16)
    H_tok = ctx.sb("H_tok", [128, TC], BF16)
    lr_bcf = ctx.sb("lr_bcf", [128, TC], F32)
    valsf = ctx.sb("valsf", [128, TC], F32)
    vsq = ctx.sb("vsq", [128, 1024], F32)
    sxb = ctx.sb("sxb", [128, CH], BF16)
    o1s = ctx.sb("o1s", [128, NP * TC], BF16)
    rows = ctx.sb("rowsm", [1, 2 * TC + 2 * 1024 + 128], F32)
    gmbc = ctx.sb("gmbc", [128, 4 * NCH], F32)
    gdbc = ctx.sb("gdbc", [128, 4 * NCH], F32)
    MU = ctx.sb("MU", [128, 512], F32)
    memcat = ctx.sb("memcat", [128, 256], F32)
    wmb = ctx.sb("wmb", [128, 256], BF16)
    wkb = ctx.sb("wkb", [128, FC * 128], BF16)
    wvb = ctx.sb("wvb", [128, FC * 128], BF16)
    wqb = ctx.sb("wqb", [128, FC * 128], BF16)
    wsmdb = ctx.sb("wsmdb", [128, FC * 3], BF16)
    wgb = ctx.sb("wgb", [128, FC], BF16)
    w0b = ctx.sb("w0b", [128, 128], BF16)
    w1b = ctx.sb("w1b", [128, 128], BF16)
    w1tb = ctx.sb("w1tb", [128, 128], BF16)
    gampl = ctx.sb("gampl", [128, 1], F32)
    ones1 = ctx.sb("ones1m", [1, 128], F32)
    cones = ctx.sb("cones", [1, 128], F32)
    onesc = ctx.sb("onescm", [128, 1], F32)
    consts = ctx.sb("constsm", [128, 2], F32)
    idb = ctx.sb("idb", [128, 128], BF16)
    psA = ctx.ps("psA", [128, 2048], F32)
    psB = ctx.ps("psB", [128, 2048], F32)

    lr_row = rows[0:1, 0:TC]
    gate_row = rows[0:1, TC:2 * TC]
    rstd_row = rows[0:1, 2 * TC:2 * TC + 1024]
    grow = rows[0:1, 2 * TC + 1024:2 * TC + 2048]
    gm_row = rows[0:1, 2 * TC + 2048:2 * TC + 2048 + 32]
    gd_row = rows[0:1, 2 * TC + 2048 + 64:2 * TC + 2048 + 96]

    lds = [ctx.sem("ldsA"), ctx.sem("ldsB")]
    ldr = [ctx.sem("ldrA"), ctx.sem("ldrB")]
    wld = ctx.sem("wldm")
    wls = ctx.sem("wlsm")
    ini = ctx.sem("inim")
    mmd = ctx.sem("mmdm")
    vec = ctx.sem("vecm")
    act = ctx.sem("actm")
    gp = ctx.sem("gpm")
    dt = ctx.sem("dtm")
    sres = ctx.sem("sres")
    st = ctx.sem("stm")

    nload = [0]
    nloadr = [0]

    with nc.Block() as block:
        ctx.block = block

        @block.gpsimd
        def _(gpsimd):
            wld.inc(gpsimd.dma_start(out=wkb[:, :].rearrange("p (c i) -> p c i", c=FC),
                                     in_=wk_i.rearrange("(c p) i -> p c i", p=128)), 16)
            wld.inc(gpsimd.dma_start(out=wvb[:, :].rearrange("p (c i) -> p c i", c=FC),
                                     in_=wv_i.rearrange("(c p) i -> p c i", p=128)), 16)
            wld.inc(gpsimd.dma_start(out=wqb[:, :].rearrange("p (c i) -> p c i", c=FC),
                                     in_=wq_i.rearrange("(c p) i -> p c i", p=128)), 16)
            wld.inc(gpsimd.dma_start(out=wsmdb[:, :].rearrange("p (c i) -> p c i", c=FC),
                                     in_=wsmd_i.rearrange("(c p) i -> p c i", p=128)), 16)
            with nc.allow_non_contiguous_dma(reason="tiny 4KB per-head gate weight column"):
                wld.inc(gpsimd.dma_start(out=wgb[:, :].rearrange("p (c i) -> p c i", c=FC),
                                         in_=wg_i.rearrange("(c p) i -> p c i", p=128)), 16)
            wld.inc(gpsimd.dma_start(out=w0b[:, :], in_=w0_i), 16)
            wld.inc(gpsimd.dma_start(out=w1b[:, :], in_=w1_i), 16)
            gpsimd.memset(ones1[:, :], 1.0)
            gpsimd.memset(cones[:, :], 2.0 * 0.01 / 128.0)
            gpsimd.memset(onesc[:, :], 1.0)
            gpsimd.memset(consts[:, 0:1], 1e-6)
            gpsimd.memset(MU[:, :], 0.0)
            gpsimd.memset(o1s[:, :], 0.0)
            ini.inc(gpsimd.iota(scf[:, 0:128], pattern=[[1, 128]], base=0,
                                channel_multiplier=-1,
                                allow_small_or_imprecise_dtypes=True), 1)
            gpsimd.wait_ge(ini.h, ini.n)
            ini.inc(gpsimd.tensor_single_scalar(idb[:, :], scf[:, 0:128], 0.0,
                                                ALU.is_equal), 1)

        @block.sync
        def _(sync):
            wls.inc(sync.dma_start(out=memcat[:, 0:128], in_=w0_i), 16)
            wls.inc(sync.dma_start(out=memcat[:, 128:256], in_=w1_i), 16)
            wls.inc(sync.dma_start(out=gampl[:, :], in_=gam_i), 16)

        @block.vector
        def _(vector):
            vector.wait_ge(wld.h, wld.n)
            vector.wait_ge(wls.h, wls.n)
            vec.inc(vector.tensor_scalar_add(gampl[:, :], gampl[:, :], 1.0), 1)

        @block.tensor
        def _(tensor):
            tensor.wait_ge(wld.h, wld.n)
            tensor.wait_ge(wls.h, wls.n)
            tensor.wait_ge(ini.h, ini.n)
            mmd.inc(tensor.transpose(psA[:, 0:64].bitcast(BF16), w1b[:, :], idb[:, :]), 1)

        @block.vector
        def _(vector):
            vector.wait_ge(mmd.h, mmd.n)
            vec.inc(vector.tensor_copy(w1tb[:, :], psA[:, 0:64].bitcast(BF16)), 1)

        sres_ready = 0

        for p in range(NP):
            batch_first = (p % 4 == 0)
            batch_last = (p % 4 == 3)
            pc = (p % 4) * NCH
            bb = dict(mmd=mmd.n, vec=vec.n, act=act.n, gp=gp.n)

            @block.sync
            def _(sync, p=p, bb=bb):
                for fcb in range(FC):
                    idx = nload[0]
                    par = idx % 2
                    if fcb >= 2:
                        sync.wait_ge(mmd.h, bb["mmd"] + fcb)
                        sync.wait_ge(gp.h, bb["gp"] + fcb)
                    lds[par].inc(sync.dma_start(
                        out=sst[:, par * TC:(par + 1) * TC],
                        in_=ssf[p * D + fcb * 128:p * D + (fcb + 1) * 128, :]), 16)
                    nload[0] += 1

            @block.tensor
            def _(tensor, p=p, bb=bb):
                tensor.wait_ge(vec.h, bb["vec"])
                tensor.wait_ge(act.h, bb["act"])
                if p == 0:
                    tensor.wait_ge(wld.h, wld.n)
                    tensor.wait_ge(ini.h, ini.n)
                for fcb in range(FC):
                    idx = nload[0] - FC + fcb
                    par = idx % 2
                    tensor.wait_ge(lds[par].h, 16 * (idx // 2 + 1))
                    tile = sst[:, par * TC:(par + 1) * TC]
                    for tt in range(4):
                        tensor.matmul(psA[:, tt * NT:(tt + 1) * NT],
                                      wkb[:, fcb * 128:(fcb + 1) * 128],
                                      tile[:, tt * NT:(tt + 1) * NT],
                                      start=(fcb == 0), stop=(fcb == FC - 1))
                    for tt in range(4):
                        mm = tensor.matmul(psB[0:1, tt * NT:(tt + 1) * NT],
                                           wsmdb[:, fcb * 3:fcb * 3 + 1],
                                           tile[:, tt * NT:(tt + 1) * NT],
                                           start=(fcb == 0), stop=(fcb == FC - 1))
                    mmd.inc(mm, 1)

            @block.vector
            def _(vector, p=p, bb=bb):
                for fcb in range(FC):
                    idx = nload[0] - FC + fcb
                    par = idx % 2
                    vector.wait_ge(lds[par].h, 16 * (idx // 2 + 1))
                    if fcb == 0 and p >= 1:
                        vector.wait_ge(mmd.h, bb["mmd"])
                    tile = sst[:, par * TC:(par + 1) * TC]
                    with nc.allow_low_precision(reason="chunk-mean in bf16 feeds sigmoid gates; well within tolerance"):
                        gp.inc(vector.tensor_reduce(
                            csb[:, fcb * NCH:(fcb + 1) * NCH],
                            tile[:, :].rearrange("q (m c) -> q m c", c=CH),
                            axis=AX.X, op=ALU.add), 1)

            @block.vector
            def _(vector, bb=bb):
                vector.wait_ge(mmd.h, bb["mmd"] + FC)
                vec.inc(vector.tensor_copy(kb[:, :], psA[:, 0:TC]), 1)

            @block.scalar
            def _(scalar, bb=bb):
                scalar.wait_ge(mmd.h, bb["mmd"] + FC)
                act.inc(scalar.activation(lr_row, psB[0:1, 0:TC], AF.Sigmoid), 1)

            b1 = dict(mmd=mmd.n, vec=vec.n, act=act.n, gp=gp.n)

            @block.sync
            def _(sync, p=p, b1=b1):
                for fcb in range(FC):
                    idx = nload[0]
                    par = idx % 2
                    if fcb < 2:
                        sync.wait_ge(mmd.h, b1["mmd"] - 1 + fcb)
                        sync.wait_ge(gp.h, b1["gp"] - 1 + fcb)
                    else:
                        sync.wait_ge(mmd.h, b1["mmd"] + fcb)
                    lds[par].inc(sync.dma_start(
                        out=sst[:, par * TC:(par + 1) * TC],
                        in_=ssf[p * D + fcb * 128:p * D + (fcb + 1) * 128, :]), 16)
                    nload[0] += 1

            @block.tensor
            def _(tensor, p=p, b1=b1):
                tensor.wait_ge(vec.h, b1["vec"])
                tensor.wait_ge(act.h, b1["act"])
                for fcb in range(FC):
                    idx = nload[0] - FC + fcb
                    par = idx % 2
                    tensor.wait_ge(lds[par].h, 16 * (idx // 2 + 1))
                    tile = sst[:, par * TC:(par + 1) * TC]
                    for tt in range(4):
                        mm = tensor.matmul(psA[:, tt * NT:(tt + 1) * NT],
                                           wvb[:, fcb * 128:(fcb + 1) * 128],
                                           tile[:, tt * NT:(tt + 1) * NT],
                                           start=(fcb == 0), stop=(fcb == FC - 1))
                    mmd.inc(mm, 1)
                tensor.wait_ge(gp.h, b1["gp"])
                for fcb in range(FC):
                    tensor.matmul(psB[0:1, 0:NCH],
                                  wsmdb[:, fcb * 3 + 1:fcb * 3 + 2],
                                  csb[:, fcb * NCH:(fcb + 1) * NCH],
                                  start=(fcb == 0), stop=(fcb == FC - 1))
                    mm = tensor.matmul(psB[0:1, 512:512 + NCH],
                                       wsmdb[:, fcb * 3 + 2:fcb * 3 + 3],
                                       csb[:, fcb * NCH:(fcb + 1) * NCH],
                                       start=(fcb == 0), stop=(fcb == FC - 1))
                mmd.inc(mm, 1)

            @block.vector
            def _(vector, b1=b1):
                vector.wait_ge(mmd.h, b1["mmd"] + FC)
                vec.inc(vector.tensor_copy(vb[:, :], psA[:, 0:TC]), 1)

            @block.scalar
            def _(scalar, b1=b1):
                scalar.wait_ge(mmd.h, b1["mmd"] + FC + 1)
                act.inc(scalar.activation(gm_row, psB[0:1, 0:NCH],
                                          AF.Sigmoid, scale=1.0 / CH), 1)
                scalar.wait_ge(act.h, act.n)
                act.inc(scalar.activation(gd_row, psB[0:1, 512:512 + NCH],
                                          AF.Sigmoid, scale=-1.0 / CH), 1)

            b2 = dict(mmd=mmd.n, vec=vec.n, act=act.n)

            @block.tensor
            def _(tensor, b2=b2):
                tensor.wait_ge(vec.h, b2["vec"])
                tensor.wait_ge(act.h, b2["act"])
                for tt in range(4):
                    tensor.matmul(psA[:, tt * NT:(tt + 1) * NT],
                                  cones[0:1, :], lr_row[0:1, tt * NT:(tt + 1) * NT],
                                  start=True, stop=True)
                tensor.matmul(psB[:, 0:NCH], ones1[0:1, :], gm_row[0:1, 0:NCH],
                              start=True, stop=True)
                mm = tensor.matmul(psB[:, 512:512 + NCH], ones1[0:1, :],
                                   gd_row[0:1, 0:NCH], start=True, stop=True)
                mmd.inc(mm, 1)

            @block.vector
            def _(vector, b2=b2, pc=pc):
                vector.wait_ge(mmd.h, b2["mmd"] + 1)
                vec.inc(vector.tensor_copy(lr_bcf[:, :], psA[:, 0:TC]), 1)
                vector.wait_ge(vec.h, vec.n)
                vec.inc(vector.tensor_copy(gmbc[:, pc:pc + NCH], psB[:, 0:NCH]), 1)
                vector.wait_ge(vec.h, vec.n)
                vec.inc(vector.tensor_copy(gdbc[:, pc:pc + NCH],
                                           psB[:, 512:512 + NCH]), 1)

            b3 = dict(mmd=mmd.n, vec=vec.n, act=act.n)

            @block.sync
            def _(sync, p=p, b3=b3, batch_last=batch_last):
                for fcb in range(FC):
                    idx = nloadr[0]
                    par = idx % 2
                    if fcb >= 2:
                        sync.wait_ge(mmd.h, b3["mmd"] + fcb)
                    ldr[par].inc(sync.dma_start(
                        out=srt[:, par * TC:par * TC + TC - 63],
                        in_=srf[p * D + fcb * 128:p * D + (fcb + 1) * 128, 63:TC]), 16)
                    if batch_last:
                        ldr[par].inc(sync.dma_start(
                            out=srt[:, par * TC + TC - 63:(par + 1) * TC],
                            in_=zpad[:, 0:63]), 16)
                    else:
                        ldr[par].inc(sync.dma_start(
                            out=srt[:, par * TC + TC - 63:(par + 1) * TC],
                            in_=srf[(p + 1) * D + fcb * 128:(p + 1) * D + (fcb + 1) * 128,
                                    0:63]), 16)
                    nloadr[0] += 1

            @block.tensor
            def _(tensor, p=p, b3=b3):
                tensor.wait_ge(vec.h, b3["vec"])
                for fcb in range(FC):
                    idx = nloadr[0] - FC + fcb
                    par = idx % 2
                    tensor.wait_ge(ldr[par].h, 32 * (idx // 2 + 1))
                    tile = srt[:, par * TC:(par + 1) * TC]
                    for tt in range(4):
                        tensor.matmul(psA[:, tt * NT:(tt + 1) * NT],
                                      wqb[:, fcb * 128:(fcb + 1) * 128],
                                      tile[:, tt * NT:(tt + 1) * NT],
                                      start=(fcb == 0), stop=(fcb == FC - 1))
                    for tt in range(4):
                        mm = tensor.matmul(psB[0:1, tt * NT:(tt + 1) * NT],
                                           wgb[:, fcb:fcb + 1],
                                           tile[:, tt * NT:(tt + 1) * NT],
                                           start=(fcb == 0), stop=(fcb == FC - 1))
                    mmd.inc(mm, 1)

            @block.vector
            def _(vector, b3=b3):
                vector.wait_ge(mmd.h, b3["mmd"] + FC)
                vec.inc(vector.tensor_copy(qb[:, :], psA[:, 0:TC]), 1)

            @block.scalar
            def _(scalar, b3=b3):
                scalar.wait_ge(mmd.h, b3["mmd"] + FC)
                act.inc(scalar.activation(gate_row, psB[0:1, 0:TC], AF.Sigmoid), 1)

            b4 = dict(mmd=mmd.n, vec=vec.n, act=act.n)

            @block.tensor
            def _(tensor, b4=b4):
                tensor.wait_ge(vec.h, b4["vec"])
                tensor.wait_ge(act.h, b4["act"])
                for tt in range(4):
                    mm = tensor.matmul(psB[:, tt * NT:(tt + 1) * NT],
                                       w0b[:, :], kb[:, tt * NT:(tt + 1) * NT],
                                       start=True, stop=True)
                mmd.inc(mm, 1)

            @block.scalar
            def _(scalar, b4=b4):
                scalar.wait_ge(mmd.h, b4["mmd"] + 1)
                act.inc(scalar.activation(sgf[:, :], psB[:, 0:TC], AF.Sigmoid), 1)

            @block.vector
            def _(vector, b4=b4):
                vector.wait_ge(act.h, b4["act"] + 1)
                vec.inc(vector.tensor_mul(s1b[:, :], sgf[:, :], psB[:, 0:TC]), 1)
                vector.wait_ge(vec.h, vec.n)
                vec.inc(vector.tensor_mul(scf[:, :], s1b[:, :], sgf[:, :]), 1)
                vector.wait_ge(vec.h, vec.n)
                vec.inc(vector.scalar_tensor_tensor(spf[:, :], scf[:, :], -1.0,
                                                    sgf[:, :], ALU.mult, ALU.add), 1)
                vector.wait_ge(vec.h, vec.n)
                vec.inc(vector.tensor_add(spf[:, :], spf[:, :], s1b[:, :]), 1)

            @block.tensor
            def _(tensor, b4=b4):
                tensor.wait_ge(vec.h, b4["vec"] + 1)
                for tt in range(4):
                    mm = tensor.matmul(psA[:, tt * NT:(tt + 1) * NT],
                                       w1b[:, :], s1b[:, tt * NT:(tt + 1) * NT],
                                       start=True, stop=True)
                mmd.inc(mm, 1)

            @block.vector
            def _(vector, b4=b4):
                vector.wait_ge(mmd.h, b4["mmd"] + 2)
                vec.inc(vector.tensor_sub(scf[:, :], psA[:, 0:TC], vb[:, :]), 1)
                vector.wait_ge(vec.h, vec.n)
                vec.inc(vector.tensor_mul(Gb[:, :], scf[:, :], lr_bcf[:, :]), 1)

            @block.tensor
            def _(tensor, b4=b4):
                tensor.wait_ge(vec.h, b4["vec"] + 6)
                tensor.wait_ge(act.h, b4["act"] + 1)
                for tt in range(4):
                    mm = tensor.matmul(psB[:, tt * NT:(tt + 1) * NT],
                                       w1tb[:, :], Gb[:, tt * NT:(tt + 1) * NT],
                                       start=True, stop=True)
                mmd.inc(mm, 1)

            @block.vector
            def _(vector, b4=b4):
                vector.wait_ge(mmd.h, b4["mmd"] + 3)
                vec.inc(vector.tensor_mul(Hb[:, :], psB[:, 0:TC], spf[:, :]), 1)
                vector.wait_ge(vec.h, vec.n)
                vec.inc(vector.tensor_copy(rows[0:1, 0:1], ones1[0:1, 0:1]), 1)

            @block.scalar
            def _(scalar, p=p, b4=b4):
                scalar.wait_ge(vec.h, b4["vec"] + 8)
                for m in range(16):
                    dt.inc(scalar.dma_start(out=k_tok[:, m * 128:(m + 1) * 128],
                                            in_=kb[:, m * 128:(m + 1) * 128],
                                            transpose=True), 16)
                    dt.inc(scalar.dma_start(out=s1_tok[:, m * 128:(m + 1) * 128],
                                            in_=s1b[:, m * 128:(m + 1) * 128],
                                            transpose=True), 16)
                    dt.inc(scalar.dma_start(out=G_tok[:, m * 128:(m + 1) * 128],
                                            in_=Gb[:, m * 128:(m + 1) * 128],
                                            transpose=True), 16)
                    dt.inc(scalar.dma_start(out=H_tok[:, m * 128:(m + 1) * 128],
                                            in_=Hb[:, m * 128:(m + 1) * 128],
                                            transpose=True), 16)

            if batch_first:
                @block.gpsimd
                def _(gpsimd, p=p):
                    if p >= 1:
                        gpsimd.wait_ge(vec.h, vec.n)
                    sres.inc(gpsimd.memset(MU[:, :], 0.0), 1)
                sres_ready += 1

            b5 = dict(mmd=mmd.n, vec=vec.n, act=act.n)

            for m in range(NCH):
                base = (m % 2) * 64
                t128 = m // 2
                tsl = slice(t128 * 128, (t128 + 1) * 128)
                csl = slice(m * CH, (m + 1) * CH)
                gcol = pc + m

                @block.tensor
                def _(tensor, p=p, m=m, base=base, tsl=tsl, csl=csl, b5=b5,
                      sres_ready=sres_ready):
                    if m == 0:
                        tensor.wait_ge(dt.h, 64 * 16 * (p + 1))
                        tensor.wait_ge(vec.h, b5["vec"])
                        tensor.wait_ge(sres.h, sres_ready)
                    if m >= 1:
                        tensor.wait_ge(vec.h, b5["vec"] + 3 * m)
                    mm = tensor.matmul(psA[:, 1024:1152],
                                       k_tok[base:base + 64, tsl],
                                       H_tok[base:base + 64, tsl],
                                       start=True, stop=True, skip_group_check=True)
                    mm = tensor.matmul(psA[:, 1152:1280],
                                       s1_tok[base:base + 64, tsl],
                                       G_tok[base:base + 64, tsl],
                                       start=True, stop=True, skip_group_check=True)
                    mmd.inc(mm, 1)
                    tensor.wait_ge(vec.h, b5["vec"] + 3 * m + 3)
                    if m >= 1:
                        tensor.wait_ge(act.h, b5["act"] + 3 * m)
                    mm = tensor.matmul(psB[:, 1024:1088], wmb[:, 0:128], qb[:, csl],
                                       start=True, stop=True)
                    mmd.inc(mm, 1)
                    tensor.wait_ge(act.h, b5["act"] + 3 * m + 2)
                    mm = tensor.matmul(psB[:, 1536:1600], wmb[:, 128:256], sxb[:, :],
                                       start=True, stop=True)
                    mmd.inc(mm, 1)

                @block.vector
                def _(vector, m=m, gcol=gcol, b5=b5, sres_ready=sres_ready):
                    vector.wait_ge(mmd.h, b5["mmd"] + 3 * m + 1)
                    if m == 0:
                        vector.wait_ge(sres.h, sres_ready)
                    vec.inc(vector.scalar_tensor_tensor(MU[:, 0:256], MU[:, 0:256],
                                                        gmbc[:, gcol:gcol + 1],
                                                        psA[:, 1024:1280],
                                                        ALU.mult, ALU.subtract), 1)
                    vector.wait_ge(vec.h, vec.n)
                    vec.inc(vector.scalar_tensor_tensor(MU[:, 256:512], MU[:, 256:512],
                                                        gdbc[:, gcol:gcol + 1],
                                                        MU[:, 0:256],
                                                        ALU.mult, ALU.add), 1)
                    vector.wait_ge(vec.h, vec.n)
                    vec.inc(vector.tensor_add(wmb[:, :], memcat[:, :],
                                              MU[:, 256:512]), 1)

                @block.scalar
                def _(scalar, m=m, b5=b5):
                    scalar.wait_ge(mmd.h, b5["mmd"] + 3 * m + 2)
                    act.inc(scalar.activation(sxb[:, :], psB[:, 1024:1088], AF.Silu), 1)
                    act.inc(scalar.copy(consts[0:1, 1:2], ones1[0:1, 0:1]), 1)
                    scalar.wait_ge(mmd.h, b5["mmd"] + 3 * m + 3)
                    act.inc(scalar.copy(valsf[:, m * CH:(m + 1) * CH],
                                        psB[:, 1536:1600]), 1)

            b6 = dict(mmd=mmd.n, vec=vec.n, act=act.n)

            for sc in range(2):
                vsl = slice(sc * 1024, sc * 1024 + 1024)
                ob = (p // 4) * 8192 + (p % 4) * 2048 + sc * 1024 + 63
                w = 961 if (batch_last and sc == 1) else 1024

                @block.scalar
                def _(scalar, sc=sc, vsl=vsl, b6=b6):
                    scalar.wait_ge(act.h, b6["act"] - 3 * NCH + 3 * (16 * (sc + 1)))
                    if sc == 1:
                        scalar.wait_ge(mmd.h, mmd.n)
                    act.inc(scalar.activation(vsq[:, :], valsf[:, vsl], AF.Square), 1)

                @block.tensor
                def _(tensor, sc=sc):
                    tensor.wait_ge(act.h, act.n)
                    tensor.wait_ge(vec.h, vec.n)
                    for u in range(2):
                        mm = tensor.matmul(psB[0:1, u * NT:(u + 1) * NT], onesc[:, :],
                                           vsq[:, u * NT:(u + 1) * NT],
                                           start=True, stop=True)
                    mmd.inc(mm, 1)

                @block.vector
                def _(vector):
                    vector.wait_ge(mmd.h, mmd.n)
                    vec.inc(vector.tensor_scalar_mul(rstd_row, psB[0:1, 0:1024],
                                                     1.0 / 128.0), 1)

                @block.scalar
                def _(scalar, sc=sc):
                    scalar.wait_ge(vec.h, vec.n)
                    act.inc(scalar.activation(rstd_row, rstd_row, AF.Ln,
                                              bias=consts[0:1, 0:1]), 1)
                    scalar.wait_ge(act.h, act.n)
                    act.inc(scalar.activation(rstd_row, rstd_row, AF.Exp, scale=-0.5), 1)

                @block.vector
                def _(vector, sc=sc):
                    vector.wait_ge(act.h, act.n)
                    vec.inc(vector.tensor_mul(grow, rstd_row,
                                              gate_row[0:1, sc * 1024:sc * 1024 + 1024]), 1)

                @block.tensor
                def _(tensor, sc=sc):
                    tensor.wait_ge(vec.h, vec.n)
                    for u in range(2):
                        mm = tensor.matmul(psB[:, u * NT:(u + 1) * NT], ones1[0:1, :],
                                           grow[0:1, u * NT:(u + 1) * NT],
                                           start=True, stop=True)
                    mmd.inc(mm, 1)

                @block.vector
                def _(vector, sc=sc, vsl=vsl, ob=ob, w=w):
                    vector.wait_ge(mmd.h, mmd.n)
                    vec.inc(vector.scalar_tensor_tensor(
                        o1s[:, ob:ob + w], valsf[:, sc * 1024:sc * 1024 + w],
                        gampl[:, 0:1], psB[:, 0:w], ALU.mult, ALU.mult), 1)
                    vector.wait_ge(vec.h, vec.n)
                    vec.inc(vector.tensor_copy(rows[0:1, 0:1], ones1[0:1, 0:1]), 1)

        @block.sync
        def _(sync):
            sync.wait_ge(vec.h, vec.n)
            st.inc(sync.dma_start(out=o1_o, in_=o1s[:, :]), 16)
            sync.wait_ge(st.h, st.n)
    return ctx



def build_post1(ctx):
    ctx.phase("q1")
    nc = ctx.nc
    TH = TC // 2
    x2_i = ctx.dt("x2", [D, TC], F32, "int")
    vals_i = ctx.dt("vals", [D, TC], BF16, "int")
    wcmb_i = ctx.dt("wcmb", [FC * D, 128], F32, "in")
    pwt_i = ctx.dt("pwt", [D, 64], F32, "in")
    u123_i = ctx.dt("u123", [64, 48], F32, "in")
    b2_i = ctx.dt("b2", [12, D], F32, "in")
    w1_i = ctx.dt("w1s", [8 * 12, D, NT], F32, "in")
    w2_i = ctx.dt("w2s", [12 * 32, 128, D], F32, "in")
    out_o = ctx.dt("out", [D, TC], F32, "out")
    zer_i = ctx.dt("zer", [TC, 80], BF16, "in")
    w_db = ctx.dt("w_db", [TC, 128], BF16, "int")
    wT_d = ctx.dt("wT_d", [128, TC], BF16, "int")

    xs = ctx.sb("xs", [128, FC * TC], F32)
    h_g = ctx.sb("h_g", [128, 32 * 1024], BF16)
    y3h = ctx.sb("y3h", [128, FC * TH], BF16)
    y3s = ctx.sb("y3s", [128, FC * TH], BF16)
    w1b = ctx.sb("w1b", [128, 2 * FC * NT], BF16)
    w2b = ctx.sb("w2b", [128, 2 * D], BF16)
    cmbb = ctx.sb("cmbb", [128, 2 * 1024], BF16)
    sq2 = ctx.sb("sq2", [128, TH], F32)
    rows = ctx.sb("rows", [1, 2 * TH], F32)
    bc1 = ctx.sb("bc1", [128, TH], BF16)
    xab = ctx.sb("xab", [64, TH], BF16)
    u123 = ctx.sb("u123s", [64, 48], BF16)
    b2b = ctx.sb("b2b", [44, FC * 128], BF16)
    pwtb = ctx.sb("pwtb", [128, FC * 64], BF16)
    wTs = ctx.sb("wTs", [128, TH], BF16)
    w48b = ctx.sb("w48b", [128, 96], BF16)
    rt_et = ctx.sb("rt_et", [128, 3 * 16], F32)
    rt_t8 = ctx.sb("rt_t8", [128, 8], F32)
    rt_kp = ctx.sb("rt_kp", [128, 16], F32)
    rt_cols = ctx.sb("rt_cols", [128, 4], F32)
    ones1 = ctx.sb("ones1", [1, 128], F32)
    onesc = ctx.sb("onesc", [128, 1], F32)
    consts = ctx.sb("consts", [128, 4], F32)
    psA = ctx.ps("psA", [128, 2048], F32)
    psB = ctx.ps("psB", [128, 2048], F32)

    ld = ctx.sem("ld")
    cli = ctx.sem("cli")
    cld = [ctx.sem("cldA"), ctx.sem("cldB")]
    w1l = [ctx.sem("w1lA"), ctx.sem("w1lB")]
    w2l = [ctx.sem("w2lA"), ctx.sem("w2lB"), ctx.sem("w2lC"), ctx.sem("w2lD")]
    bcl = ctx.sem("bcl")
    mmd = ctx.sem("mmd")
    vec = ctx.sem("vec")
    act = ctx.sem("act")
    gp = ctx.sem("gp")
    stw = [ctx.sem("stwA"), ctx.sem("stwB")]
    st = ctx.sem("st")

    with nc.Block() as block:
        ctx.block = block

        @block.sync
        def _(sync):
            for fcb in range(FC):
                ld.inc(sync.dma_start(out=xs[:, fcb * TC:(fcb + 1) * TC],
                                      in_=x2_i[fcb * 128:(fcb + 1) * 128, :]), 16)
            for fcb in range(FC):
                ld.inc(sync.dma_start(out=h_g[:, fcb * TC:(fcb + 1) * TC],
                                      in_=vals_i[fcb * 128:(fcb + 1) * 128, :]), 16)

        @block.gpsimd
        def _(gpsimd):
            cli.inc(gpsimd.dma_start(out=pwtb[:, :].rearrange("p (c o) -> p c o", c=FC),
                                     in_=pwt_i.rearrange("(c p) o -> p c o", p=128)), 16)
            cli.inc(gpsimd.dma_start(out=u123[:, :], in_=u123_i), 16)
            cli.inc(gpsimd.dma_start(out=b2b[32:44, :].rearrange("p (c o) -> p c o", c=FC),
                                     in_=b2_i.rearrange("l (c o) -> l c o", c=FC)), 16)
            gp.inc(gpsimd.memset(ones1[:, :], 1.0), 1)
            gpsimd.memset(onesc[:, :], 1.0)
            gpsimd.memset(consts[:, 0:1], 1e-5)
            gpsimd.memset(w48b[:, :], 0.0)
            gp.inc(gpsimd.memset(wTs[:, :], 0.0), 1)

        b0 = dict(mmd=mmd.n, vec=vec.n)

        @block.gpsimd
        def _(gpsimd, b0=b0):
            for oc in range(FC):
                if oc >= 2:
                    gpsimd.wait_ge(mmd.h, b0["mmd"] + oc)
                cld[oc % 2].inc(gpsimd.dma_start(
                    out=cmbb[:, (oc % 2) * 1024:(oc % 2 + 1) * 1024]
                        .rearrange("p (c o) -> p c o", c=FC),
                    in_=wcmb_i[oc * D:(oc + 1) * D, :].rearrange("(c p) o -> p c o", p=128)), 16)

        @block.tensor
        def _(tensor, b0=b0):
            tensor.wait_ge(ld.h, 16 * 16)
            for oc in range(FC):
                tensor.wait_ge(cld[oc % 2].h, 16 * (oc // 2 + 1))
                if oc >= 2:
                    tensor.wait_ge(vec.h, b0["vec"] + oc - 1)
                bank = psA if oc % 2 == 0 else psB
                for tt in range(4):
                    for hc in range(FC):
                        mm = tensor.matmul(
                            bank[:, tt * NT:(tt + 1) * NT],
                            cmbb[:, (oc % 2) * 1024 + hc * 128:(oc % 2) * 1024 + (hc + 1) * 128],
                            h_g[:, hc * TC + tt * NT:hc * TC + (tt + 1) * NT],
                            start=(hc == 0), stop=(hc == FC - 1))
                mmd.inc(mm, 1)

        @block.vector
        def _(vector, b0=b0):
            for oc in range(FC):
                vector.wait_ge(mmd.h, b0["mmd"] + oc + 1)
                bank = psA if oc % 2 == 0 else psB
                osl = slice(oc * TC, (oc + 1) * TC)
                vec.inc(vector.tensor_add(xs[:, osl], xs[:, osl], bank[:, 0:TC]), 1)
            vector.wait_ge(vec.h, vec.n)
            vec.inc(vector.tensor_copy(rows[0:1, 0:1], onesc[0:1, 0:1]), 1)

        for th in range(2):
            ho = th * TH

            bl = dict(mmd=mmd.n, vec=vec.n, act=act.n)

            @block.scalar
            def _(scalar, bl=bl):
                scalar.wait_ge(vec.h, bl["vec"])

            @block.tensor
            def _(tensor, bl=bl):
                tensor.wait_ge(vec.h, bl["vec"])
                tensor.wait_ge(gp.h, 2)

            emit_norm_fm(ctx, xs, y3h, sq2, rows, ones1, onesc, psA, psB,
                         mmd, vec, act, TH, kind="ln", eps_ap=consts[0:1, 0:1],
                         in_stride=TC, in_off=ho)

            b1 = dict(mmd=mmd.n, vec=vec.n, act=act.n)

            @block.tensor
            def _(tensor, b1=b1, ho=ho):
                tensor.wait_ge(vec.h, b1["vec"])
                tensor.wait_ge(cli.h, 48)
                for fcb in range(FC):
                    for tt in range(2):
                        mm = tensor.matmul(psA[0:64, tt * NT:(tt + 1) * NT],
                                           pwtb[:, fcb * 64:(fcb + 1) * 64],
                                           y3h[:, fcb * TH + tt * NT:fcb * TH + (tt + 1) * NT],
                                           start=(fcb == 0), stop=(fcb == FC - 1))
                mmd.inc(mm, 1)

            @block.vector
            def _(vector, b1=b1):
                vector.wait_ge(mmd.h, b1["mmd"] + 1)
                vec.inc(vector.tensor_copy(xab[:, :], psA[0:64, 0:TH]), 1)
                vector.wait_ge(vec.h, vec.n)
                vec.inc(vector.tensor_copy(rt_cols[:, 0:1], onesc[:, 0:1]), 1)

            b2r = dict(mmd=mmd.n, vec=vec.n, act=act.n, stw0=stw[0].n, stw1=stw[1].n,
                       st=st.n)
            NV = 22

            for i in range(8):
                @block.tensor
                def _(tensor, i=i, b2r=b2r):
                    if i == 0:
                        tensor.wait_ge(vec.h, b2r["vec"])
                    else:
                        tensor.wait_ge(act.h, b2r["act"] + 3 * i)
                    for r in range(3):
                        mm = tensor.matmul(psB[:, r * 512:r * 512 + 12],
                                           xab[:, i * 128:(i + 1) * 128],
                                           u123[:, r * 16:r * 16 + 12],
                                           start=True, stop=True)
                    mmd.inc(mm, 1)

                @block.vector
                def _(vector, i=i, b2r=b2r):
                    b = b2r["vec"] + NV * i
                    vector.wait_ge(mmd.h, b2r["mmd"] + i + 1)
                    vector.wait_ge(vec.h, b)
                    for r in range(3):
                        vec.inc(vector.reduce_max(rt_cols[:, r:r + 1],
                                                  psB[:, r * 512:r * 512 + 12],
                                                  axis=AX.X, negate=True), 1)
                        vector.wait_ge(vec.h, b + r + 1)

                @block.scalar
                def _(scalar, i=i, b2r=b2r):
                    scalar.wait_ge(vec.h, b2r["vec"] + NV * i + 3)
                    for r in range(3):
                        act.inc(scalar.activation(
                            rt_et[:, r * 16:r * 16 + 12], psB[:, r * 512:r * 512 + 12],
                            AF.Exp, bias=rt_cols[:, r:r + 1],
                            scale=1.0 / (1.0 + 1e-8)), 1)

                @block.vector
                def _(vector, i=i, b2r=b2r):
                    b = b2r["vec"] + NV * i + 3
                    vector.wait_ge(act.h, b2r["act"] + 3 * i + 3)
                    if i >= 2:
                        vector.wait_ge(stw[i % 2].h,
                                       (b2r["stw0"] if i % 2 == 0 else b2r["stw1"])
                                       + 16 * (i // 2))
                    nb = 0
                    for r in range(3):
                        ww = w48b[:, (i % 2) * 48 + r * 16:(i % 2) * 48 + r * 16 + 12]
                        vec.inc(vector.max(out=rt_t8[:, :],
                                           in_=rt_et[:, r * 16:r * 16 + 12]), 1)
                        nb += 1; vector.wait_ge(vec.h, b + nb)
                        vec.inc(vector.tensor_scalar(rt_kp[:, 0:12],
                                                     rt_et[:, r * 16:r * 16 + 12],
                                                     rt_t8[:, 2:3], None, ALU.is_ge), 1)
                        nb += 1; vector.wait_ge(vec.h, b + nb)
                        vec.inc(vector.tensor_mul(rt_kp[:, 0:12], rt_kp[:, 0:12],
                                                  rt_et[:, r * 16:r * 16 + 12]), 1)
                        nb += 1; vector.wait_ge(vec.h, b + nb)
                        vec.inc(vector.reduce_sum(rt_cols[:, 3:4], rt_kp[:, 0:12],
                                                  axis=AX.X), 1)
                        nb += 1; vector.wait_ge(vec.h, b + nb)
                        vec.inc(vector.reciprocal(rt_cols[:, 3:4], rt_cols[:, 3:4]), 1)
                        nb += 1; vector.wait_ge(vec.h, b + nb)
                        with nc.allow_low_precision(reason="router combine weights rounded to bf16"):
                            vec.inc(vector.tensor_scalar_mul(ww, rt_kp[:, 0:12],
                                                             rt_cols[:, 3:4]), 1)
                        nb += 1; vector.wait_ge(vec.h, b + nb)
                    vec.inc(vector.tensor_copy(rt_cols[:, 0:1], rt_cols[:, 0:1]), 1)

                @block.sync
                def _(sync, i=i, th=th, b2r=b2r):
                    sync.wait_ge(vec.h, b2r["vec"] + NV * (i + 1))
                    if th == 0 and i == 0:
                        st.inc(sync.dma_start(out=w_db[:, 48:128], in_=zer_i), 16)
                    stw[i % 2].inc(sync.dma_start(
                        out=w_db[th * TH + i * 128:th * TH + (i + 1) * 128, 0:48],
                        in_=w48b[:, (i % 2) * 48:(i % 2) * 48 + 48]), 16)

            @block.sync
            def _(sync, th=th, b2r=b2r):
                sync.wait_ge(stw[0].h, b2r["stw0"] + 16 * 4)
                sync.wait_ge(stw[1].h, b2r["stw1"] + 16 * 4)
                sync.wait_ge(st.h, st.n)
                st.inc(sync.dma_start(out=wTs[:, :],
                                      in_=w_db[th * TH:(th + 1) * TH, :],
                                      transpose=True), 16)
                sync.wait_ge(st.h, st.n)
                st.inc(sync.dma_start(out=wT_d[:, th * TH:(th + 1) * TH],
                                      in_=wTs[:, :]), 16)

            b3p = dict(mmd=mmd.n, vec=vec.n)

            @block.tensor
            def _(tensor, th=th, b3p=b3p):
                tensor.wait_ge(st.h, st.n)
                for Hc in range(FC):
                    bank = psA if Hc % 2 == 0 else psB
                    if Hc >= 2:
                        tensor.wait_ge(vec.h, b3p["vec"] + Hc - 1)
                    for tt in range(2):
                        mm = tensor.matmul(bank[:, tt * NT:(tt + 1) * NT],
                                           b2b[32:44, Hc * 128:(Hc + 1) * 128],
                                           wTs[32:44, tt * NT:(tt + 1) * NT],
                                           start=True, stop=True)
                    mmd.inc(mm, 1)

            @block.vector
            def _(vector, th=th, ho=ho, b3p=b3p):
                for Hc in range(FC):
                    vector.wait_ge(mmd.h, b3p["mmd"] + Hc + 1)
                    bank = psA if Hc % 2 == 0 else psB
                    osl = slice(Hc * TC + ho, Hc * TC + ho + TH)
                    vec.inc(vector.tensor_add(xs[:, osl], xs[:, osl], bank[:, 0:TH]), 1)
                vector.wait_ge(vec.h, vec.n)
                vec.inc(vector.tensor_copy(rt_cols[:, 0:1], rt_cols[:, 0:1]), 1)

            bw = dict(mmd=mmd.n, vec=vec.n, act=act.n, bcl=bcl.n,
                      w1l0=w1l[0].n, w1l1=w1l[1].n)

            @block.gpsimd
            def _(gpsimd, th=th, bw=bw):
                for hg in range(8):
                    for e in range(12):
                        u = hg * 12 + e
                        if u >= 2:
                            gpsimd.wait_ge(mmd.h, bw["mmd"] + u)
                        w1o = w1b[:, (u % 2) * FC * NT:(u % 2 + 1) * FC * NT] \
                            .rearrange("p (c o) -> p c o", c=FC)
                        w1r = w1_i[hg * 12 + e].rearrange("(c p) o -> p c o", p=128)
                        for q in range(4):
                            w1l[u % 2].inc(gpsimd.dma_start(
                                out=w1o[:, 2 * q:2 * q + 2, :],
                                in_=w1r[:, 2 * q:2 * q + 2, :]), 16)

            @block.sync
            def _(sync, th=th, bw=bw):
                for hg in range(8):
                    for e in range(12):
                        u = hg * 12 + e
                        if u >= 1:
                            sync.wait_ge(vec.h, bw["vec"] + 9 * u)
                        else:
                            sync.wait_ge(st.h, st.n)
                        bcl.inc(sync.dma_start(
                            out=bc1[:, :],
                            in_=wT_d[e:e + 1, th * TH:(th + 1) * TH]
                                .broadcast_to((128, TH))), 16)

            @block.vector
            def _(vector, th=th, bw=bw):
                for hg in range(8):
                    for e in range(12):
                        u = hg * 12 + e
                        vector.wait_ge(bcl.h, bw["bcl"] + 16 * (u + 1))
                        if u >= 1:
                            vector.wait_ge(mmd.h, bw["mmd"] + u)
                        for fcb in range(FC):
                            vec.inc(vector.tensor_mul(y3s[:, fcb * TH:(fcb + 1) * TH],
                                                      y3h[:, fcb * TH:(fcb + 1) * TH],
                                                      bc1[:, :]), 1)
                        vector.wait_ge(vec.h, vec.n)
                        vec.inc(vector.tensor_copy(rt_cols[:, 0:1], rt_cols[:, 0:1]), 1)

            @block.tensor
            def _(tensor, th=th, bw=bw):
                tensor.wait_ge(vec.h, bw["vec"])
                tensor.wait_ge(act.h, bw["act"])
                for hg in range(8):
                    for e in range(12):
                        u = hg * 12 + e
                        if hg >= 1 and e == 0:
                            tensor.wait_ge(act.h, bw["act"] + 4 * hg)
                        tensor.wait_ge(w1l[u % 2].h,
                                       (bw["w1l0"] if u % 2 == 0 else bw["w1l1"])
                                       + 64 * (u // 2 + 1))
                        tensor.wait_ge(vec.h, bw["vec"] + 9 * (u + 1))
                        wsl = (u % 2) * FC * NT
                        for hc in range(4):
                            ob = psA if hc < 2 else psB
                            osl = (hc % 2) * 1024
                            for fcb in range(FC):
                                for tt in range(2):
                                    mm = tensor.matmul(
                                        ob[:, osl + tt * NT:osl + (tt + 1) * NT],
                                        w1b[:, wsl + fcb * NT + hc * 128:wsl + fcb * NT + (hc + 1) * 128],
                                        y3s[:, fcb * TH + tt * NT:fcb * TH + (tt + 1) * NT],
                                        start=(e == 0 and fcb == 0),
                                        stop=(e == 11 and fcb == FC - 1))
                        mmd.inc(mm, 1)

            @block.scalar
            def _(scalar, th=th, bw=bw):
                for hg in range(8):
                    scalar.wait_ge(mmd.h, bw["mmd"] + 12 * (hg + 1))
                    for hc in range(4):
                        ob = psA if hc < 2 else psB
                        osl = (hc % 2) * 1024
                        act.inc(scalar.activation(
                            h_g[:, (hg * 4 + hc) * 1024:(hg * 4 + hc + 1) * 1024],
                            ob[:, osl:osl + 1024], AF.Gelu), 1)

            b5 = dict(mmd=mmd.n, vec=vec.n, act=act.n, bcl=bcl.n,
                      w2l0=w2l[0].n, w2l1=w2l[1].n,
                      w2l2=w2l[2].n, w2l3=w2l[3].n)

            @block.gpsimd
            def _(gpsimd, th=th, b5=b5):
                for hh in range(2):
                    for e in range(12):
                        for hidc in range(32):
                            u = (hh * 12 + e) * 32 + hidc
                            if u >= 3:
                                gpsimd.wait_ge(mmd.h, b5["mmd"] + max(1, u - 2))
                            w2l[u % 4].inc(gpsimd.dma_start(
                                out=w2b[:, (u % 4) * 512:(u % 4) * 512 + 512],
                                in_=w2_i[e * 32 + hidc, :, hh * 512:(hh + 1) * 512]), 16)

            @block.sync
            def _(sync, th=th, b5=b5):
                for hh in range(2):
                    for e in range(12):
                        u = hh * 12 + e
                        if u >= 1:
                            sync.wait_ge(vec.h, b5["vec"] + 9 * u)
                        else:
                            sync.wait_ge(vec.h, b5["vec"])
                        bcl.inc(sync.dma_start(
                            out=bc1[:, :],
                            in_=wT_d[16 + e:17 + e, th * TH:(th + 1) * TH]
                                .broadcast_to((128, TH))), 16)

            @block.tensor
            def _(tensor, th=th, b5=b5):
                tensor.wait_ge(act.h, b5["act"])
                tensor.wait_ge(vec.h, b5["vec"])
                for hh in range(2):
                    for e in range(12):
                        ue = hh * 12 + e
                        if ue >= 1:
                            tensor.wait_ge(vec.h, b5["vec"] + 9 * ue)
                        for hidc in range(32):
                            u = ue * 32 + hidc
                            tensor.wait_ge(w2l[u % 4].h,
                                           b5[f"w2l{u % 4}"]
                                           + 16 * (u // 4 + 1))
                            for Hc in range(4):
                                ob = psA if Hc < 2 else psB
                                osl = (Hc % 2) * 1024
                                for tt in range(2):
                                    mm = tensor.matmul(
                                        ob[:, osl + tt * NT:osl + (tt + 1) * NT],
                                        w2b[:, (u % 4) * 512 + Hc * 128:(u % 4) * 512 + (Hc + 1) * 128],
                                        h_g[:, hidc * 1024 + tt * NT:hidc * 1024 + (tt + 1) * NT],
                                        start=(hidc == 0), stop=(hidc == 31))
                            mmd.inc(mm, 1)

            @block.vector
            def _(vector, th=th, ho=ho, b5=b5):
                for hh in range(2):
                    for e in range(12):
                        ue = hh * 12 + e
                        vector.wait_ge(mmd.h, b5["mmd"] + 32 * (ue + 1))
                        vector.wait_ge(bcl.h, b5["bcl"] + 16 * (ue + 1))
                        for Hc in range(4):
                            ob = psA if Hc < 2 else psB
                            osl = (Hc % 2) * 1024
                            xsl = slice((hh * 4 + Hc) * TC + ho, (hh * 4 + Hc) * TC + ho + TH)
                            vec.inc(vector.tensor_mul(sq2[:, 0:TH], ob[:, osl:osl + 1024],
                                                      bc1[:, :]), 1)
                            vector.wait_ge(vec.h, vec.n)
                            vec.inc(vector.tensor_add(xs[:, xsl], xs[:, xsl],
                                                      sq2[:, 0:TH]), 1)
                            vector.wait_ge(vec.h, vec.n)
                        vec.inc(vector.tensor_copy(rt_cols[:, 0:1], rt_cols[:, 0:1]), 1)

        @block.sync
        def _(sync):
            sync.wait_ge(vec.h, vec.n)
            for fcb in range(FC):
                st.inc(sync.dma_start(out=out_o[fcb * 128:(fcb + 1) * 128, :],
                                      in_=xs[:, fcb * TC:(fcb + 1) * TC]), 16)
            sync.wait_ge(st.h, st.n)
    return ctx




def emit_drain(ctx):
    nc = ctx.nc
    sems = [s for s in ctx.sems.values() if s.n > 0]
    with nc.Block() as blk:
        @blk.sync
        def _(sync):
            for s in sems:
                sync.wait_ge(s.h, s.n)


def emit_ag(ctx, tag, src, dst):
    nc = ctx.nc
    ctx.phase(tag)
    cc = ctx.sem("cc")
    with nc.Block() as blk:
        @blk.gpsimd
        def _(g):
            cc.inc(g.collective_compute(
                "AllGather", ALU.bypass,
                replica_groups=[list(range(NCORES))],
                ins=[src.opt()], outs=[dst.opt()]), 1)
            g.wait_ge(cc.h, cc.n)


def emit_a2a(ctx, tag, src, rname, dst):
    nc = ctx.nc
    ctx.phase(tag)
    rr = ctx.dt(rname, [NPANEL * 128, TC], BF16, "int")
    cp = ctx.sem("cp")
    cc = ctx.sem("cc")
    with nc.Block() as blk:
        @blk.sync
        def _(sync):
            for p in range(NPANEL):
                cp.inc(sync.dma_start(out=rr[p * 128:(p + 1) * 128, :],
                                      in_=src[:, p * TC:(p + 1) * TC]), 16)
        @blk.gpsimd
        def _(g):
            g.wait_ge(cp.h, cp.n)
            cc.inc(g.collective_compute(
                "AllToAll", ALU.bypass,
                replica_groups=[list(range(NCORES))],
                ins=[rr.opt()], outs=[dst.opt()]), 1)
            g.wait_ge(cc.h, cc.n)


def build_fused():
    ctx = Ctx("fused", fused=True)
    nc = ctx.nc
    with nc.cleanup_on_exit():
        build_pre0(ctx)
        emit_drain(ctx)
    with nc.cleanup_on_exit():
        y0f = ctx.dt("y0f", [NPANEL * D, TC], BF16, "int", shared=True)
        emit_ag(ctx, "cy0", ctx.dram["y0"], y0f)
    with nc.cleanup_on_exit():
        build_mid0(ctx)
        emit_drain(ctx)
    with nc.cleanup_on_exit():
        o_l = ctx.dt("o", [D, TC], BF16, "int")
        emit_a2a(ctx, "co", ctx.dram["oh"], "o_r", o_l)
    with nc.cleanup_on_exit():
        build_l0post(ctx)
        emit_drain(ctx)
    with nc.cleanup_on_exit():
        ssf = ctx.dt("ssf", [NPANEL * D, TC], BF16, "int", shared=True)
        emit_ag(ctx, "css", ctx.dram["ss"], ssf)
        srf = ctx.dt("srf", [NPANEL * D, TC], BF16, "int", shared=True)
        emit_ag(ctx, "csr", ctx.dram["sr"], srf)
    with nc.cleanup_on_exit():
        build_mid1(ctx)
        emit_drain(ctx)
    with nc.cleanup_on_exit():
        vals = ctx.dt("vals", [D, TC], BF16, "int")
        emit_a2a(ctx, "cv", ctx.dram["o1h"], "o1_r", vals)
    with nc.cleanup_on_exit():
        build_post1(ctx)
        emit_drain(ctx)
    return ctx



def fused_in_maps(inputs):
    xf = np.asarray(inputs["L0_attn_hidden_states"], np.float32).reshape(B * T, D)
    gw = np.asarray(inputs["L0_attn_g_norm_weight"], np.float32)
    Wo = np.asarray(inputs["L0_attn_Wo"], np.float32)
    wot_blocks = np.stack([Wo.T[:, oc * 128:(oc + 1) * 128] for oc in range(FC)])
    Wc = np.asarray(inputs["L0_ffn_classifier_weight"], np.float32)
    bias = np.asarray(inputs["L0_ffn_e_score_correction_bias"], np.float32)
    sg = np.asarray(inputs["L1_attn_store_norm_g"], np.float32)
    rg = np.asarray(inputs["L1_attn_retrieve_norm_g"], np.float32)
    gup = np.asarray(inputs["L0_ffn_gate_up_proj"], np.float32)[:32]
    dnp = np.asarray(inputs["L0_ffn_down_proj"], np.float32)
    l0wgu = np.ascontiguousarray(np.transpose(gup, (0, 2, 1)))
    l0wdn = np.ascontiguousarray(np.transpose(dnp, (0, 2, 1)))
    wkv = np.asarray(inputs["L1_attn_w_kv"], np.float32)
    wq1 = np.asarray(inputs["L1_attn_w_q"], np.float32)
    ws = np.asarray(inputs["L1_attn_w_step"], np.float32)
    wm = np.asarray(inputs["L1_attn_w_mom"], np.float32)
    wd = np.asarray(inputs["L1_attn_w_decay"], np.float32)
    wg1 = np.asarray(inputs["L1_attn_w_gate"], np.float32)
    gam = np.asarray(inputs["L1_attn_mh_gamma"], np.float32)
    Wcm = np.asarray(inputs["L1_attn_w_combine"], np.float32)
    wcmb_blocks = np.stack([Wcm[:, oc * 128:(oc + 1) * 128] for oc in range(FC)])
    Pw = np.asarray(inputs["L1_ffn_P_w"], np.float32)
    U1 = np.asarray(inputs["L1_ffn_U1"], np.float32)
    U2 = np.asarray(inputs["L1_ffn_U2"], np.float32)
    U3 = np.asarray(inputs["L1_ffn_U3"], np.float32)
    u123 = np.zeros((64, 48), np.float32)
    u123[:, 0:12] = U1.T; u123[:, 16:28] = U2.T; u123[:, 32:44] = U3.T
    W1 = np.asarray(inputs["L1_ffn_W1"], np.float32)
    W2 = np.asarray(inputs["L1_ffn_W2"], np.float32)
    w1s = np.empty((96, D, NT), np.float32)
    for hg in range(8):
        for e in range(12):
            w1s[hg * 12 + e] = W1[e].T[:, hg * 512:(hg + 1) * 512]
    w2s = np.empty((384, 128, D), np.float32)
    for e in range(12):
        for hidc in range(32):
            w2s[e * 32 + hidc] = W2[e].T[hidc * 128:(hidc + 1) * 128, :]
    common = {
        "wot": np.ascontiguousarray(wot_blocks.reshape(FC * D, 128)),
        "wg1t": np.ascontiguousarray(np.asarray(inputs["L0_attn_Wg1"], np.float32).T),
        "wg2t": np.ascontiguousarray(np.asarray(inputs["L0_attn_Wg2"], np.float32).T),
        "gnorm": np.ascontiguousarray(gw.reshape(FC, 128).T),
        "wct": np.ascontiguousarray(Wc.T),
        "biasbc": np.ascontiguousarray(np.broadcast_to(bias[None, :], (128, 40))),
        "sgain": np.ascontiguousarray(sg.reshape(FC, 128).T),
        "rgain": np.ascontiguousarray(rg.reshape(FC, 128).T),
        "l0wgu": l0wgu,
        "l0wdn": l0wdn,
        "zpad": np.zeros((128, 64), BFD),
        "wsmd0": None,
        "w0": np.asarray(inputs["L1_attn_mem_w0"], np.float32),
        "w1": np.asarray(inputs["L1_attn_mem_w1"], np.float32),
        "zer": np.zeros((TC, 80), BFD),
        "wcmb": np.ascontiguousarray(wcmb_blocks.reshape(FC * D, 128)),
        "pwt": np.ascontiguousarray(Pw.T),
        "u123": u123,
        "b2": np.asarray(inputs["L1_ffn_b2"], np.float32),
        "w1s": w1s, "w2s": w2s,
    }
    del common["wsmd0"]
    in_maps = []
    for h in range(NCORES):
        m = dict(common)
        m["x"] = np.ascontiguousarray(xf[h * TC:(h + 1) * TC].T)
        m["wq"] = np.ascontiguousarray(
            np.asarray(inputs["L0_attn_Wq"], np.float32)[h * 128:(h + 1) * 128, :].T)
        m["wk"] = np.ascontiguousarray(
            np.asarray(inputs["L0_attn_Wk"], np.float32)[h * 128:(h + 1) * 128, :].T)
        m["wv"] = np.ascontiguousarray(
            np.asarray(inputs["L0_attn_Wv"], np.float32)[h * 128:(h + 1) * 128, :].T)
        m["wk1"] = np.ascontiguousarray(wkv[:, h * 128:(h + 1) * 128])
        m["wv1"] = np.ascontiguousarray(wkv[:, 1024 + h * 128:1024 + (h + 1) * 128])
        m["wq1"] = np.ascontiguousarray(wq1[:, h * 128:(h + 1) * 128])
        m["wsmd"] = np.ascontiguousarray(
            np.stack([ws[:, h], wm[:, h], wd[:, h]], axis=1))
        m["wg"] = np.ascontiguousarray(wg1[:, h:h + 1])
        m["gam"] = np.ascontiguousarray(gam[h, 0, :].reshape(128, 1))
        in_maps.append(m)
    return in_maps


def kernel(**inputs):
    ctx = build_fused()
    in_maps = fused_in_maps(inputs)
    res = run_bass_kernel_spmd(ctx.nc, in_maps, core_ids=list(range(NCORES)))
    full = np.concatenate(
        [np.asarray(res.results[c]["out"], np.float32).T for c in range(NCORES)],
        axis=0)
    return full.reshape(B, T, D)
